# version 11
# baseline (speedup 1.0000x reference)
import os
import numpy as np
import ml_dtypes
import concourse.bass as bass
import concourse.mybir as mybir
from concourse.bass_utils import run_bass_kernel_spmd
from concourse.bass_core_rust import IndirectOffsetOnAxis

F32 = mybir.dt.float32
BF16 = mybir.dt.bfloat16
I32 = mybir.dt.int32
AF = mybir.ActivationFunctionType
ALU = mybir.AluOpType
AXX = mybir.AxisListType.X

B_, T, D = 2, 8192, 2048
C = 128
NCHUNK = T // C
TILE = 512
CPT = TILE // C
NTILE = T // TILE
NE = 2
NPAD = 8
OGROWS = T
EPS = 1e-5
NEG = -3.0e38
POS = 3.0e38

_TAPS = set(s for s in os.environ.get("KDBG", "").split(",") if s)
_KPH = os.environ.get("KPH", "all")
_KTILES = int(os.environ.get("KTILES", "16"))
_KNOG = os.environ.get("KNOG", "0") == "1"
_KGSER = os.environ.get("KGSER", "1") == "1"
_CACHE = {}


class Bld:
    def __init__(self, nc):
        self.nc = nc
        self.sems = {}
        self.cnt = {}
        self.last_wait = {}
        self._ctx = None

    def sem(self, name):
        if name not in self.sems:
            self.sems[name] = self._ctx.enter_context(self.nc.semaphore("s_" + name))
            self.cnt[name] = 0
        return self.sems[name]

    def inc(self, inst, name, by=1):
        inst.then_inc(self.sem(name), by)
        self.cnt[name] += by
        return (name, self.cnt[name])

    def tok(self, name):
        self.sem(name)
        return (name, self.cnt[name])

    def wait(self, eng, tok):
        if tok is None:
            return
        name, val = tok
        if val <= 0:
            return
        key = (eng.engine, name)
        if self.last_wait.get(key, -1) >= val:
            return
        eng.wait_ge(self.sems[name], val)
        self.last_wait[key] = val

    def waits(self, eng, toks):
        for t in toks:
            self.wait(eng, t)

    def raw(self, eng, inst, sem):
        tok = self.inc(inst, sem)
        self.wait(eng, tok)
        return tok

    def dma(self, eng, name, out, in_, **kw):
        eng.dma_start(out=out, in_=in_, **kw).then_inc(self.sem(name), 16)
        self.cnt[name] += 16
        return (name, self.cnt[name])


class PBank:
    def __init__(self, tens):
        self.t = tens
        self.writer = None
        self.readers = []

    def pre_write(self, b, eng):
        b.waits(eng, self.readers)
        self.readers = []

    def pre_read(self, b, eng):
        b.wait(eng, self.writer)


def build_program(cap):
    drows = NE * cap + NPAD
    rank512 = OGROWS + drows * 4
    ncap = cap // 128
    rblocks = []
    r = cap
    while r > 0:
        rblocks.append(min(512, r))
        r -= min(512, r)

    nc = bass.Bass("TRN2", target_bir_lowering=False)
    b = Bld(nc)

    dram = {}
    for name, shape, dt in (
        ("xb", [T, D], F32), ("xout", [2048, D], F32),
        ("wq", [D, 512], F32), ("wk", [D, 512], F32), ("wv", [D, 512], F32),
        ("wg", [D, 512], F32), ("wo", [D, D], F32), ("wab", [D, 8], F32),
        ("ba4", [4, 1], F32), ("bb4", [4, 1], F32),
        ("cw", [128, 48], F32), ("cb", [128, 12], F32),
        ("lnwB", [128, 512], F32), ("lnbB", [128, 512], F32),
        ("fx", [NE * cap, D], F32),
        ("fgw", [NE, 11, 128, D], F32), ("fuw", [NE, 11, 128, D], F32),
        ("fdw", [NE, 4, 128, 11 * 512], F32),
        ("fwcol", [128, NE * ncap], F32),
        ("cidx", [128, 64], I32), ("oidx", [128, 64], I32),
    ):
        dram[name] = nc.dram_tensor(name, shape, dt, kind="ExternalInput")
    out_d = nc.dram_tensor("out", [2048, D], F32, kind="ExternalOutput")
    dbg_d = nc.dram_tensor("dbg", [2048, D], F32, kind="ExternalOutput") if _TAPS else None
    dbg_row = [0]

    ag_in = nc.dram_tensor("ag_in", [rank512, 512], BF16, kind="Internal")
    ag_out = nc.dram_tensor("ag_out", [8 * rank512, 512], BF16, kind="Internal", addr_space="Shared")

    si = np.arange(128)[:, None]
    ti = np.arange(128)[None, :]
    consts = {
        "id_bf": np.eye(128).astype(ml_dtypes.bfloat16),
        "id_f": np.eye(128, dtype=np.float32),
        "ut_s": np.triu(np.ones((128, 128), np.float32)),
        "e127_s": np.where(si == 127, 1.0, 0.0).astype(np.float32) * np.ones((1, 128), np.float32),
        "bndi_s": np.where(si <= ti, 0.0, NEG).astype(np.float32),
        "bnds_s": np.where(si < ti, 0.0, NEG).astype(np.float32),
        "bndt_s": np.where(si > ti, 0.0, POS).astype(np.float32),
        "lts_s": np.where(si > ti, 1.0, 0.0).astype(np.float32),
    }
    ehp = np.zeros((4, 512), np.float32)
    for h in range(4):
        ehp[h, h * 128:(h + 1) * 128] = 1.0
    consts["ehp_s"] = ehp
    cten = {k: nc.inline_tensor(v, name="c_" + k) for k, v in consts.items()}

    from contextlib import ExitStack
    with ExitStack() as ctx:
        b._ctx = ctx
        E = ctx.enter_context
        te, ve, ac, gp, sy = nc.tensor, nc.vector, nc.scalar, nc.gpsimd, nc.sync

        PA = PBank(E(nc.psum_tensor("PA", [128, 512], F32)))
        PB = PBank(E(nc.psum_tensor("PB", [128, 512], F32)))
        PC = PBank(E(nc.psum_tensor("PC", [128, 512], F32)))
        PD = PBank(E(nc.psum_tensor("PD", [128, 512], F32)))
        PE = PBank(E(nc.psum_tensor("PE", [128, 512], F32)))
        PF = PBank(E(nc.psum_tensor("PF", [128, 512], F32)))
        PT = PBank(E(nc.psum_tensor("PT", [128, 1024], BF16)))
        PS = PBank(E(nc.psum_tensor("PS", [128, 512], F32)))
        PARITY = [(PA, PB, PC), (PD, PE, PF)]

        sb = lambda name, shape, dt: E(nc.sbuf_tensor(name, shape, dt))
        cs = {k: sb(k, list(v.shape), BF16 if v.dtype == ml_dtypes.bfloat16 else F32)
              for k, v in consts.items()}
        lnw_s = sb("lnw_s", [128, 512], F32)
        lnb_s = sb("lnb_s", [128, 512], F32)
        cw_s = sb("cw_s", [128, 48], F32)
        cb_s = sb("cb_s", [128, 12], F32)
        ba_s = sb("ba_s", [4, 1], F32)
        bb_s = sb("bb_s", [4, 1], F32)
        w_sb = sb("w_sb", [128, 32768], BF16)
        wab_s = sb("wab_s", [128, 128], BF16)
        z32 = [sb(f"z32_{p}", [128, 512], F32) for p in range(2)]
        zbf = sb("zbf", [128, 512], BF16)
        fw_s = sb("fw_s", [128, NE * ncap], F32)
        cidx_s = sb("cidx_s", [128, 64], I32)
        oidx_s = sb("oidx_s", [128, 64], I32)
        smalls = [sb(f"smalls{p}", [128, 64], F32) for p in range(3)]
        stat = sb("stat", [128, 24], F32)

        SM = dict(laT=0, lbT=4, bT=8, gT=12, ngT=16, gamT=20, gCb=24, cs=28, gCr=44,
                  gpT=32, bgp=36, r2T=40, osum=44, osq=48, omu=52, orstd=56, tmp=60)

        def tap(name, src_ap, rows=128, cols=None, via=None):
            if name not in _TAPS:
                return
            for snm in ("dv", "ac", "pe", "gp"):
                b.wait(sy, b.tok(snm))
            cols = cols if cols is not None else src_ap.shape[-1]
            r = dbg_row[0]
            b.wait(gp, b.tok("dbgst"))
            for snm in ("dv", "ac", "pe", "gp"):
                b.wait(gp, b.tok(snm))
            b.dma(gp, "dbgst", dbg_d[r:r + rows, 0:cols], src_ap)
            for eng in (te, ve, ac, gp, sy):
                b.wait(eng, b.tok("dbgst"))
            dbg_row[0] += rows
            print(f"  [tap {name} -> dbg rows {r}:{r+rows} cols {cols}]")

        for k in cs:
            b.dma(sy, "init", cs[k][:], cten[k].ap())
        for dst, src in ((lnw_s, "lnwB"), (lnb_s, "lnbB"), (cw_s, "cw"),
                         (cb_s, "cb"), (ba_s, "ba4"), (bb_s, "bb4"),
                         (fw_s, "fwcol"), (cidx_s, "cidx"), (oidx_s, "oidx")):
            b.dma(sy, "init", dst[:], dram[src][:, :])
        t_init = b.tok("init")

        for k4, wn in enumerate(("wq", "wk", "wv", "wg")):
            b.dma(gp, "wl", w_sb[:].rearrange("p (m fc o) -> p m fc o", m=4, o=512)[:, k4],
                  dram[wn][:, :].rearrange("(fc p) o -> p fc o", p=128))
        b.dma(gp, "wl", wab_s[:].rearrange("p (fc o) -> p fc o", o=8),
              dram["wab"][:, :].rearrange("(fc p) o -> p fc o", p=128))
        t_wl = b.tok("wl")

        b.wait(gp, t_wl)
        gp.memset(z32[0][:], 0.0)
        tz0 = b.inc(gp.memset(zbf[:], 0.0), "gp")

        with ExitStack() as arenaP:
            EP = arenaP.enter_context
            sp = lambda name, shape, dt: EP(nc.sbuf_tensor(name, shape, dt))
            x_tm = [sp(f"x_tm{p}", [128, 2048], F32) for p in range(2)]
            sqscr = sp("sqscr", [128, 256], F32)
            xn_tm = sp("xn_tm", [128, 2048], BF16)
            xn_fm = sp("xn_fm", [128, 8192], BF16)
            halo_sb = sp("halo_sb", [128, 36], BF16)
            ctmp = [sp(f"ctmp{p}", [128, 512], BF16) for p in range(2)]
            qkv = [sp(f"qkv{p}", [128, 6144], BF16) for p in range(2)]
            g_ch = [sp(f"g_ch{p}", [128, 512], BF16) for p in range(3)]
            zz = sp("zz", [4, 512], F32)
            abt1 = sp("abt1", [4, 512], F32)
            la4 = sp("la4", [4, 512], F32)
            lb4 = sp("lb4", [4, 512], F32)
            b4 = sp("b4", [4, 512], F32)
            rows_cat = sp("rows_cat", [4, 384], F32)
            e_scr = sp("e_scr", [128, 2048], F32)
            nsm1 = [sp(f"nsm1_{p}", [128, 512], BF16) for p in range(2)]
            ntm1 = [sp(f"ntm1_{p}", [128, 512], BF16) for p in range(2)]
            dsm = [sp(f"dsm{p}", [128, 512], BF16) for p in range(3)]
            ktm = [sp(f"ktm{p}", [128, 512], BF16) for p in range(3)]
            vtm = [sp(f"vtm{p}", [128, 512], BF16) for p in range(2)]
            nsm_s = [sp(f"nsm_s{p}", [128, 512], BF16) for p in range(2)]
            ntm_s = [sp(f"ntm_s{p}", [128, 512], BF16) for p in range(2)]
            rhs0 = [sp(f"rhs0_{p}", [128, 1024], BF16) for p in range(2)]
            bufU = [sp(f"bufU{p}", [128, 512], BF16) for p in range(2)]
            bufUf = [sp(f"bufUf{p}", [128, 512], BF16) for p in range(2)]
            bufW = [sp(f"bufW{p}", [128, 512], BF16) for p in range(2)]
            nwfm = [sp(f"nwfm{p}", [128, 512], BF16) for p in range(2)]
            u_sb = sp("u_sb", [128, 512], BF16)
            usc = sp("usc", [128, 512], BF16)
            rz_t = sp("rz_t", [128, 512], BF16)
            o_sb = sp("o_sb", [128, 512], F32)
            on_b = sp("on_b", [128, 512], BF16)
            gtanh = sp("gtanh", [128, 512], BF16)
            gcA = sp("gcA", [128, 512], BF16)
            gcB = sp("gcB", [128, 512], BF16)
            og_sb = [sp(f"og_sb{p}", [128, 512], BF16) for p in range(2)]

            FB1, FB2, FX = PD, PE, PF

            b.wait(gp, tz0)
            t_cz = b.inc(gp.memset(halo_sb[:], 0.0), "gp")

            tk = {
                "x_free": [None, None], "xn_done": None, "gp_stat": None, "xn_tnm": None,
                "xnfm": None, "qkv_free": [None, None],
                "qkv": [None, None], "g": [None] * 3, "g_free": [None] * 3,
                "ab": None, "ab_free": None, "ab_free2": None,
                "kvtm": [None] * 3, "nd": [None, None],
                "sm_free": [None] * 3, "og_free": [None, None],
                "z_ok": (tz0[0], tz0[1]),
                "gts": None, "rhs0": [None, None], "rhs0_free": [None, None],
                "ufin_free": [None, None], "nw_free": [None, None], "usb_free": None,
                "usb": None, "lad": [(None, None), (None, None)], "og_done": None,
            }

            def xn_ln(i, tcn, tok_escr):
                t0 = i * TILE
                jp = (i * 4 + tcn) % 2
                b.waits(sy, [tk["x_free"][jp]])
                tld = b.dma(sy, f"xl{jp}", x_tm[jp][:], dram["xb"][t0 + tcn * 128: t0 + (tcn + 1) * 128, :])
                b.waits(ve, [tld, tk["gp_stat"]])
                tgs = b.inc(ve.reduce_sum(stat[:, 0:1], x_tm[jp][:], AXX), "dv")
                b.waits(ac, [tld, tk["x_free"][1 - jp], tok_escr])
                tsq = b.inc(ac.activation(e_scr[:], x_tm[jp][:], AF.Square,
                                          accum_out=stat[:, 1:2]), "ac")
                b.waits(ve, [tsq, tgs])
                b.raw(ve, ve.tensor_scalar(stat[:, 3:4], stat[:, 0:1], 1.0 / D, None, ALU.mult), "dv")
                b.raw(ve, ve.tensor_tensor(stat[:, 13:14], stat[:, 3:4], stat[:, 3:4], ALU.mult), "dv")
                b.raw(ve, ve.tensor_scalar(stat[:, 4:5], stat[:, 13:14], EPS, None, ALU.subtract), "dv")
                tvar = b.raw(ve, ve.scalar_tensor_tensor(stat[:, 6:7], stat[:, 1:2], 1.0 / D,
                                                         stat[:, 4:5], ALU.mult, ALU.subtract), "dv")
                tk["gp_stat"] = tvar
                b.wait(ac, tvar)
                b.raw(ac, ac.activation(stat[:, 7:8], stat[:, 6:7], AF.Ln), "ac")
                trs = b.inc(ac.activation(stat[:, 8:9], stat[:, 7:8], AF.Exp, scale=-0.5), "ac")
                b.waits(ve, [trs, tk["xn_done"]])
                tnm = b.inc(ve.tensor_scalar(xn_tm[:], x_tm[jp][:], stat[:, 3:4], stat[:, 8:9],
                                             ALU.subtract, ALU.mult), "dv")
                tk["x_free"][jp] = tnm
                tk["xn_tnm"] = tnm

            def xn_tr(i, tcn):
                tnm = tk["xn_tnm"]
                for half in range(2):
                    PT.pre_write(b, te)
                    b.waits(te, [tnm, t_init])
                    for fc8 in range(8):
                        fc = half * 8 + fc8
                        inst = te.transpose(PT.t[:, fc8 * 128:(fc8 + 1) * 128],
                                            xn_tm[:, fc * 128:(fc + 1) * 128], cs["id_bf"][:])
                    tpe = b.inc(inst, "pe")
                    PT.writer = tpe
                    b.wait(ac, tpe)
                    for fc8 in range(8):
                        fc = half * 8 + fc8
                        inst = ac.activation(
                            xn_fm[:, fc * 512 + tcn * 128: fc * 512 + (tcn + 1) * 128],
                            PT.t[:, fc8 * 128:(fc8 + 1) * 128], AF.Copy)
                    tac = b.inc(inst, "ac")
                    PT.readers.append(tac)
                    tk["xn_done"] = tpe
                if tcn == 3:
                    tk["xnfm"] = b.tok("ac")

            def tile_rest(i):
                par = i % 2
                prev_trl = None
                for chn in range(2):
                    FX.pre_write(b, te)
                    b.waits(te, [tk["ab_free"], tk["ab_free2"], tk["xnfm"]])
                    for fc in range(16):
                        inst = te.matmul(FX.t[0:4, :],
                                         wab_s[:, fc * 8 + chn * 4: fc * 8 + chn * 4 + 4],
                                         xn_fm[:, fc * 512:(fc + 1) * 512],
                                         start=(fc == 0), stop=(fc == 15))
                    tpe = b.inc(inst, "pe")
                    FX.writer = tpe
                    b.waits(ve, [tpe, prev_trl])
                    dst = la4 if chn == 0 else lb4
                    tzz = b.inc(ve.tensor_scalar(zz[:], FX.t[0:4, :],
                                                 (ba_s if chn == 0 else bb_s)[:],
                                                 None, ALU.add), "dv")
                    FX.readers.append(tzz)
                    b.wait(ac, tzz)
                    b.raw(ac, ac.activation(abt1[:], zz[:], AF.Abs), "ac")
                    b.raw(ac, ac.activation(dst[:], abt1[:], AF.Exp, scale=-1.0), "ac")
                    b.raw(ac, ac.activation(abt1[:], dst[:], AF.Ln, bias=1.0), "ac")
                    trl = b.inc(ac.activation(dst[:], zz[:], AF.Relu, scale=-1.0), "ac")
                    prev_trl = trl
                    b.wait(ve, trl)
                    b.inc(ve.scalar_tensor_tensor(dst[:], abt1[:], -1.0, dst[:],
                                                  ALU.mult, ALU.subtract), "dv")
                b.wait(ac, b.tok("dv"))
                tk["ab"] = b.inc(ac.activation(b4[:], lb4[:], AF.Exp), "ac")

                pp = [FB1, FB2]
                prev_tcv = None
                for m in range(3):
                    for oc in range(4):
                        blk = m * 4 + oc
                        bank = pp[blk % 2]
                        bank.pre_write(b, te)
                        b.waits(te, [tk["xnfm"], t_wl, tk["qkv_free"][par]])
                        for fc in range(16):
                            inst = te.matmul(bank.t[:],
                                             w_sb[:, m * 8192 + fc * 512 + oc * 128:
                                                  m * 8192 + fc * 512 + (oc + 1) * 128],
                                             xn_fm[:, fc * 512:(fc + 1) * 512],
                                             start=(fc == 0), stop=(fc == 15))
                        tpe = b.inc(inst, "pe")
                        bank.writer = tpe
                        cwc = lambda j2: cw_s[:, blk * 4 + j2: blk * 4 + j2 + 1]
                        hl = lambda a, b2: halo_sb[:, blk * 3 + a: blk * 3 + b2]
                        b.waits(ve, [tpe, prev_tcv, t_cz])
                        b.raw(ve, ve.tensor_scalar(ctmp[0][:, 3:512], bank.t[:, 0:509],
                                                   cwc(0), None, ALU.mult), "dv")
                        b.raw(ve, ve.tensor_scalar(ctmp[0][:, 0:3], hl(0, 3),
                                                   cwc(0), None, ALU.mult), "dv")
                        b.raw(ve, ve.scalar_tensor_tensor(ctmp[1][:, 2:512], bank.t[:, 0:510],
                                                          cwc(1), ctmp[0][:, 2:512], ALU.mult, ALU.add), "dv")
                        b.raw(ve, ve.scalar_tensor_tensor(ctmp[1][:, 0:2], hl(1, 3),
                                                          cwc(1), ctmp[0][:, 0:2], ALU.mult, ALU.add), "dv")
                        b.raw(ve, ve.scalar_tensor_tensor(ctmp[0][:, 1:512], bank.t[:, 0:511],
                                                          cwc(2), ctmp[1][:, 1:512], ALU.mult, ALU.add), "dv")
                        b.raw(ve, ve.scalar_tensor_tensor(ctmp[0][:, 0:1], hl(2, 3),
                                                          cwc(2), ctmp[1][:, 0:1], ALU.mult, ALU.add), "dv")
                        b.raw(ve, ve.scalar_tensor_tensor(ctmp[1][:], bank.t[:],
                                                          cwc(3), ctmp[0][:], ALU.mult, ALU.add), "dv")
                        thl = b.inc(ve.tensor_copy(hl(0, 3), bank.t[:, 509:512]), "dv")
                        bank.readers.append(thl)
                        b.wait(ac, thl)
                        if m == 1:
                            tsl = b.inc(ac.activation(ctmp[0][:], ctmp[1][:], AF.Silu,
                                                      bias=cb_s[:, blk: blk + 1]), "ac")
                            b.wait(ve, tsl)
                            prev_tcv = b.inc(
                                ve.tensor_scalar(qkv[par][:, blk * 512:(blk + 1) * 512],
                                                 ctmp[0][:], float(C) ** -0.5, None, ALU.mult), "dv")
                        else:
                            prev_tcv = b.inc(ac.activation(qkv[par][:, blk * 512:(blk + 1) * 512],
                                                           ctmp[1][:], AF.Silu,
                                                           bias=cb_s[:, blk: blk + 1]), "ac")
                tk["qkv"][par] = ("MULTI", [b.tok("ac"), b.tok("dv")])

            def qkv_wait(eng, par):
                tt = tk["qkv"][par]
                if tt and tt[0] == "MULTI":
                    b.waits(eng, tt[1])

            def front_a(j):
                p3 = j % 3
                i, ccn = j // 4, j % 4
                sm = smalls[p3]
                t_lo = ccn * 128

                FX.pre_write(b, te)
                b.waits(te, [tk["sm_free"][p3], tk["ab"]])
                te.transpose(FX.t[:, 0:4], la4[0:4, t_lo:t_lo + 128], cs["id_f"][0:4, 0:4])
                te.transpose(FX.t[:, 4:8], lb4[0:4, t_lo:t_lo + 128], cs["id_f"][0:4, 0:4])
                tpe = b.inc(te.transpose(FX.t[:, 8:12], b4[0:4, t_lo:t_lo + 128],
                                         cs["id_f"][0:4, 0:4]), "pe")
                FX.writer = tpe
                tk["ab_free"] = tpe
                b.wait(ac, tpe)
                te1 = b.inc(ac.activation(sm[:, SM["laT"]:SM["laT"] + 12], FX.t[:, 0:12], AF.Copy), "ac")
                FX.readers.append(te1)

                FX.pre_write(b, te)
                b.wait(te, te1)
                te.matmul(FX.t[:, 16:20], cs["ut_s"][:], sm[:, SM["laT"]:SM["laT"] + 4],
                          start=True, stop=True)
                te.matmul(FX.t[:, 20:24], cs["lts_s"][:], sm[:, SM["laT"]:SM["laT"] + 4],
                          start=True, stop=True)
                tpe = b.inc(te.matmul(FX.t[0:4, 128:256], sm[:, SM["laT"]:SM["laT"] + 4],
                                      cs["ut_s"][:], start=True, stop=True), "pe")
                FX.writer = tpe
                b.wait(ac, tpe)
                ac.activation(sm[:, SM["gT"]:SM["gT"] + 4], FX.t[:, 16:20], AF.Copy)
                ac.activation(sm[:, SM["ngT"]:SM["ngT"] + 4], FX.t[:, 16:20], AF.Copy, scale=-1.0)
                ac.activation(sm[:, SM["gamT"]:SM["gamT"] + 4], FX.t[:, 16:20], AF.Exp)
                ac.activation(sm[:, SM["cs"]:SM["cs"] + 4], FX.t[:, 20:24], AF.Exp)
                te2 = b.inc(ac.activation(rows_cat[:, 0:128], FX.t[0:4, 128:256], AF.Copy), "ac")
                FX.readers.append(te2)
                b.wait(ve, te2)
                b.inc(ve.tensor_tensor(sm[:, SM["gCb"]:SM["gCb"] + 4],
                                       sm[:, SM["cs"]:SM["cs"] + 4],
                                       sm[:, SM["gamT"]:SM["gamT"] + 4], ALU.mult), "dv")
                ve.scalar_tensor_tensor(rows_cat[:, 256:384], la4[0:4, t_lo:t_lo + 128], -1.0,
                                        rows_cat[:, 0:128], ALU.mult, ALU.add)
                b.raw(ve, ve.scalar_tensor_tensor(sm[:, SM["tmp"]:SM["tmp"] + 4],
                                                  sm[:, SM["laT"]:SM["laT"] + 4], -1.0,
                                                  sm[:, SM["gT"]:SM["gT"] + 4], ALU.mult, ALU.add), "dv")
                ve.tensor_tensor(rows_cat[:, 128:256], rows_cat[:, 256:384],
                                 lb4[0:4, t_lo:t_lo + 128], ALU.add)
                tr2 = b.inc(ve.tensor_tensor(sm[:, SM["r2T"]:SM["r2T"] + 4],
                                             sm[:, SM["tmp"]:SM["tmp"] + 4],
                                             sm[:, SM["lbT"]:SM["lbT"] + 4], ALU.add), "dv")
                tk["ab_free2"] = tr2
                b.wait(ac, tr2)
                tgp = b.inc(ac.activation(sm[:, SM["gpT"]:SM["gpT"] + 4],
                                          sm[:, SM["tmp"]:SM["tmp"] + 4], AF.Exp), "ac")
                b.wait(ve, tgp)
                tbgp = b.inc(ve.scalar_tensor_tensor(sm[:, SM["bgp"]:SM["bgp"] + 4],
                                                     sm[:, SM["gpT"]:SM["gpT"] + 4], -1.0,
                                                     sm[:, SM["bT"]:SM["bT"] + 4],
                                                     ALU.mult, ALU.mult), "dv")
                tk["bgp"] = tbgp

                FX.pre_write(b, te)
                b.waits(te, [tk["g_free"][p3], tk["xnfm"]])
                for fc in range(16):
                    inst = te.matmul(FX.t[:],
                                     xn_fm[:, fc * 512 + t_lo: fc * 512 + t_lo + 128],
                                     w_sb[:, 3 * 8192 + fc * 512: 3 * 8192 + (fc + 1) * 512],
                                     start=(fc == 0), stop=(fc == 15))
                tg = b.inc(inst, "pe")
                FX.writer = tg
                b.waits(ac, [tg, tk["gts"]])
                tth = b.inc(ac.activation(gtanh[:], FX.t[:], AF.Tanh, scale=0.5), "ac")
                FX.readers.append(tth)
                b.wait(ve, tth)
                tk["gts"] = b.inc(ve.tensor_scalar(g_ch[p3][:], gtanh[:], 1.0, 0.5,
                                                   ALU.add, ALU.mult), "dv")
                tk["g"][p3] = tk["gts"]

                FB1.pre_write(b, te)
                FB2.pre_write(b, te)
                b.waits(te, [tr2])
                for h in range(4):
                    te.matmul(FB1.t[:, h * 128:(h + 1) * 128],
                              cs["ehp_s"][0:4, h * 128:(h + 1) * 128], rows_cat[:, 0:128],
                              start=True, stop=True)
                for h in range(4):
                    inst = te.matmul(FB2.t[:, h * 128:(h + 1) * 128],
                                     cs["ehp_s"][0:4, h * 128:(h + 1) * 128], rows_cat[:, 128:256],
                                     start=True, stop=True)
                tbcp = b.inc(inst, "pe")
                FB1.writer = tbcp
                FB2.writer = tbcp
                bc_g = lambda h: FB1.t[:, h * 128:(h + 1) * 128]
                bc_r2 = lambda h: FB2.t[:, h * 128:(h + 1) * 128]

                b.waits(ve, [tbcp, b.tok("ac")])
                for h in range(4):
                    ve.scalar_tensor_tensor(e_scr[:, h * 128:(h + 1) * 128],
                                            bc_g(h),
                                            sm[:, SM["gT"] + h: SM["gT"] + h + 1],
                                            cs["bndi_s"][:], ALU.subtract, ALU.min)
                for h in range(4):
                    ve.scalar_tensor_tensor(e_scr[:, 512 + h * 128: 512 + (h + 1) * 128],
                                            bc_r2(h),
                                            sm[:, SM["gT"] + h: SM["gT"] + h + 1],
                                            cs["bnds_s"][:], ALU.subtract, ALU.min)
                for h in range(4):
                    inst = ve.scalar_tensor_tensor(e_scr[:, 1024 + h * 128: 1024 + (h + 1) * 128],
                                                   bc_g(h),
                                                   sm[:, SM["r2T"] + h: SM["r2T"] + h + 1],
                                                   cs["bndt_s"][:], ALU.subtract, ALU.max)
                targ3 = b.inc(inst, "dv")
                FB1.readers.append(targ3)
                FB2.readers.append(targ3)
                tk["targ3"] = targ3
                b.wait(ac, targ3)
                b.raw(ac, ac.activation(e_scr[:, 1536:2048], e_scr[:, 0:512], AF.Exp), "ac")
                b.raw(ac, ac.activation(e_scr[:, 0:512], e_scr[:, 512:1024], AF.Exp), "ac")
                tk["texp"] = b.inc(ac.activation(e_scr[:, 512:1024], e_scr[:, 1024:1536],
                                                 AF.Exp, scale=-1.0), "ac")

            def front_b(j):
                p2 = j % 2
                p3 = j % 3
                i, ccn = j // 4, j % 4
                sm = smalls[p3]
                t_lo = ccn * 128
                kfm = lambda h: qkv[i % 2][:, (4 + h) * 512 + t_lo: (4 + h) * 512 + t_lo + 128]
                vfm = lambda h: qkv[i % 2][:, (8 + h) * 512 + t_lo: (8 + h) * 512 + t_lo + 128]
                qfm = lambda h: qkv[i % 2][:, h * 512 + t_lo: h * 512 + t_lo + 128]

                FB1.pre_write(b, te)
                FB2.pre_write(b, te)
                PT.pre_write(b, te)
                qkv_wait(te, i % 2)
                for h in range(4):
                    te.matmul(FB1.t[:, h * 128:(h + 1) * 128], kfm(h), kfm(h), start=True, stop=True)
                for h in range(4):
                    te.matmul(FB2.t[:, h * 128:(h + 1) * 128], kfm(h), qfm(h), start=True, stop=True)
                for h in range(4):
                    te.transpose(PT.t[:, h * 128:(h + 1) * 128], kfm(h), cs["id_bf"][:])
                for h in range(4):
                    inst = te.transpose(PT.t[:, 512 + h * 128: 512 + (h + 1) * 128], vfm(h), cs["id_bf"][:])
                tp4 = b.inc(inst, "pe")
                FB1.writer = tp4
                FB2.writer = tp4
                PT.writer = tp4
                tk["qkv_free"][i % 2] = tp4
                b.wait(ac, tp4)
                ac.activation(ktm[p3][:], PT.t[:, 0:512], AF.Copy)
                tkv = b.inc(ac.activation(vtm[p2][:], PT.t[:, 512:1024], AF.Copy), "ac")
                PT.readers.append(tkv)
                tk["kvtm"][p3] = tkv

                b.waits(ve, [tk["texp"], tp4])
                for h in range(4):
                    ve.scalar_tensor_tensor(dsm[p3][:, h * 128:(h + 1) * 128],
                                            e_scr[:, 1536 + h * 128: 1536 + (h + 1) * 128], 1.0,
                                            FB2.t[:, h * 128:(h + 1) * 128], ALU.mult, ALU.mult)
                for h in range(4):
                    ve.scalar_tensor_tensor(nsm1[p2][:, h * 128:(h + 1) * 128],
                                            e_scr[:, h * 128:(h + 1) * 128], -1.0,
                                            FB1.t[:, h * 128:(h + 1) * 128], ALU.mult, ALU.mult)
                for h in range(4):
                    inst = ve.scalar_tensor_tensor(ntm1[p2][:, h * 128:(h + 1) * 128],
                                                   e_scr[:, 512 + h * 128: 512 + (h + 1) * 128], -1.0,
                                                   FB1.t[:, h * 128:(h + 1) * 128], ALU.mult, ALU.mult)
                tb2 = b.inc(inst, "dv")
                FB1.readers.append(tb2)
                FB2.readers.append(tb2)
                tk["nd"][p2] = tb2

                b.waits(ve, [tkv, tk["bgp"], tk["rhs0_free"][p2]])
                for h in range(4):
                    ve.tensor_scalar(rhs0[p2][:, h * 128:(h + 1) * 128],
                                     vtm[p2][:, h * 128:(h + 1) * 128],
                                     sm[:, SM["bT"] + h: SM["bT"] + h + 1], None, ALU.mult)
                for h in range(4):
                    inst = ve.tensor_scalar(rhs0[p2][:, 512 + h * 128: 512 + (h + 1) * 128],
                                            ktm[p3][:, h * 128:(h + 1) * 128],
                                            sm[:, SM["bgp"] + h: SM["bgp"] + h + 1], None, ALU.mult)
                tk["rhs0"][p2] = b.inc(inst, "dv")
                if j == 0:
                    tap("fsm0", smalls[0][:, 0:64], cols=64)
                    tap("fdsm0", dsm[0][:, 0:512])
                    tap("fnsm0", nsm1[0][:, 0:512])
                    tap("fntm0", ntm1[0][:, 0:512])

            def ladder(j):
                p2 = j % 2
                ucur, wcur = rhs0[p2][:, 0:512], rhs0[p2][:, 512:1024]
                tok_u = tok_w = tk["rhs0"][p2]
                ncs, nct = nsm1[p2], ntm1[p2]
                tok_n = tk["nd"][p2]
                for lvl in range(7):
                    if lvl < 6:
                        PA.pre_write(b, te)
                        PB.pre_write(b, te)
                        b.waits(te, [tok_n])
                        for h in range(4):
                            te.matmul(PA.t[:, h * 128:(h + 1) * 128],
                                      ncs[:, h * 128:(h + 1) * 128],
                                      nct[:, h * 128:(h + 1) * 128], start=True, stop=True)
                        for h in range(4):
                            inst = te.matmul(PB.t[:, h * 128:(h + 1) * 128],
                                             nct[:, h * 128:(h + 1) * 128],
                                             ncs[:, h * 128:(h + 1) * 128], start=True, stop=True)
                        tsq_pe = b.inc(inst, "pe")
                        PA.writer = tsq_pe
                        PB.writer = tsq_pe
                    PC.pre_write(b, te)
                    b.waits(te, [tok_u, tok_n])
                    for h in range(4):
                        te.matmul(PC.t[:, h * 128:(h + 1) * 128], cs["id_bf"][:],
                                  ucur[:, h * 128:(h + 1) * 128], start=True, stop=False)
                        inst = te.matmul(PC.t[:, h * 128:(h + 1) * 128],
                                         ncs[:, h * 128:(h + 1) * 128],
                                         ucur[:, h * 128:(h + 1) * 128], start=False, stop=True)
                    tU_pe = b.inc(inst, "pe")
                    PC.writer = tU_pe
                    PS.pre_write(b, te)
                    b.waits(te, [tok_w])
                    for h in range(4):
                        te.matmul(PS.t[:, h * 128:(h + 1) * 128], cs["id_bf"][:],
                                  wcur[:, h * 128:(h + 1) * 128], start=True, stop=False)
                        inst = te.matmul(PS.t[:, h * 128:(h + 1) * 128],
                                         ncs[:, h * 128:(h + 1) * 128],
                                         wcur[:, h * 128:(h + 1) * 128], start=False, stop=True)
                    tW_pe = b.inc(inst, "pe")
                    PS.writer = tW_pe
                    if lvl < 6:
                        b.wait(ac, tsq_pe)
                        ac.activation(ntm_s[lvl % 2][:], PA.t[:], AF.Copy)
                        tok_n = b.inc(ac.activation(nsm_s[lvl % 2][:], PB.t[:], AF.Copy), "ac")
                        PA.readers.append(tok_n)
                        PB.readers.append(tok_n)
                        ncs, nct = nsm_s[lvl % 2], ntm_s[lvl % 2]
                    b.wait(ac, tU_pe)
                    udst = bufUf[p2] if lvl == 6 else bufU[lvl % 2]
                    if lvl == 6:
                        b.wait(ac, tk["ufin_free"][p2])
                    tok_u = b.inc(ac.activation(udst[:], PC.t[:], AF.Copy), "ac")
                    PC.readers.append(tok_u)
                    ucur = udst
                    b.wait(ve, tW_pe)
                    tok_w = b.inc(ve.tensor_copy(bufW[lvl % 2][:], PS.t[:]), "dv")
                    PS.readers.append(tok_w)
                    wcur = bufW[lvl % 2]
                    if lvl == 0:
                        tk["rhs0_free"][p2] = tW_pe
                PT.pre_write(b, te)
                b.waits(te, [tok_w])
                for h in range(4):
                    inst = te.transpose(PT.t[:, h * 128:(h + 1) * 128],
                                        bufW[0][:, h * 128:(h + 1) * 128], cs["id_bf"][:])
                tpe = b.inc(inst, "pe")
                PT.writer = tpe
                b.waits(ac, [tpe, tk["nw_free"][p2]])
                tnw = b.inc(ac.activation(nwfm[p2][:], PT.t[:, 0:512], AF.Copy), "ac")
                PT.readers.append(tnw)
                tk["lad"][p2] = (tok_u, tnw)
                if j == 0:
                    tap("fu0", bufUf[0][:, 0:512])
                    tap("fw0", bufW[0][:, 0:512])

            def back_z(j):
                p2 = j % 2
                p3 = j % 3
                sm = smalls[p3]
                zcur, znxt = z32[j % 2], z32[(j + 1) % 2]
                i, ccn = j // 4, j % 4
                t_lo = ccn * 128
                qfm = lambda h: qkv[i % 2][:, h * 512 + t_lo: h * 512 + t_lo + 128]
                FB1.pre_write(b, te)
                b.waits(te, list(tk["lad"][p2]) + [tk["z_ok"]])
                for h in range(4):
                    te.matmul(FB1.t[:, h * 128:(h + 1) * 128], cs["id_bf"][:],
                              bufUf[p2][:, h * 128:(h + 1) * 128], start=True, stop=False)
                    inst = te.matmul(FB1.t[:, h * 128:(h + 1) * 128],
                                     nwfm[p2][:, h * 128:(h + 1) * 128],
                                     zbf[:, h * 128:(h + 1) * 128], start=False, stop=True)
                tUm = b.inc(inst, "pe")
                FB1.writer = tUm
                tk["ufin_free"][p2] = tUm
                tk["nw_free"][p2] = tUm
                PS.pre_write(b, te)
                for h in range(4):
                    inst = te.matmul(PS.t[:, h * 128:(h + 1) * 128], qfm(h),
                                     zbf[:, h * 128:(h + 1) * 128], start=True, stop=True)
                tO1 = b.inc(inst, "pe")
                PS.writer = tO1
                b.waits(ve, [tUm, tk["usb_free"]])
                tusb = b.inc(ve.tensor_copy(u_sb[:], FB1.t[:]), "dv")
                b.wait(ve, tUm)
                for h in range(4):
                    inst = ve.tensor_scalar(usc[:, h * 128:(h + 1) * 128],
                                            FB1.t[:, h * 128:(h + 1) * 128],
                                            sm[:, SM["cs"] + h: SM["cs"] + h + 1], None, ALU.mult)
                tusc = b.inc(inst, "dv")
                FB1.readers.append(tusc)
                FB1.readers.append(tusb)
                tk["usb"] = tusb
                b.wait(ve, tO1)
                for h in range(4):
                    inst = ve.tensor_scalar(rz_t[:, h * 128:(h + 1) * 128],
                                            PS.t[:, h * 128:(h + 1) * 128],
                                            sm[:, SM["gamT"] + h: SM["gamT"] + h + 1],
                                            None, ALU.mult)
                trz = b.raw(ve, inst, "dv")
                PS.readers.append(trz)
                PS.pre_write(b, te)
                b.waits(te, [tusc, tk["kvtm"][p3]])
                for h in range(4):
                    inst = te.matmul(PS.t[:, h * 128:(h + 1) * 128],
                                     ktm[p3][:, h * 128:(h + 1) * 128],
                                     usc[:, h * 128:(h + 1) * 128], start=True, stop=True)
                tzm = b.inc(inst, "pe")
                PS.writer = tzm
                b.wait(ve, tzm)
                for h in range(4):
                    inst = ve.scalar_tensor_tensor(znxt[:, h * 128:(h + 1) * 128],
                                                   zcur[:, h * 128:(h + 1) * 128],
                                                   sm[:, SM["gCb"] + h: SM["gCb"] + h + 1],
                                                   PS.t[:, h * 128:(h + 1) * 128], ALU.mult, ALU.add)
                tz1 = b.raw(ve, inst, "dv")
                PS.readers.append(tz1)
                tz = b.inc(ve.tensor_copy(zbf[:], znxt[:]), "dv")
                tk["z_ok"] = tz

            def back_o(j):
                p3 = j % 3
                sm = smalls[p3]
                FB1.pre_write(b, te)
                b.waits(te, [tk["usb"]])
                for h in range(4):
                    inst = te.matmul(FB1.t[:, h * 128:(h + 1) * 128],
                                     dsm[p3][:, h * 128:(h + 1) * 128],
                                     u_sb[:, h * 128:(h + 1) * 128], start=True, stop=True)
                tO = b.inc(inst, "pe")
                FB1.writer = tO
                tk["usb_free"] = tO
                b.wait(ve, tO)
                for h in range(4):
                    inst = ve.tensor_tensor(o_sb[:, h * 128:(h + 1) * 128],
                                            rz_t[:, h * 128:(h + 1) * 128],
                                            FB1.t[:, h * 128:(h + 1) * 128], ALU.add)
                tO2 = b.inc(inst, "dv")
                FB1.readers.append(tO2)
                b.wait(ve, tO2)
                for h in range(4):
                    inst = ve.reduce_sum(sm[:, SM["osum"] + h: SM["osum"] + h + 1],
                                         o_sb[:, h * 128:(h + 1) * 128], AXX)
                tsum = b.raw(ve, inst, "dv")
                b.wait(ac, tsum)
                for h in range(4):
                    inst = ac.activation(sqscr[:, 0:128], o_sb[:, h * 128:(h + 1) * 128], AF.Square,
                                         accum_out=sm[:, SM["osq"] + h: SM["osq"] + h + 1])
                    if h < 3:
                        b.raw(ac, inst, "ac")
                tsq = b.inc(inst, "ac")
                b.wait(ve, tsq)
                b.raw(ve, ve.tensor_scalar(sm[:, SM["omu"]:SM["omu"] + 4],
                                           sm[:, SM["osum"]:SM["osum"] + 4],
                                           1.0 / 128.0, None, ALU.mult), "dv")
                b.raw(ve, ve.tensor_tensor(sm[:, SM["tmp"]:SM["tmp"] + 4],
                                           sm[:, SM["omu"]:SM["omu"] + 4],
                                           sm[:, SM["omu"]:SM["omu"] + 4], ALU.mult), "dv")
                b.raw(ve, ve.tensor_scalar(sm[:, SM["osum"]:SM["osum"] + 4],
                                           sm[:, SM["tmp"]:SM["tmp"] + 4],
                                           EPS, None, ALU.subtract), "dv")
                tvar = b.inc(ve.scalar_tensor_tensor(sm[:, SM["orstd"]:SM["orstd"] + 4],
                                                     sm[:, SM["osq"]:SM["osq"] + 4], 1.0 / 128.0,
                                                     sm[:, SM["osum"]:SM["osum"] + 4],
                                                     ALU.mult, ALU.subtract), "dv")
                b.wait(ac, tvar)
                b.raw(ac, ac.activation(sm[:, SM["tmp"]:SM["tmp"] + 4],
                                        sm[:, SM["orstd"]:SM["orstd"] + 4], AF.Ln), "ac")
                trs = b.inc(ac.activation(sm[:, SM["orstd"]:SM["orstd"] + 4],
                                          sm[:, SM["tmp"]:SM["tmp"] + 4], AF.Exp, scale=-0.5), "ac")
                b.waits(ve, [trs, tk.get("og_done")])
                for h in range(4):
                    inst = ve.tensor_scalar(on_b[:, h * 128:(h + 1) * 128],
                                            o_sb[:, h * 128:(h + 1) * 128],
                                            sm[:, SM["omu"] + h: SM["omu"] + h + 1],
                                            sm[:, SM["orstd"] + h: SM["orstd"] + h + 1],
                                            ALU.subtract, ALU.mult)
                tOn = b.inc(inst, "dv")
                par2 = j % 2
                b.waits(gp, [tOn, tk["g"][p3], tk["og_free"][par2], t_init])
                gslc = g_ch[p3][:]
                b.raw(gp, gp.tensor_tensor(gcA[:], on_b[:], lnw_s[:], ALU.mult), "gp")
                b.raw(gp, gp.tensor_tensor(gcB[:], gcA[:], gslc, ALU.mult), "gp")
                b.raw(gp, gp.tensor_tensor(gcA[:], gslc, lnb_s[:], ALU.mult), "gp")
                tog = b.inc(gp.tensor_tensor(og_sb[par2][:], gcB[:], gcA[:], ALU.add), "gp")
                tk["g_free"][p3] = tog
                tk["og_done"] = tog
                b.wait(sy, tog)
                tk["og_free"][par2] = b.dma(sy, f"ogst{par2}", ag_in[j * 128:(j + 1) * 128, :],
                                            og_sb[par2][:])
                tk["sm_free"][p3] = tOn
                if j == 0:
                    tap("o0", o_sb[:, 0:512])

            nch = _KTILES * CPT
            for tcn in range(4):
                xn_ln(0, tcn, None)
                xn_tr(0, tcn)
            for m in range(nch + 2):
                if m < nch:
                    if m % 4 == 0:
                        tile_rest(m // 4)
                    front_a(m)
                if 1 <= m <= nch:
                    ladder(m - 1)
                if m >= 2:
                    back_z(m - 2)
                if m < nch:
                    front_b(m)
                if m < nch and (m // 4 + 1) < _KTILES:
                    xn_ln(m // 4 + 1, m % 4, tk["nd"][m % 2])
                if m >= 2:
                    back_o(m - 2)
                if m < nch and (m // 4 + 1) < _KTILES:
                    xn_tr(m // 4 + 1, m % 4)
            t_scan_pe = b.tok("pe")

        if _KPH == "p":
            with ExitStack() as arenaS, nc.sbuf_tensor("bounce", [128, 2048], F32) as bnc:
                prev = None
                for tc in range(16):
                    b.waits(sy, [prev])
                    tld = b.dma(sy, "bnc", bnc[:], dram["xout"][tc * 128:(tc + 1) * 128, :])
                    b.wait(sy, tld)
                    prev = b.dma(sy, "bnc2", out_d[tc * 128:(tc + 1) * 128, :], bnc[:])
                b.wait(sy, b.tok("bnc2"))
                for snm in ("ogst0", "ogst1"):
                    b.wait(sy, b.tok(snm))
            return nc, rank512
        with ExitStack() as arenaF:
            EF = arenaF.enter_context
            sf = lambda name, shape, dt: EF(nc.sbuf_tensor(name, shape, dt))
            fx_tm = [sf(f"fx_tm{p}", [128, 2048], F32) for p in range(2)]
            fsq = sf("fsq", [128, 1024], F32)
            fxn_tm = sf("fxn_tm", [128, 2048], BF16)
            fxn_fm = sf("fxn_fm", [128, 16 * cap], BF16)
            fh = sf("fh", [128, 11 * cap], BF16)
            fgu = [[sf(f"fgu{m}{p}", [128, 2048], BF16) for p in range(2)] for m in range(2)]
            fdwb = [sf(f"fdwb{p}", [128, 11 * 512], BF16) for p in range(2)]
            fdo = [sf(f"fdo{p}", [128, 512], BF16) for p in range(2)]
            fhtmp = sf("fhtmp", [128, 512], BF16)

            b.wait(gp, b.tok("gp"))
            tzp = b.inc(gp.memset(fdo[0][:], 0.0), "gp")
            b.wait(sy, tzp)
            tzd = b.dma(sy, "zpd",
                        ag_in[OGROWS + 4 * NE * cap: OGROWS + 4 * NE * cap + NPAD * 4, :],
                        fdo[0][0:NPAD * 4, :])

            ft = {"xn_done": None, "fx_free": [None, None],
                  "fgu_free": [[None, None], [None, None]], "fdw_free": [None, None],
                  "fdo_free": [tzd, None], "h_free": None, "fxn_free": None}

            for e in range(NE):
                b.waits(te, [ft["fxn_free"]])
                for rc in range(ncap):
                    pr = rc % 2
                    b.waits(sy, [ft["fx_free"][pr]])
                    tld = b.dma(sy, f"fxl{pr}", fx_tm[pr][:],
                                dram["fx"][e * cap + rc * 128: e * cap + (rc + 1) * 128, :])
                    b.wait(ve, tld)
                    b.raw(ve, ve.reduce_sum(stat[:, 0:1], fx_tm[pr][:], AXX), "dv")
                    b.waits(ac, [tld, ft["fx_free"][1 - pr]])
                    sqh = fsq[:, 0:512]
                    b.raw(ac, ac.activation(sqh, fx_tm[pr][:, 0:512], AF.Square, accum_out=stat[:, 1:2]), "ac")
                    b.raw(ac, ac.activation(sqh, fx_tm[pr][:, 512:1024], AF.Square, accum_out=stat[:, 2:3]), "ac")
                    b.raw(ac, ac.activation(sqh, fx_tm[pr][:, 1024:1536], AF.Square, accum_out=stat[:, 9:10]), "ac")
                    tsq = b.inc(ac.activation(sqh, fx_tm[pr][:, 1536:2048], AF.Square,
                                              accum_out=stat[:, 10:11]), "ac")
                    b.wait(ve, tsq)
                    ve.tensor_scalar(stat[:, 3:4], stat[:, 0:1], 1.0 / D, None, ALU.mult)
                    ve.tensor_tensor(stat[:, 11:12], stat[:, 1:2], stat[:, 2:3], ALU.add)
                    b.raw(ve, ve.tensor_tensor(stat[:, 12:13], stat[:, 9:10], stat[:, 10:11], ALU.add), "dv")
                    ve.tensor_tensor(stat[:, 13:14], stat[:, 3:4], stat[:, 3:4], ALU.mult)
                    b.raw(ve, ve.tensor_tensor(stat[:, 5:6], stat[:, 11:12], stat[:, 12:13], ALU.add), "dv")
                    b.raw(ve, ve.tensor_scalar(stat[:, 4:5], stat[:, 13:14], EPS, None, ALU.subtract), "dv")
                    tvar = b.raw(ve, ve.scalar_tensor_tensor(stat[:, 6:7], stat[:, 5:6], 1.0 / D,
                                                             stat[:, 4:5], ALU.mult, ALU.subtract), "dv")
                    b.wait(ac, tvar)
                    b.raw(ac, ac.activation(stat[:, 7:8], stat[:, 6:7], AF.Ln), "ac")
                    trs = b.inc(ac.activation(stat[:, 8:9], stat[:, 7:8], AF.Exp, scale=-0.5), "ac")
                    b.waits(ve, [trs, ft["xn_done"]])
                    tnm = b.inc(ve.tensor_scalar(fxn_tm[:], fx_tm[pr][:], stat[:, 3:4], stat[:, 8:9],
                                                 ALU.subtract, ALU.mult), "dv")
                    ft["fx_free"][pr] = tnm
                    for half in range(2):
                        PT.pre_write(b, te)
                        b.wait(te, tnm)
                        for fc8 in range(8):
                            fc = half * 8 + fc8
                            inst = te.transpose(PT.t[:, fc8 * 128:(fc8 + 1) * 128],
                                                fxn_tm[:, fc * 128:(fc + 1) * 128], cs["id_bf"][:])
                        tpe = b.inc(inst, "pe")
                        PT.writer = tpe
                        b.wait(ac, tpe)
                        for fc8 in range(8):
                            fc = half * 8 + fc8
                            inst = ac.activation(fxn_fm[:, fc * cap + rc * 128: fc * cap + (rc + 1) * 128],
                                                 PT.t[:, fc8 * 128:(fc8 + 1) * 128], AF.Copy)
                        tac = b.inc(inst, "ac")
                        PT.readers.append(tac)
                        ft["xn_done"] = tpe
                t_fxn = b.tok("ac")

                for ic in range(11):
                    pr = ic % 2
                    tws = []
                    for m, wname in ((0, "fgw"), (1, "fuw")):
                        b.waits(gp, [ft["fgu_free"][m][pr]])
                        tws.append(b.dma(gp, f"fgu{m}{pr}", fgu[m][pr][:],
                                         dram[wname][e:e + 1, ic:ic + 1]
                                         .rearrange("a b p f -> (a b p) f")))
                    banks = (PA, PB) if pr == 0 else (PD, PE)
                    r0 = 0
                    for rb in rblocks:
                        for m in range(2):
                            bank = banks[m]
                            bank.pre_write(b, te)
                            b.waits(te, [tws[m], t_fxn])
                            for fc in range(16):
                                inst = te.matmul(bank.t[:, 0:rb],
                                                 fgu[m][pr][:, fc * 128:(fc + 1) * 128],
                                                 fxn_fm[:, fc * cap + r0: fc * cap + r0 + rb],
                                                 start=(fc == 0), stop=(fc == 15))
                            tpe = b.inc(inst, "pe")
                            bank.writer = tpe
                            if rb == rblocks[-1] and r0 + rb >= cap:
                                ft["fgu_free"][m][pr] = tpe
                        b.waits(ac, [b.tok("pe"), ft.get("prev_th")])
                        tsl = b.inc(ac.activation(fhtmp[:, 0:rb], banks[0].t[:, 0:rb], AF.Silu), "ac")
                        banks[0].readers.append(tsl)
                        banks[1].pre_read(b, ve)
                        b.waits(ve, [tsl, ft["h_free"]])
                        th = b.inc(ve.tensor_tensor(fh[:, ic * cap + r0: ic * cap + r0 + rb],
                                                    fhtmp[:, 0:rb], banks[1].t[:, 0:rb], ALU.mult), "dv")
                        banks[1].readers.append(th)
                        ft["prev_th"] = th
                        r0 += rb
                t_h = b.tok("dv")
                ft["fxn_free"] = b.tok("pe")

                for hg in range(4):
                    pr = hg % 2
                    b.waits(gp, [ft["fdw_free"][pr]])
                    twd = b.dma(gp, f"fdw{pr}", fdwb[pr][:],
                                dram["fdw"][e:e + 1, hg:hg + 1].rearrange("a b p f -> (a b p) f"))
                    for rc in range(ncap):
                        bank = (PC, PF)[rc % 2]
                        bank.pre_write(b, te)
                        b.waits(te, [twd, t_h])
                        for ic in range(11):
                            inst = te.matmul(bank.t[:],
                                             fh[:, ic * cap + rc * 128: ic * cap + (rc + 1) * 128],
                                             fdwb[pr][:, ic * 512:(ic + 1) * 512],
                                             start=(ic == 0), stop=(ic == 10))
                        tpe = b.inc(inst, "pe")
                        bank.writer = tpe
                        if rc == ncap - 1:
                            ft["fdw_free"][pr] = tpe
                        dpr = (hg * ncap + rc) % 2
                        b.waits(ve, [tpe, ft["fdo_free"][dpr]])
                        tds = b.inc(ve.tensor_scalar(fdo[dpr][:], bank.t[:],
                                                     fw_s[:, e * ncap + rc: e * ncap + rc + 1],
                                                     None, ALU.mult), "dv")
                        bank.readers.append(tds)
                        b.wait(sy, tds)
                        d0 = (e * cap + rc * 128) * 4
                        dst = ag_in[OGROWS + d0 + hg: OGROWS + d0 + 512 + hg, :] \
                            .rearrange("(p f) c -> p f c", f=4)[:, 0]
                        ft["fdo_free"][dpr] = b.dma(sy, f"fdst{dpr}", dst, fdo[dpr][:])
                ft["h_free"] = b.tok("pe")

        if _KPH == "pf":
            with nc.sbuf_tensor("bounce", [128, 2048], F32) as bnc:
                prev = None
                for tc in range(16):
                    b.waits(sy, [prev])
                    tld = b.dma(sy, "bnc", bnc[:], dram["xout"][tc * 128:(tc + 1) * 128, :])
                    b.wait(sy, tld)
                    prev = b.dma(sy, "bnc2", out_d[tc * 128:(tc + 1) * 128, :], bnc[:])
                b.wait(sy, b.tok("bnc2"))
                for snm in ("ogst0", "ogst1", "fdst0", "fdst1", "zpd"):
                    b.wait(sy, b.tok(snm))
            return nc, rank512
        for s in ("ogst0", "ogst1", "fdst0", "fdst1", "zpd"):
            b.wait(gp, b.tok(s))
        gp.collective_compute(
            "AllGather", ALU.bypass, replica_groups=[list(range(8))],
            ins=[ag_in.ap().opt()], outs=[ag_out.ap().opt()],
        ).then_inc(b.sem("cc"))
        b.cnt["cc"] += 1
        t_ag = b.tok("cc")

        if _KPH == "pfa":
            b.wait(gp, t_ag)
            with nc.sbuf_tensor("bounce", [128, 2048], F32) as bnc:
                prev = None
                for tc in range(16):
                    b.waits(sy, [prev])
                    tld = b.dma(sy, "bnc", bnc[:], dram["xout"][tc * 128:(tc + 1) * 128, :])
                    b.wait(sy, tld)
                    prev = b.dma(sy, "bnc2", out_d[tc * 128:(tc + 1) * 128, :], bnc[:])
                b.wait(sy, b.tok("bnc2"))
                b.wait(sy, t_ag)
            return nc, rank512
        b.waits(gp, [t_scan_pe])
        two = b.dma(gp, "wol", w_sb[:].rearrange("p (fc o) -> p fc o", o=D),
                    dram["wo"][:, :].rearrange("(fc p) o -> p fc o", p=128))

        with ExitStack() as arenaW:
            EW = arenaW.enter_context
            sw = lambda name, shape, dt: EW(nc.sbuf_tensor(name, shape, dt))
            ogt = [sw(f"ogt{q}", [128, 512], BF16) for q in range(4)]
            xo_fm = [sw(f"xo_fm{p}", [128, 2048], BF16) for p in range(2)]
            fcb = [sw(f"fcb{q}", [128, 2048], BF16) for q in range(4)]
            fcA = sw("fcA", [128, 2048], BF16)
            fcB = sw("fcB", [128, 2048], BF16)
            fsum = sw("fsum", [128, 2048], F32)
            xrl = [sw(f"xrl{p}", [128, 2048], F32) for p in range(2)]
            outsb = [sw(f"outsb{p}", [128, 2048], F32) for p in range(2)]
            wt1 = sw("wt1", [128, 2048], F32)

            wtk = {"ogt_free": [None] * 4, "fcb_free": [None] * 4,
                   "xofm_free": [None, None], "fsum_free": None,
                   "xrl_free": [None, None], "out_free": [None, None]}

            for tc in range(16):
                pr = tc % 2
                b.wait(gp, t_ag)
                tog = []
                for jr in range(4):
                    b.waits(gp, [wtk["ogt_free"][jr]])
                    if _KNOG:
                        tog.append(b.dma(gp, f"ogth{jr}", ogt[jr][:],
                                         ag_out[tc * 128:(tc + 1) * 128, :]))
                        continue
                    gp.indirect_dma_start(
                        out=ogt[jr][:], out_offset=None, in_=ag_out[:, :],
                        in_offset=IndirectOffsetOnAxis(oidx_s[:, tc * 4 + jr: tc * 4 + jr + 1], 0),
                    ).then_inc(b.sem(f"ogth{jr}"), 16)
                    b.cnt[f"ogth{jr}"] += 16
                    tog.append(b.tok(f"ogth{jr}"))
                    if _KGSER:
                        b.wait(gp, b.tok(f"ogth{jr}"))
                tfc = []
                for j2 in range(4):
                    b.waits(gp, [wtk["fcb_free"][j2]])
                    if _KNOG:
                        tfc.append(b.dma(gp, f"fcbg{j2}", fcb[j2][:],
                                         ag_out[tc * 512:(tc + 1) * 512, :]
                                         .rearrange("(p f) c -> p (f c)", f=4)))
                        continue
                    gp.indirect_dma_start(
                        out=fcb[j2][:], out_offset=None, in_=ag_out[:, :],
                        in_offset=IndirectOffsetOnAxis(cidx_s[:, tc * 4 + j2: tc * 4 + j2 + 1], 0),
                    ).then_inc(b.sem(f"fcbg{j2}"), 16)
                    b.cnt[f"fcbg{j2}"] += 16
                    tfc.append(b.tok(f"fcbg{j2}"))
                    if _KGSER:
                        b.wait(gp, b.tok(f"fcbg{j2}"))
                b.waits(sy, [wtk["xrl_free"][pr]])
                txr = b.dma(sy, f"xrl{pr}", xrl[pr][:], dram["xout"][tc * 128:(tc + 1) * 128, :])

                for half in range(2):
                    PT.pre_write(b, te)
                    b.waits(te, tog[half * 2:(half + 1) * 2] + [wtk["xofm_free"][pr]])
                    for q2 in range(2):
                        jr = half * 2 + q2
                        for cc4 in range(4):
                            inst = te.transpose(PT.t[:, q2 * 512 + cc4 * 128: q2 * 512 + (cc4 + 1) * 128],
                                                ogt[jr][:, cc4 * 128:(cc4 + 1) * 128], cs["id_bf"][:])
                    tpe = b.inc(inst, "pe")
                    PT.writer = tpe
                    for q2 in range(2):
                        wtk["ogt_free"][half * 2 + q2] = tpe
                    b.wait(ac, tpe)
                    for q2 in range(2):
                        jr = half * 2 + q2
                        for cc4 in range(4):
                            fc = jr * 4 + cc4
                            inst = ac.activation(xo_fm[pr][:, fc * 128:(fc + 1) * 128],
                                                 PT.t[:, q2 * 512 + cc4 * 128: q2 * 512 + (cc4 + 1) * 128],
                                                 AF.Copy)
                    tac = b.inc(inst, "ac")
                    PT.readers.append(tac)
                txo = b.tok("ac")

                b.waits(ve, tfc + [wtk["fsum_free"]])
                ve.tensor_tensor(fcA[:], fcb[0][:], fcb[1][:], ALU.add)
                b.raw(ve, ve.tensor_tensor(fcB[:], fcb[2][:], fcb[3][:], ALU.add), "dv")
                tfs = b.inc(ve.tensor_tensor(fsum[:], fcA[:], fcB[:], ALU.add), "dv")
                for j2 in range(4):
                    wtk["fcb_free"][j2] = tfs

                b.waits(te, [two, txo])
                banksW = (PA, PB, PD, PE)
                for og4 in range(4):
                    bank = banksW[og4]
                    bank.pre_write(b, te)
                    for fc in range(16):
                        inst = te.matmul(bank.t[:],
                                         xo_fm[pr][:, fc * 128:(fc + 1) * 128],
                                         w_sb[:, fc * 2048 + og4 * 512: fc * 2048 + (og4 + 1) * 512],
                                         start=(fc == 0), stop=(fc == 15))
                    tpe = b.inc(inst, "pe")
                    bank.writer = tpe
                    if og4 == 3:
                        wtk["xofm_free"][pr] = tpe
                b.waits(ve, [tpe, txr, tfs, wtk["out_free"][pr]])
                for og4 in range(4):
                    inst = ve.tensor_tensor(wt1[:, og4 * 512:(og4 + 1) * 512], banksW[og4].t[:],
                                            xrl[pr][:, og4 * 512:(og4 + 1) * 512], ALU.add)
                tmid = b.raw(ve, inst, "dv")
                for og4 in range(4):
                    inst = ve.tensor_tensor(outsb[pr][:, og4 * 512:(og4 + 1) * 512],
                                            wt1[:, og4 * 512:(og4 + 1) * 512],
                                            fsum[:, og4 * 512:(og4 + 1) * 512], ALU.add)
                tout = b.inc(inst, "dv")
                for og4 in range(4):
                    banksW[og4].readers.append(tmid)
                wtk["fsum_free"] = b.tok("dv")
                wtk["xrl_free"][pr] = b.tok("dv")
                b.wait(sy, b.tok("dv"))
                wtk["out_free"][pr] = b.dma(sy, f"outst{pr}", out_d[tc * 128:(tc + 1) * 128, :],
                                            outsb[pr][:])

            b.wait(sy, b.tok("outst0"))
            b.wait(sy, b.tok("outst1"))
            if _TAPS:
                b.wait(sy, b.tok("dbgst"))

    return nc, rank512


def _ffn_prep(topk_idx, topk_weight, n_experts=16, topk=4):
    fe = np.asarray(topk_idx).reshape(-1)
    fw = np.asarray(topk_weight).reshape(-1).astype(np.float32)
    n_pairs = fe.shape[0]
    tok_of_pair = np.arange(n_pairs) // topk
    order = np.argsort(fe, kind="stable")
    row_of_pair = np.empty(n_pairs, np.int64)
    row_of_pair[order] = np.arange(n_pairs)
    gs = np.bincount(fe, minlength=n_experts)
    return tok_of_pair[order], fw[order], gs, row_of_pair


def kernel(**inputs):
    x = np.ascontiguousarray(np.asarray(inputs["L0_attn_x"], np.float32))
    Wq = np.asarray(inputs["L0_attn_Wq"], np.float32)
    Wk = np.asarray(inputs["L0_attn_Wk"], np.float32)
    Wv = np.asarray(inputs["L0_attn_Wv"], np.float32)
    Wa = np.asarray(inputs["L0_attn_Wa"], np.float32)
    ba = np.asarray(inputs["L0_attn_ba"], np.float32)
    Wb = np.asarray(inputs["L0_attn_Wb"], np.float32)
    bb = np.asarray(inputs["L0_attn_bb"], np.float32)
    Wg = np.asarray(inputs["L0_attn_Wg"], np.float32)
    Wo = np.asarray(inputs["L0_attn_Wo"], np.float32)
    qcw = np.asarray(inputs["L0_attn_qcw"], np.float32)
    qcb = np.asarray(inputs["L0_attn_qcb"], np.float32)
    kcw = np.asarray(inputs["L0_attn_kcw"], np.float32)
    kcb = np.asarray(inputs["L0_attn_kcb"], np.float32)
    vcw = np.asarray(inputs["L0_attn_vcw"], np.float32)
    vcb = np.asarray(inputs["L0_attn_vcb"], np.float32)
    ln_w = np.asarray(inputs["L0_attn_ln_w"], np.float32)
    ln_b = np.asarray(inputs["L0_attn_ln_b"], np.float32)
    tidx = np.asarray(inputs["L0_ffn_topk_idx"])
    tw = np.asarray(inputs["L0_ffn_topk_weight"], np.float32)
    gw = np.asarray(inputs["L0_ffn_gate_weight"], np.float32)
    uw = np.asarray(inputs["L0_ffn_up_weight"], np.float32)
    dw = np.asarray(inputs["L0_ffn_down_weight"], np.float32)

    tok_rows, w_rows, gs, row_of_pair = _ffn_prep(tidx, tw)
    cap = 640
    if gs.max() > cap:
        cap = int(-(-int(gs.max()) // 128) * 128)
    if cap not in _CACHE:
        _CACHE[cap] = build_program(cap)
    nc, rank512 = _CACHE[cap]

    starts = np.concatenate([[0], np.cumsum(gs)])
    pad_row = np.empty(len(tok_rows), np.int64)
    for e in range(16):
        seg = slice(starts[e], starts[e + 1])
        pad_row[seg] = (e // NE) * 0
        pad_row[seg] = (e % NE) * cap + np.arange(gs[e])
    rank_of_row = np.repeat(np.arange(16) // NE, gs)

    in_maps = []
    for c in range(8):
        bi, hg = c // 4, c % 4
        hs = hg * 4
        m = {}
        m["xb"] = x[bi]
        m["xout"] = np.ascontiguousarray(x[bi, 2048 * hg: 2048 * (hg + 1)])
        m["wq"] = np.ascontiguousarray(Wq[:, hs * 128:(hs + 4) * 128])
        m["wk"] = np.ascontiguousarray(Wk[:, hs * 128:(hs + 4) * 128])
        m["wv"] = np.ascontiguousarray(Wv[:, hs * 128:(hs + 4) * 128])
        m["wg"] = np.ascontiguousarray(Wg[:, hs * 128:(hs + 4) * 128])
        m["wo"] = Wo
        m["wab"] = np.ascontiguousarray(
            np.concatenate([Wa[:, hg * 4: hg * 4 + 4], Wb[:, hg * 4: hg * 4 + 4]], axis=1))
        m["ba4"] = np.ascontiguousarray(ba[hg * 4: hg * 4 + 4].reshape(4, 1))
        m["bb4"] = np.ascontiguousarray(bb[hg * 4: hg * 4 + 4].reshape(4, 1))
        cwA = np.zeros((128, 48), np.float32)
        cbA = np.zeros((128, 12), np.float32)
        for mi, (cwm, cbm) in enumerate(((qcw, qcb), (kcw, kcb), (vcw, vcb))):
            for oc in range(4):
                ch0 = hs * 128 + oc * 128
                cwA[:, (mi * 4 + oc) * 4:(mi * 4 + oc) * 4 + 4] = cwm[ch0:ch0 + 128, :]
                cbA[:, mi * 4 + oc] = cbm[ch0:ch0 + 128]
        m["cw"] = cwA
        m["cb"] = cbA
        m["lnwB"] = np.ascontiguousarray(np.tile(ln_w[None, :], (128, 4)))
        m["lnbB"] = np.ascontiguousarray(np.tile(ln_b[None, :], (128, 4)))
        fxr = np.zeros((NE * cap, D), np.float32)
        fwc = np.zeros((128, NE * (cap // 128)), np.float32)
        fgwA = np.zeros((NE, 11, 128, D), np.float32)
        fuwA = np.zeros((NE, 11, 128, D), np.float32)
        fdwA = np.zeros((NE, 4, 128, 11 * 512), np.float32)
        for el in range(NE):
            e = NE * c + el
            n = gs[e]
            seg = slice(starts[e], starts[e + 1])
            fxr[el * cap: el * cap + n] = x[0, tok_rows[seg]]
            wpad = np.zeros(cap, np.float32)
            wpad[:n] = w_rows[seg]
            fwc[:, el * (cap // 128):(el + 1) * (cap // 128)] = wpad.reshape(cap // 128, 128).T
            fgwA[el] = gw[e].reshape(11, 128, 16, 128).transpose(0, 3, 2, 1).reshape(11, 128, D)
            fuwA[el] = uw[e].reshape(11, 128, 16, 128).transpose(0, 3, 2, 1).reshape(11, 128, D)
            fdwA[el] = dw[e].reshape(4, 512, 11, 128).transpose(0, 3, 2, 1).reshape(4, 128, 11 * 512)
        m["fx"] = fxr
        m["fwcol"] = fwc
        m["fgw"] = fgwA
        m["fuw"] = fuwA
        m["fdw"] = fdwA
        zrow = 0 * rank512 + OGROWS + 4 * (NE * cap)
        cidx = np.full((128, 64), zrow, np.int64)
        base_flat = 2048 * c
        if base_flat < 2048:
            for tc in range(16):
                for j2 in range(4):
                    toks = base_flat + tc * 128 + np.arange(128)
                    pairs = toks * 4 + j2
                    rows = row_of_pair[pairs]
                    cidx[:, tc * 4 + j2] = (rank_of_row[rows] * rank512 + OGROWS
                                            + 4 * (pad_row[rows]))
        m["cidx"] = cidx.astype(np.int32)
        g0 = 4 * bi
        oidx = np.zeros((128, 64), np.int64)
        for tc in range(16):
            for jr in range(4):
                oidx[:, tc * 4 + jr] = ((g0 + jr) * rank512 + 2048 * hg + tc * 128
                                        + np.arange(128))
        m["oidx"] = oidx.astype(np.int32)
        in_maps.append(m)

    res = run_bass_kernel_spmd(nc, in_maps, core_ids=list(range(8)))
    shards = [np.asarray(res.results[c]["out"]) for c in range(8)]
    if _TAPS:
        kernel.dbg = [np.asarray(res.results[c]["dbg"]) for c in range(8)]
    return np.concatenate(shards, axis=0).reshape(B_, T, D)


# version 15
# speedup vs baseline: 1.2966x; 1.2966x over previous
import os
import numpy as np
import ml_dtypes
import concourse.bass as bass
import concourse.mybir as mybir
from concourse.bass_utils import run_bass_kernel_spmd
from concourse.bass_core_rust import IndirectOffsetOnAxis

F32 = mybir.dt.float32
BF16 = mybir.dt.bfloat16
I32 = mybir.dt.int32
AF = mybir.ActivationFunctionType
ALU = mybir.AluOpType
AXX = mybir.AxisListType.X

B_, T, D = 2, 8192, 2048
C = 128
NCHUNK = T // C
TILE = 512
CPT = TILE // C
NTILE = T // TILE
NE = 2
NPAD = 8
OGROWS = T
EPS = 1e-5
NEG = -3.0e38
POS = 3.0e38

_TAPS = set(s for s in os.environ.get("KDBG", "").split(",") if s)
_KPH = os.environ.get("KPH", "all")
_KTILES = int(os.environ.get("KTILES", "16"))
_KNOG = os.environ.get("KNOG", "0") == "1"
_KGSER = os.environ.get("KGSER", "1") == "1"
_CACHE = {}


class Bld:
    def __init__(self, nc):
        self.nc = nc
        self.sems = {}
        self.cnt = {}
        self.last_wait = {}
        self._ctx = None

    def sem(self, name):
        if name not in self.sems:
            self.sems[name] = self._ctx.enter_context(self.nc.semaphore("s_" + name))
            self.cnt[name] = 0
        return self.sems[name]

    def inc(self, inst, name, by=1):
        inst.then_inc(self.sem(name), by)
        self.cnt[name] += by
        return (name, self.cnt[name])

    def tok(self, name):
        self.sem(name)
        return (name, self.cnt[name])

    def wait(self, eng, tok):
        if tok is None:
            return
        name, val = tok
        if val <= 0:
            return
        key = (eng.engine, name)
        if self.last_wait.get(key, -1) >= val:
            return
        eng.wait_ge(self.sems[name], val)
        self.last_wait[key] = val

    def waits(self, eng, toks):
        for t in toks:
            self.wait(eng, t)

    def raw(self, eng, inst, sem):
        tok = self.inc(inst, sem)
        self.wait(eng, tok)
        return tok

    def dma(self, eng, name, out, in_, **kw):
        eng.dma_start(out=out, in_=in_, **kw).then_inc(self.sem(name), 16)
        self.cnt[name] += 16
        return (name, self.cnt[name])


class PBank:
    def __init__(self, tens):
        self.t = tens
        self.writer = None
        self.readers = []

    def pre_write(self, b, eng):
        b.waits(eng, self.readers)
        self.readers = []

    def pre_read(self, b, eng):
        b.wait(eng, self.writer)


def build_program(cap):
    drows = NE * cap + NPAD
    rank512 = OGROWS + drows * 4
    ncap = cap // 128
    rblocks = []
    r = cap
    while r > 0:
        rblocks.append(min(512, r))
        r -= min(512, r)

    nc = bass.Bass("TRN2", target_bir_lowering=False)
    b = Bld(nc)

    dram = {}
    for name, shape, dt in (
        ("xb", [T, D], F32), ("xout", [2048, D], F32),
        ("wq", [D, 512], F32), ("wk", [D, 512], F32), ("wv", [D, 512], F32),
        ("wg", [D, 512], F32), ("wo", [D, D], F32), ("wab", [D, 8], F32),
        ("ba4", [4, 1], F32), ("bb4", [4, 1], F32),
        ("cw", [128, 48], F32), ("cb", [128, 12], F32),
        ("lnwB", [128, 512], F32), ("lnbB", [128, 512], F32),
        ("fx", [NE * cap, D], F32),
        ("fgw", [NE, 11, 128, D], F32), ("fuw", [NE, 11, 128, D], F32),
        ("fdw", [NE, 4, 128, 11 * 512], F32),
        ("fwcol", [128, NE * ncap], F32),
        ("cidx", [128, 64], I32), ("oidx", [128, 64], I32),
    ):
        dram[name] = nc.dram_tensor(name, shape, dt, kind="ExternalInput")
    out_d = nc.dram_tensor("out", [2048, D], F32, kind="ExternalOutput")
    dbg_d = nc.dram_tensor("dbg", [2048, D], F32, kind="ExternalOutput") if _TAPS else None
    dbg_row = [0]

    ag_in = nc.dram_tensor("ag_in", [rank512, 512], BF16, kind="Internal")
    drows4 = rank512 - OGROWS
    ago = nc.dram_tensor("ago", [16 * 4096, 512], BF16, kind="Internal", addr_space="Shared")
    agd = nc.dram_tensor("agd", [8 * drows4, 512], BF16, kind="Internal", addr_space="Shared")

    si = np.arange(128)[:, None]
    ti = np.arange(128)[None, :]
    consts = {
        "id_bf": np.eye(128).astype(ml_dtypes.bfloat16),
        "id_f": np.eye(128, dtype=np.float32),
        "ut_s": np.triu(np.ones((128, 128), np.float32)),
        "e127_s": np.where(si == 127, 1.0, 0.0).astype(np.float32) * np.ones((1, 128), np.float32),
        "bndi_s": np.where(si <= ti, 0.0, NEG).astype(np.float32),
        "bnds_s": np.where(si < ti, 0.0, NEG).astype(np.float32),
        "bndt_s": np.where(si > ti, 0.0, POS).astype(np.float32),
        "lts_s": np.where(si > ti, 1.0, 0.0).astype(np.float32),
    }
    ehp = np.zeros((4, 512), np.float32)
    for h in range(4):
        ehp[h, h * 128:(h + 1) * 128] = 1.0
    consts["ehp_s"] = ehp
    cten = {k: nc.inline_tensor(v, name="c_" + k) for k, v in consts.items()}

    from contextlib import ExitStack
    with ExitStack() as ctx:
        b._ctx = ctx
        E = ctx.enter_context
        te, ve, ac, gp, sy = nc.tensor, nc.vector, nc.scalar, nc.gpsimd, nc.sync

        PA = PBank(E(nc.psum_tensor("PA", [128, 512], F32)))
        PB = PBank(E(nc.psum_tensor("PB", [128, 512], F32)))
        PC = PBank(E(nc.psum_tensor("PC", [128, 512], F32)))
        PD = PBank(E(nc.psum_tensor("PD", [128, 512], F32)))
        PE = PBank(E(nc.psum_tensor("PE", [128, 512], F32)))
        PF = PBank(E(nc.psum_tensor("PF", [128, 512], F32)))
        PT = PBank(E(nc.psum_tensor("PT", [128, 1024], BF16)))
        PS = PBank(E(nc.psum_tensor("PS", [128, 512], F32)))
        PARITY = [(PA, PB, PC), (PD, PE, PF)]

        sb = lambda name, shape, dt: E(nc.sbuf_tensor(name, shape, dt))
        cs = {k: sb(k, list(v.shape), BF16 if v.dtype == ml_dtypes.bfloat16 else F32)
              for k, v in consts.items()}
        lnw_s = sb("lnw_s", [128, 512], F32)
        lnb_s = sb("lnb_s", [128, 512], F32)
        cw_s = sb("cw_s", [128, 48], F32)
        cb_s = sb("cb_s", [128, 12], F32)
        ba_s = sb("ba_s", [4, 1], F32)
        bb_s = sb("bb_s", [4, 1], F32)
        w_sb = sb("w_sb", [128, 32768], BF16)
        wab_s = sb("wab_s", [128, 128], BF16)
        z32 = [sb(f"z32_{p}", [128, 512], F32) for p in range(2)]
        zbf = sb("zbf", [128, 512], BF16)
        fw_s = sb("fw_s", [128, NE * ncap], F32)
        cidx_s = sb("cidx_s", [128, 64], I32)
        oidx_s = sb("oidx_s", [128, 64], I32)
        smalls = [sb(f"smalls{p}", [128, 64], F32) for p in range(3)]
        stat = sb("stat", [128, 24], F32)

        SM = dict(laT=0, lbT=4, bT=8, gT=12, ngT=16, gamT=20, gCb=24, cs=28, gCr=44,
                  gpT=32, bgp=36, r2T=40, osum=44, osq=48, omu=52, orstd=56, tmp=60)

        def tap(name, src_ap, rows=128, cols=None, via=None):
            if name not in _TAPS:
                return
            for snm in ("dv", "ac", "pe", "gp"):
                b.wait(sy, b.tok(snm))
            cols = cols if cols is not None else src_ap.shape[-1]
            r = dbg_row[0]
            b.wait(gp, b.tok("dbgst"))
            for snm in ("dv", "ac", "pe", "gp"):
                b.wait(gp, b.tok(snm))
            b.dma(gp, "dbgst", dbg_d[r:r + rows, 0:cols], src_ap)
            for eng in (te, ve, ac, gp, sy):
                b.wait(eng, b.tok("dbgst"))
            dbg_row[0] += rows
            print(f"  [tap {name} -> dbg rows {r}:{r+rows} cols {cols}]")

        for k in cs:
            b.dma(sy, "init", cs[k][:], cten[k].ap())
        for dst, src in ((lnw_s, "lnwB"), (lnb_s, "lnbB"), (cw_s, "cw"),
                         (cb_s, "cb"), (ba_s, "ba4"), (bb_s, "bb4"),
                         (fw_s, "fwcol"), (cidx_s, "cidx"), (oidx_s, "oidx")):
            b.dma(sy, "init", dst[:], dram[src][:, :])
        t_init = b.tok("init")

        for k4, wn in enumerate(("wq", "wk", "wv", "wg")):
            b.dma(gp, "wl", w_sb[:].rearrange("p (m fc o) -> p m fc o", m=4, o=512)[:, k4],
                  dram[wn][:, :].rearrange("(fc p) o -> p fc o", p=128))
        b.dma(gp, "wl", wab_s[:].rearrange("p (fc o) -> p fc o", o=8),
              dram["wab"][:, :].rearrange("(fc p) o -> p fc o", p=128))
        t_wl = b.tok("wl")

        b.wait(gp, t_wl)
        gp.memset(z32[0][:], 0.0)
        tz0 = b.inc(gp.memset(zbf[:], 0.0), "gp")

        with ExitStack() as arenaP:
            EP = arenaP.enter_context
            sp = lambda name, shape, dt: EP(nc.sbuf_tensor(name, shape, dt))
            x_tm = [sp(f"x_tm{p}", [128, 2048], F32) for p in range(2)]
            sqscr = sp("sqscr", [128, 256], F32)
            xn_tm = sp("xn_tm", [128, 2048], BF16)
            xn_fm = sp("xn_fm", [128, 8192], BF16)
            halo_sb = sp("halo_sb", [128, 36], BF16)
            ctmp = [sp(f"ctmp{p}", [128, 512], BF16) for p in range(2)]
            qkv = [sp(f"qkv{p}", [128, 6144], BF16) for p in range(2)]
            g_ch = [sp(f"g_ch{p}", [128, 512], BF16) for p in range(3)]
            zz = sp("zz", [4, 512], F32)
            abt1 = sp("abt1", [4, 512], F32)
            la4 = sp("la4", [4, 512], F32)
            lb4 = sp("lb4", [4, 512], F32)
            b4 = sp("b4", [4, 512], F32)
            rows_cat = sp("rows_cat", [4, 384], F32)
            e_scr = sp("e_scr", [128, 2048], F32)
            nsm1 = [sp(f"nsm1_{p}", [128, 512], BF16) for p in range(2)]
            ntm1 = [sp(f"ntm1_{p}", [128, 512], BF16) for p in range(2)]
            dsm = [sp(f"dsm{p}", [128, 512], BF16) for p in range(3)]
            ktm = [sp(f"ktm{p}", [128, 512], BF16) for p in range(3)]
            vtm = [sp(f"vtm{p}", [128, 512], BF16) for p in range(2)]
            nsm_s = [sp(f"nsm_s{p}", [128, 512], BF16) for p in range(2)]
            ntm_s = [sp(f"ntm_s{p}", [128, 512], BF16) for p in range(2)]
            rhs0 = [sp(f"rhs0_{p}", [128, 512], BF16) for p in range(2)]
            bufU = [sp(f"bufU{p}", [128, 512], BF16) for p in range(2)]
            bufUf = [sp(f"bufUf{p}", [128, 512], BF16) for p in range(2)]
            rbuf = sp("rbuf", [128, 512], BF16)
            usc = sp("usc", [128, 512], BF16)
            rz_t = sp("rz_t", [128, 512], BF16)
            o_sb = sp("o_sb", [128, 512], F32)
            on_b = sp("on_b", [128, 512], BF16)
            gtanh = sp("gtanh", [128, 512], BF16)
            gcA = sp("gcA", [128, 512], BF16)
            gcB = sp("gcB", [128, 512], BF16)
            og_sb = [sp(f"og_sb{p}", [128, 512], BF16) for p in range(2)]

            FB1, FB2, FX = PD, PE, PF

            b.wait(gp, tz0)
            t_cz = b.inc(gp.memset(halo_sb[:], 0.0), "gp")

            tk = {
                "x_free": [None, None], "xn_done": None, "gp_stat": None, "xn_tnm": None,
                "xnfm": None, "qkv_free": [None, None],
                "qkv": [None, None], "g": [None] * 3, "g_free": [None] * 3,
                "ab": None, "ab_free": None, "ab_free2": None,
                "kvtm": [None] * 3, "nd": [None, None],
                "sm_free": [None] * 3, "og_free": [None, None],
                "z_ok": (tz0[0], tz0[1]),
                "gts": None, "rhs0": [None, None], "rhs0_free": [None, None],
                "ufin_free": [[], []], "ufz": [None, None],
                "lad": [None, None], "rtok": [None, None], "og_done": None,
            }

            def xn_ln(i, tcn, tok_escr):
                t0 = i * TILE
                jp = (i * 4 + tcn) % 2
                b.waits(sy, [tk["x_free"][jp]])
                tld = b.dma(sy, f"xl{jp}", x_tm[jp][:], dram["xb"][t0 + tcn * 128: t0 + (tcn + 1) * 128, :])
                b.waits(ve, [tld, tk["gp_stat"]])
                tgs = b.inc(ve.reduce_sum(stat[:, 0:1], x_tm[jp][:], AXX), "dv")
                b.waits(ac, [tld, tk["x_free"][1 - jp], tok_escr])
                tsq = b.inc(ac.activation(e_scr[:], x_tm[jp][:], AF.Square,
                                          accum_out=stat[:, 1:2]), "ac")
                b.waits(ve, [tsq, tgs])
                b.raw(ve, ve.tensor_scalar(stat[:, 3:4], stat[:, 0:1], 1.0 / D, None, ALU.mult), "dv")
                b.raw(ve, ve.tensor_tensor(stat[:, 13:14], stat[:, 3:4], stat[:, 3:4], ALU.mult), "dv")
                b.raw(ve, ve.tensor_scalar(stat[:, 4:5], stat[:, 13:14], EPS, None, ALU.subtract), "dv")
                tvar = b.raw(ve, ve.scalar_tensor_tensor(stat[:, 6:7], stat[:, 1:2], 1.0 / D,
                                                         stat[:, 4:5], ALU.mult, ALU.subtract), "dv")
                tk["gp_stat"] = tvar
                b.wait(ac, tvar)
                b.raw(ac, ac.activation(stat[:, 7:8], stat[:, 6:7], AF.Ln), "ac")
                trs = b.inc(ac.activation(stat[:, 8:9], stat[:, 7:8], AF.Exp, scale=-0.5), "ac")
                b.waits(ve, [trs, tk["xn_done"]])
                tnm = b.inc(ve.tensor_scalar(xn_tm[:], x_tm[jp][:], stat[:, 3:4], stat[:, 8:9],
                                             ALU.subtract, ALU.mult), "dv")
                tk["x_free"][jp] = tnm
                tk["xn_tnm"] = tnm

            def xn_tr(i, tcn):
                tnm = tk["xn_tnm"]
                for half in range(2):
                    PT.pre_write(b, te)
                    b.waits(te, [tnm, t_init])
                    for fc8 in range(8):
                        fc = half * 8 + fc8
                        inst = te.transpose(PT.t[:, fc8 * 128:(fc8 + 1) * 128],
                                            xn_tm[:, fc * 128:(fc + 1) * 128], cs["id_bf"][:])
                    tpe = b.inc(inst, "pe")
                    PT.writer = tpe
                    b.wait(ac, tpe)
                    for fc8 in range(8):
                        fc = half * 8 + fc8
                        inst = ac.activation(
                            xn_fm[:, fc * 512 + tcn * 128: fc * 512 + (tcn + 1) * 128],
                            PT.t[:, fc8 * 128:(fc8 + 1) * 128], AF.Copy)
                    tac = b.inc(inst, "ac")
                    PT.readers.append(tac)
                    tk["xn_done"] = tpe
                if tcn == 3:
                    tk["xnfm"] = b.tok("ac")

            def tile_rest(i):
                par = i % 2
                prev_trl = None
                for chn in range(2):
                    FX.pre_write(b, te)
                    b.waits(te, [tk["ab_free"], tk["ab_free2"], tk["xnfm"], t_wl])
                    for fc in range(16):
                        inst = te.matmul(FX.t[0:4, :],
                                         wab_s[:, fc * 8 + chn * 4: fc * 8 + chn * 4 + 4],
                                         xn_fm[:, fc * 512:(fc + 1) * 512],
                                         start=(fc == 0), stop=(fc == 15))
                    tpe = b.inc(inst, "pe")
                    FX.writer = tpe
                    b.waits(ve, [tpe, prev_trl])
                    dst = la4 if chn == 0 else lb4
                    tzz = b.inc(ve.tensor_scalar(zz[:], FX.t[0:4, :],
                                                 (ba_s if chn == 0 else bb_s)[:],
                                                 None, ALU.add), "dv")
                    FX.readers.append(tzz)
                    b.wait(ac, tzz)
                    b.raw(ac, ac.activation(abt1[:], zz[:], AF.Abs), "ac")
                    b.raw(ac, ac.activation(dst[:], abt1[:], AF.Exp, scale=-1.0), "ac")
                    b.raw(ac, ac.activation(abt1[:], dst[:], AF.Ln, bias=1.0), "ac")
                    trl = b.inc(ac.activation(dst[:], zz[:], AF.Relu, scale=-1.0), "ac")
                    prev_trl = trl
                    b.wait(ve, trl)
                    b.inc(ve.scalar_tensor_tensor(dst[:], abt1[:], -1.0, dst[:],
                                                  ALU.mult, ALU.subtract), "dv")
                b.wait(ac, b.tok("dv"))
                tk["ab"] = b.inc(ac.activation(b4[:], lb4[:], AF.Exp), "ac")

                pp = [FB1, FB2]
                prev_tcv = None
                for m in range(3):
                    for oc in range(4):
                        blk = m * 4 + oc
                        bank = pp[blk % 2]
                        bank.pre_write(b, te)
                        b.waits(te, [tk["xnfm"], t_wl, tk["qkv_free"][par]])
                        for fc in range(16):
                            inst = te.matmul(bank.t[:],
                                             w_sb[:, m * 8192 + fc * 512 + oc * 128:
                                                  m * 8192 + fc * 512 + (oc + 1) * 128],
                                             xn_fm[:, fc * 512:(fc + 1) * 512],
                                             start=(fc == 0), stop=(fc == 15))
                        tpe = b.inc(inst, "pe")
                        bank.writer = tpe
                        cwc = lambda j2: cw_s[:, blk * 4 + j2: blk * 4 + j2 + 1]
                        hl = lambda a, b2: halo_sb[:, blk * 3 + a: blk * 3 + b2]
                        b.waits(ve, [tpe, prev_tcv, t_cz])
                        b.raw(ve, ve.tensor_scalar(ctmp[0][:, 3:512], bank.t[:, 0:509],
                                                   cwc(0), None, ALU.mult), "dv")
                        b.raw(ve, ve.tensor_scalar(ctmp[0][:, 0:3], hl(0, 3),
                                                   cwc(0), None, ALU.mult), "dv")
                        b.raw(ve, ve.scalar_tensor_tensor(ctmp[1][:, 2:512], bank.t[:, 0:510],
                                                          cwc(1), ctmp[0][:, 2:512], ALU.mult, ALU.add), "dv")
                        b.raw(ve, ve.scalar_tensor_tensor(ctmp[1][:, 0:2], hl(1, 3),
                                                          cwc(1), ctmp[0][:, 0:2], ALU.mult, ALU.add), "dv")
                        b.raw(ve, ve.scalar_tensor_tensor(ctmp[0][:, 1:512], bank.t[:, 0:511],
                                                          cwc(2), ctmp[1][:, 1:512], ALU.mult, ALU.add), "dv")
                        b.raw(ve, ve.scalar_tensor_tensor(ctmp[0][:, 0:1], hl(2, 3),
                                                          cwc(2), ctmp[1][:, 0:1], ALU.mult, ALU.add), "dv")
                        b.raw(ve, ve.scalar_tensor_tensor(ctmp[1][:], bank.t[:],
                                                          cwc(3), ctmp[0][:], ALU.mult, ALU.add), "dv")
                        thl = b.inc(ve.tensor_copy(hl(0, 3), bank.t[:, 509:512]), "dv")
                        bank.readers.append(thl)
                        b.wait(ac, thl)
                        if m == 1:
                            tsl = b.inc(ac.activation(ctmp[0][:], ctmp[1][:], AF.Silu,
                                                      bias=cb_s[:, blk: blk + 1]), "ac")
                            b.wait(ve, tsl)
                            prev_tcv = b.inc(
                                ve.tensor_scalar(qkv[par][:, blk * 512:(blk + 1) * 512],
                                                 ctmp[0][:], float(C) ** -0.5, None, ALU.mult), "dv")
                        else:
                            prev_tcv = b.inc(ac.activation(qkv[par][:, blk * 512:(blk + 1) * 512],
                                                           ctmp[1][:], AF.Silu,
                                                           bias=cb_s[:, blk: blk + 1]), "ac")
                tk["qkv"][par] = ("MULTI", [b.tok("ac"), b.tok("dv")])

            def qkv_wait(eng, par):
                tt = tk["qkv"][par]
                if tt and tt[0] == "MULTI":
                    b.waits(eng, tt[1])

            def front_a(j):
                p3 = j % 3
                i, ccn = j // 4, j % 4
                sm = smalls[p3]
                t_lo = ccn * 128

                FX.pre_write(b, te)
                b.waits(te, [tk["sm_free"][p3], tk["ab"]])
                te.transpose(FX.t[:, 0:4], la4[0:4, t_lo:t_lo + 128], cs["id_f"][0:4, 0:4])
                te.transpose(FX.t[:, 4:8], lb4[0:4, t_lo:t_lo + 128], cs["id_f"][0:4, 0:4])
                tpe = b.inc(te.transpose(FX.t[:, 8:12], b4[0:4, t_lo:t_lo + 128],
                                         cs["id_f"][0:4, 0:4]), "pe")
                FX.writer = tpe
                tk["ab_free"] = tpe
                b.wait(ac, tpe)
                te1 = b.inc(ac.activation(sm[:, SM["laT"]:SM["laT"] + 12], FX.t[:, 0:12], AF.Copy), "ac")
                FX.readers.append(te1)

                FX.pre_write(b, te)
                b.wait(te, te1)
                te.matmul(FX.t[:, 16:20], cs["ut_s"][:], sm[:, SM["laT"]:SM["laT"] + 4],
                          start=True, stop=True)
                te.matmul(FX.t[:, 20:24], cs["lts_s"][:], sm[:, SM["laT"]:SM["laT"] + 4],
                          start=True, stop=True)
                tpe = b.inc(te.matmul(FX.t[0:4, 128:256], sm[:, SM["laT"]:SM["laT"] + 4],
                                      cs["ut_s"][:], start=True, stop=True), "pe")
                FX.writer = tpe
                b.wait(ac, tpe)
                ac.activation(sm[:, SM["gT"]:SM["gT"] + 4], FX.t[:, 16:20], AF.Copy)
                ac.activation(sm[:, SM["ngT"]:SM["ngT"] + 4], FX.t[:, 16:20], AF.Copy, scale=-1.0)
                ac.activation(sm[:, SM["gamT"]:SM["gamT"] + 4], FX.t[:, 16:20], AF.Exp)
                ac.activation(sm[:, SM["cs"]:SM["cs"] + 4], FX.t[:, 20:24], AF.Exp)
                te2 = b.inc(ac.activation(rows_cat[:, 0:128], FX.t[0:4, 128:256], AF.Copy), "ac")
                FX.readers.append(te2)
                b.wait(ve, te2)
                b.inc(ve.tensor_tensor(sm[:, SM["gCb"]:SM["gCb"] + 4],
                                       sm[:, SM["cs"]:SM["cs"] + 4],
                                       sm[:, SM["gamT"]:SM["gamT"] + 4], ALU.mult), "dv")
                ve.scalar_tensor_tensor(rows_cat[:, 256:384], la4[0:4, t_lo:t_lo + 128], -1.0,
                                        rows_cat[:, 0:128], ALU.mult, ALU.add)
                b.raw(ve, ve.scalar_tensor_tensor(sm[:, SM["tmp"]:SM["tmp"] + 4],
                                                  sm[:, SM["laT"]:SM["laT"] + 4], -1.0,
                                                  sm[:, SM["gT"]:SM["gT"] + 4], ALU.mult, ALU.add), "dv")
                ve.tensor_tensor(rows_cat[:, 128:256], rows_cat[:, 256:384],
                                 lb4[0:4, t_lo:t_lo + 128], ALU.add)
                tr2 = b.inc(ve.tensor_tensor(sm[:, SM["r2T"]:SM["r2T"] + 4],
                                             sm[:, SM["tmp"]:SM["tmp"] + 4],
                                             sm[:, SM["lbT"]:SM["lbT"] + 4], ALU.add), "dv")
                tk["ab_free2"] = tr2
                b.wait(ac, tr2)
                tgp = b.inc(ac.activation(sm[:, SM["gpT"]:SM["gpT"] + 4],
                                          sm[:, SM["tmp"]:SM["tmp"] + 4], AF.Exp), "ac")
                b.wait(ve, tgp)
                tbgp = b.inc(ve.scalar_tensor_tensor(sm[:, SM["bgp"]:SM["bgp"] + 4],
                                                     sm[:, SM["gpT"]:SM["gpT"] + 4], -1.0,
                                                     sm[:, SM["bT"]:SM["bT"] + 4],
                                                     ALU.mult, ALU.mult), "dv")
                tk["bgp"] = tbgp

                FX.pre_write(b, te)
                b.waits(te, [tk["g_free"][p3], tk["xnfm"]])
                for fc in range(16):
                    inst = te.matmul(FX.t[:],
                                     xn_fm[:, fc * 512 + t_lo: fc * 512 + t_lo + 128],
                                     w_sb[:, 3 * 8192 + fc * 512: 3 * 8192 + (fc + 1) * 512],
                                     start=(fc == 0), stop=(fc == 15))
                tg = b.inc(inst, "pe")
                FX.writer = tg
                b.waits(ac, [tg, tk["gts"]])
                tth = b.inc(ac.activation(gtanh[:], FX.t[:], AF.Tanh, scale=0.5), "ac")
                FX.readers.append(tth)
                b.wait(ve, tth)
                tk["gts"] = b.inc(ve.tensor_scalar(g_ch[p3][:], gtanh[:], 1.0, 0.5,
                                                   ALU.add, ALU.mult), "dv")
                tk["g"][p3] = tk["gts"]

                FB1.pre_write(b, te)
                FB2.pre_write(b, te)
                b.waits(te, [tr2])
                for h in range(4):
                    te.matmul(FB1.t[:, h * 128:(h + 1) * 128],
                              cs["ehp_s"][0:4, h * 128:(h + 1) * 128], rows_cat[:, 0:128],
                              start=True, stop=True)
                for h in range(4):
                    inst = te.matmul(FB2.t[:, h * 128:(h + 1) * 128],
                                     cs["ehp_s"][0:4, h * 128:(h + 1) * 128], rows_cat[:, 128:256],
                                     start=True, stop=True)
                tbcp = b.inc(inst, "pe")
                FB1.writer = tbcp
                FB2.writer = tbcp
                bc_g = lambda h: FB1.t[:, h * 128:(h + 1) * 128]
                bc_r2 = lambda h: FB2.t[:, h * 128:(h + 1) * 128]

                b.waits(ve, [tbcp, b.tok("ac")])
                for h in range(4):
                    ve.scalar_tensor_tensor(e_scr[:, h * 128:(h + 1) * 128],
                                            bc_g(h),
                                            sm[:, SM["gT"] + h: SM["gT"] + h + 1],
                                            cs["bndi_s"][:], ALU.subtract, ALU.min)
                for h in range(4):
                    ve.scalar_tensor_tensor(e_scr[:, 512 + h * 128: 512 + (h + 1) * 128],
                                            bc_r2(h),
                                            sm[:, SM["gT"] + h: SM["gT"] + h + 1],
                                            cs["bnds_s"][:], ALU.subtract, ALU.min)
                for h in range(4):
                    inst = ve.scalar_tensor_tensor(e_scr[:, 1024 + h * 128: 1024 + (h + 1) * 128],
                                                   bc_g(h),
                                                   sm[:, SM["r2T"] + h: SM["r2T"] + h + 1],
                                                   cs["bndt_s"][:], ALU.subtract, ALU.max)
                targ3 = b.inc(inst, "dv")
                FB1.readers.append(targ3)
                FB2.readers.append(targ3)
                tk["targ3"] = targ3
                b.wait(ac, targ3)
                b.raw(ac, ac.activation(e_scr[:, 1536:2048], e_scr[:, 0:512], AF.Exp), "ac")
                b.raw(ac, ac.activation(e_scr[:, 0:512], e_scr[:, 512:1024], AF.Exp), "ac")
                tk["texp"] = b.inc(ac.activation(e_scr[:, 512:1024], e_scr[:, 1024:1536],
                                                 AF.Exp, scale=-1.0), "ac")

            def front_b(j):
                p2 = j % 2
                p3 = j % 3
                i, ccn = j // 4, j % 4
                sm = smalls[p3]
                t_lo = ccn * 128
                kfm = lambda h: qkv[i % 2][:, (4 + h) * 512 + t_lo: (4 + h) * 512 + t_lo + 128]
                vfm = lambda h: qkv[i % 2][:, (8 + h) * 512 + t_lo: (8 + h) * 512 + t_lo + 128]
                qfm = lambda h: qkv[i % 2][:, h * 512 + t_lo: h * 512 + t_lo + 128]

                FB1.pre_write(b, te)
                FB2.pre_write(b, te)
                PT.pre_write(b, te)
                qkv_wait(te, i % 2)
                for h in range(4):
                    te.matmul(FB1.t[:, h * 128:(h + 1) * 128], kfm(h), kfm(h), start=True, stop=True)
                for h in range(4):
                    te.matmul(FB2.t[:, h * 128:(h + 1) * 128], kfm(h), qfm(h), start=True, stop=True)
                for h in range(4):
                    te.transpose(PT.t[:, h * 128:(h + 1) * 128], kfm(h), cs["id_bf"][:])
                for h in range(4):
                    inst = te.transpose(PT.t[:, 512 + h * 128: 512 + (h + 1) * 128], vfm(h), cs["id_bf"][:])
                tp4 = b.inc(inst, "pe")
                FB1.writer = tp4
                FB2.writer = tp4
                PT.writer = tp4
                tk["qkv_free"][i % 2] = tp4
                b.wait(ac, tp4)
                ac.activation(ktm[p3][:], PT.t[:, 0:512], AF.Copy)
                tkv = b.inc(ac.activation(vtm[p2][:], PT.t[:, 512:1024], AF.Copy), "ac")
                PT.readers.append(tkv)
                tk["kvtm"][p3] = tkv

                b.waits(ve, [tk["texp"], tp4])
                for h in range(4):
                    ve.scalar_tensor_tensor(dsm[p3][:, h * 128:(h + 1) * 128],
                                            e_scr[:, 1536 + h * 128: 1536 + (h + 1) * 128], 1.0,
                                            FB2.t[:, h * 128:(h + 1) * 128], ALU.mult, ALU.mult)
                for h in range(4):
                    ve.scalar_tensor_tensor(nsm1[p2][:, h * 128:(h + 1) * 128],
                                            e_scr[:, h * 128:(h + 1) * 128], -1.0,
                                            FB1.t[:, h * 128:(h + 1) * 128], ALU.mult, ALU.mult)
                for h in range(4):
                    inst = ve.scalar_tensor_tensor(ntm1[p2][:, h * 128:(h + 1) * 128],
                                                   e_scr[:, 512 + h * 128: 512 + (h + 1) * 128], -1.0,
                                                   FB1.t[:, h * 128:(h + 1) * 128], ALU.mult, ALU.mult)
                tb2 = b.inc(inst, "dv")
                FB1.readers.append(tb2)
                FB2.readers.append(tb2)
                tk["nd"][p2] = tb2

                b.waits(ve, [tkv, tk["bgp"], tk["rhs0_free"][p2]])
                for h in range(4):
                    inst = ve.tensor_scalar(rhs0[p2][:, h * 128:(h + 1) * 128],
                                            vtm[p2][:, h * 128:(h + 1) * 128],
                                            sm[:, SM["bT"] + h: SM["bT"] + h + 1], None, ALU.mult)
                tk["rhs0"][p2] = b.inc(inst, "dv")
                if j == 0:
                    tap("fsm0", smalls[0][:, 0:64], cols=64)
                    tap("fdsm0", dsm[0][:, 0:512])
                    tap("fnsm0", nsm1[0][:, 0:512])
                    tap("fntm0", ntm1[0][:, 0:512])

            def r_step(j):
                p2 = j % 2
                p3 = j % 3
                sm = smalls[p3]
                i, ccn = j // 4, j % 4
                t_lo = ccn * 128
                kfm = lambda h: qkv[i % 2][:, (4 + h) * 512 + t_lo: (4 + h) * 512 + t_lo + 128]
                PS.pre_write(b, te)
                b.waits(te, [tk["z_ok"], tk["qkv_free"][i % 2] if False else None])
                for h in range(4):
                    inst = te.matmul(PS.t[:, h * 128:(h + 1) * 128], kfm(h),
                                     zbf[:, h * 128:(h + 1) * 128], start=True, stop=True)
                tkz = b.inc(inst, "pe")
                PS.writer = tkz
                b.waits(ve, [tkz, tk["rhs0"][p2]])
                for h in range(4):
                    inst = ve.scalar_tensor_tensor(rbuf[:, h * 128:(h + 1) * 128],
                                                   PS.t[:, h * 128:(h + 1) * 128],
                                                   sm[:, SM["bgp"] + h: SM["bgp"] + h + 1],
                                                   rhs0[p2][:, h * 128:(h + 1) * 128],
                                                   ALU.mult, ALU.add)
                trr = b.inc(inst, "dv")
                PS.readers.append(trr)
                tk["rtok"][p2] = trr
                tk["rhs0_free"][p2] = trr

            def ladder(j):
                p2 = j % 2
                ucur = rbuf
                tok_u = tk["rtok"][p2]
                ncs, nct = nsm1[p2], ntm1[p2]
                tok_n = tk["nd"][p2]
                for lvl in range(7):
                    if lvl < 6:
                        PA.pre_write(b, te)
                        PB.pre_write(b, te)
                        b.waits(te, [tok_n])
                        for h in range(4):
                            te.matmul(PA.t[:, h * 128:(h + 1) * 128],
                                      ncs[:, h * 128:(h + 1) * 128],
                                      nct[:, h * 128:(h + 1) * 128], start=True, stop=True)
                        for h in range(4):
                            inst = te.matmul(PB.t[:, h * 128:(h + 1) * 128],
                                             nct[:, h * 128:(h + 1) * 128],
                                             ncs[:, h * 128:(h + 1) * 128], start=True, stop=True)
                        tsq_pe = b.inc(inst, "pe")
                        PA.writer = tsq_pe
                        PB.writer = tsq_pe
                    PC.pre_write(b, te)
                    b.waits(te, [tok_u, tok_n])
                    te.matmul(PC.t[:], cs["id_bf"][:], ucur[:], start=True, stop=False)
                    for h in range(4):
                        inst = te.matmul(PC.t[:, h * 128:(h + 1) * 128],
                                         ncs[:, h * 128:(h + 1) * 128],
                                         ucur[:, h * 128:(h + 1) * 128], start=False,
                                         stop=True)
                    tU_pe = b.inc(inst, "pe")
                    PC.writer = tU_pe
                    if lvl < 6:
                        b.wait(ac, tsq_pe)
                        ac.activation(ntm_s[lvl % 2][:], PA.t[:], AF.Copy)
                        tok_n = b.inc(ac.activation(nsm_s[lvl % 2][:], PB.t[:], AF.Copy), "ac")
                        PA.readers.append(tok_n)
                        PB.readers.append(tok_n)
                        ncs, nct = nsm_s[lvl % 2], ntm_s[lvl % 2]
                    udst = bufUf[p2] if lvl == 6 else bufU[lvl % 2]
                    if lvl % 2 == 1:
                        b.wait(ve, tU_pe)
                        tok_u = b.inc(ve.tensor_copy(udst[:], PC.t[:]), "dv")
                    else:
                        b.wait(ac, tU_pe)
                        if lvl == 6:
                            b.waits(ac, tk["ufin_free"][p2])
                        tok_u = b.inc(ac.activation(udst[:], PC.t[:], AF.Copy), "ac")
                    PC.readers.append(tok_u)
                    ucur = udst
                tk["lad"][p2] = tok_u
                if j == 0:
                    tap("fu0", bufUf[0][:, 0:512])

            def back_z(j):
                p2 = j % 2
                p3 = j % 3
                sm = smalls[p3]
                zcur, znxt = z32[j % 2], z32[(j + 1) % 2]
                i, ccn = j // 4, j % 4
                t_lo = ccn * 128
                qfm = lambda h: qkv[i % 2][:, h * 512 + t_lo: h * 512 + t_lo + 128]
                PS.pre_write(b, te)
                b.waits(te, [tk["z_ok"]])
                for h in range(4):
                    inst = te.matmul(PS.t[:, h * 128:(h + 1) * 128], qfm(h),
                                     zbf[:, h * 128:(h + 1) * 128], start=True, stop=True)
                tO1 = b.inc(inst, "pe")
                PS.writer = tO1
                b.waits(ve, [tk["lad"][p2]])
                for h in range(4):
                    inst = ve.tensor_scalar(usc[:, h * 128:(h + 1) * 128],
                                            bufUf[p2][:, h * 128:(h + 1) * 128],
                                            sm[:, SM["cs"] + h: SM["cs"] + h + 1], None, ALU.mult)
                tusc = b.inc(inst, "dv")
                tk["ufz"][p2] = tusc
                b.wait(ve, tO1)
                for h in range(4):
                    inst = ve.tensor_scalar(rz_t[:, h * 128:(h + 1) * 128],
                                            PS.t[:, h * 128:(h + 1) * 128],
                                            sm[:, SM["gamT"] + h: SM["gamT"] + h + 1],
                                            None, ALU.mult)
                trz = b.raw(ve, inst, "dv")
                PS.readers.append(trz)
                PS.pre_write(b, te)
                b.waits(te, [tusc, tk["kvtm"][p3]])
                for h in range(4):
                    inst = te.matmul(PS.t[:, h * 128:(h + 1) * 128],
                                     ktm[p3][:, h * 128:(h + 1) * 128],
                                     usc[:, h * 128:(h + 1) * 128], start=True, stop=True)
                tzm = b.inc(inst, "pe")
                PS.writer = tzm
                b.wait(ve, tzm)
                for h in range(4):
                    inst = ve.scalar_tensor_tensor(znxt[:, h * 128:(h + 1) * 128],
                                                   zcur[:, h * 128:(h + 1) * 128],
                                                   sm[:, SM["gCb"] + h: SM["gCb"] + h + 1],
                                                   PS.t[:, h * 128:(h + 1) * 128], ALU.mult, ALU.add)
                tz1 = b.raw(ve, inst, "dv")
                PS.readers.append(tz1)
                tz = b.inc(ve.tensor_copy(zbf[:], znxt[:]), "dv")
                tk["z_ok"] = tz

            def back_o(j):
                p3 = j % 3
                sm = smalls[p3]
                p2 = j % 2
                FB1.pre_write(b, te)
                b.waits(te, [tk["lad"][p2]])
                for h in range(4):
                    inst = te.matmul(FB1.t[:, h * 128:(h + 1) * 128],
                                     dsm[p3][:, h * 128:(h + 1) * 128],
                                     bufUf[p2][:, h * 128:(h + 1) * 128], start=True, stop=True)
                tO = b.inc(inst, "pe")
                FB1.writer = tO
                tk["ufin_free"][p2] = [tk["ufz"][p2], tO]
                b.wait(ve, tO)
                for h in range(4):
                    inst = ve.tensor_tensor(o_sb[:, h * 128:(h + 1) * 128],
                                            rz_t[:, h * 128:(h + 1) * 128],
                                            FB1.t[:, h * 128:(h + 1) * 128], ALU.add)
                tO2 = b.inc(inst, "dv")
                FB1.readers.append(tO2)
                b.wait(ve, tO2)
                for h in range(4):
                    inst = ve.reduce_sum(sm[:, SM["osum"] + h: SM["osum"] + h + 1],
                                         o_sb[:, h * 128:(h + 1) * 128], AXX)
                tsum = b.raw(ve, inst, "dv")
                b.wait(ac, tsum)
                for h in range(4):
                    inst = ac.activation(sqscr[:, 0:128], o_sb[:, h * 128:(h + 1) * 128], AF.Square,
                                         accum_out=sm[:, SM["osq"] + h: SM["osq"] + h + 1])
                    if h < 3:
                        b.raw(ac, inst, "ac")
                tsq = b.inc(inst, "ac")
                b.wait(ve, tsq)
                b.raw(ve, ve.tensor_scalar(sm[:, SM["omu"]:SM["omu"] + 4],
                                           sm[:, SM["osum"]:SM["osum"] + 4],
                                           1.0 / 128.0, None, ALU.mult), "dv")
                b.raw(ve, ve.tensor_tensor(sm[:, SM["tmp"]:SM["tmp"] + 4],
                                           sm[:, SM["omu"]:SM["omu"] + 4],
                                           sm[:, SM["omu"]:SM["omu"] + 4], ALU.mult), "dv")
                b.raw(ve, ve.tensor_scalar(sm[:, SM["osum"]:SM["osum"] + 4],
                                           sm[:, SM["tmp"]:SM["tmp"] + 4],
                                           EPS, None, ALU.subtract), "dv")
                tvar = b.inc(ve.scalar_tensor_tensor(sm[:, SM["orstd"]:SM["orstd"] + 4],
                                                     sm[:, SM["osq"]:SM["osq"] + 4], 1.0 / 128.0,
                                                     sm[:, SM["osum"]:SM["osum"] + 4],
                                                     ALU.mult, ALU.subtract), "dv")
                b.wait(ac, tvar)
                b.raw(ac, ac.activation(sm[:, SM["tmp"]:SM["tmp"] + 4],
                                        sm[:, SM["orstd"]:SM["orstd"] + 4], AF.Ln), "ac")
                trs = b.inc(ac.activation(sm[:, SM["orstd"]:SM["orstd"] + 4],
                                          sm[:, SM["tmp"]:SM["tmp"] + 4], AF.Exp, scale=-0.5), "ac")
                b.waits(ve, [trs, tk.get("og_done")])
                for h in range(4):
                    inst = ve.tensor_scalar(on_b[:, h * 128:(h + 1) * 128],
                                            o_sb[:, h * 128:(h + 1) * 128],
                                            sm[:, SM["omu"] + h: SM["omu"] + h + 1],
                                            sm[:, SM["orstd"] + h: SM["orstd"] + h + 1],
                                            ALU.subtract, ALU.mult)
                tOn = b.inc(inst, "dv")
                par2 = j % 2
                b.waits(gp, [tOn, tk["g"][p3], tk["og_free"][par2], t_init])
                gslc = g_ch[p3][:]
                b.raw(gp, gp.tensor_tensor(gcA[:], on_b[:], lnw_s[:], ALU.mult), "gp")
                b.raw(gp, gp.tensor_tensor(gcB[:], gcA[:], gslc, ALU.mult), "gp")
                b.raw(gp, gp.tensor_tensor(gcA[:], gslc, lnb_s[:], ALU.mult), "gp")
                tog = b.inc(gp.tensor_tensor(og_sb[par2][:], gcB[:], gcA[:], ALU.add), "gp")
                tk["g_free"][p3] = tog
                tk["og_done"] = tog
                b.wait(sy, tog)
                tk["og_free"][par2] = b.dma(sy, f"ogst{par2}", ag_in[j * 128:(j + 1) * 128, :],
                                            og_sb[par2][:])
                tk["sm_free"][p3] = tOn
                if j == 0:
                    tap("o0", o_sb[:, 0:512])

            nch = _KTILES * CPT
            for tcn in range(4):
                xn_ln(0, tcn, None)
                xn_tr(0, tcn)
            for m in range(nch + 2):
                if m >= 2:
                    back_z(m - 2)
                if m < nch:
                    if m % 4 == 0:
                        tile_rest(m // 4)
                    front_a(m)
                if 1 <= m <= nch:
                    r_step(m - 1)
                    ladder(m - 1)
                if m < nch:
                    front_b(m)
                if m < nch and (m // 4 + 1) < _KTILES:
                    xn_ln(m // 4 + 1, m % 4, tk["nd"][m % 2])
                if m >= 2:
                    back_o(m - 2)
                if m - 2 == nch // 2 - 1 and nch == 64:
                    b.waits(gp, [b.tok("ogst0"), b.tok("ogst1")])
                    gp.collective_compute(
                        "AllGather", ALU.bypass, replica_groups=[list(range(8))],
                        ins=[ag_in[0:4096, :].opt()], outs=[ago[0:8 * 4096, :].opt()],
                    ).then_inc(b.sem("cc1"))
                    b.cnt["cc1"] += 1
                if m < nch and (m // 4 + 1) < _KTILES:
                    xn_tr(m // 4 + 1, m % 4)
            t_scan_pe = b.tok("pe")
            if nch == 64:
                b.waits(gp, [b.tok("ogst0"), b.tok("ogst1")])
                gp.collective_compute(
                    "AllGather", ALU.bypass, replica_groups=[list(range(8))],
                    ins=[ag_in[4096:8192, :].opt()], outs=[ago[8 * 4096:16 * 4096, :].opt()],
                ).then_inc(b.sem("cc2"))
                b.cnt["cc2"] += 1

        if _KPH == "p":
            with ExitStack() as arenaS, nc.sbuf_tensor("bounce", [128, 2048], F32) as bnc:
                prev = None
                for tc in range(16):
                    b.waits(sy, [prev])
                    tld = b.dma(sy, "bnc", bnc[:], dram["xout"][tc * 128:(tc + 1) * 128, :])
                    b.wait(sy, tld)
                    prev = b.dma(sy, "bnc2", out_d[tc * 128:(tc + 1) * 128, :], bnc[:])
                b.wait(sy, b.tok("bnc2"))
                for snm in ("ogst0", "ogst1"):
                    b.wait(sy, b.tok(snm))
            return nc, rank512
        with ExitStack() as arenaF:
            EF = arenaF.enter_context
            sf = lambda name, shape, dt: EF(nc.sbuf_tensor(name, shape, dt))
            fx_tm = [sf(f"fx_tm{p}", [128, 2048], F32) for p in range(2)]
            fsq = sf("fsq", [128, 1024], F32)
            fxn_tm = sf("fxn_tm", [128, 2048], BF16)
            fxn_fm = sf("fxn_fm", [128, 16 * cap], BF16)
            fh = sf("fh", [128, 11 * cap], BF16)
            fgu = [[sf(f"fgu{m}{p}", [128, 2048], BF16) for p in range(2)] for m in range(2)]
            fdwb = [sf(f"fdwb{p}", [128, 11 * 512], BF16) for p in range(2)]
            fdo = [sf(f"fdo{p}", [128, 512], BF16) for p in range(2)]
            fhtmp = sf("fhtmp", [128, 512], BF16)

            b.wait(gp, b.tok("gp"))
            tzp = b.inc(gp.memset(fdo[0][:], 0.0), "gp")
            b.wait(sy, tzp)
            tzd = b.dma(sy, "zpd",
                        ag_in[OGROWS + 4 * NE * cap: OGROWS + 4 * NE * cap + NPAD * 4, :],
                        fdo[0][0:NPAD * 4, :])

            ft = {"xn_done": None, "fx_free": [None, None],
                  "fgu_free": [[None, None], [None, None]], "fdw_free": [None, None],
                  "fdo_free": [tzd, None], "h_free": None, "fxn_free": None}

            for e in range(NE):
                b.waits(te, [ft["fxn_free"]])
                for rc in range(ncap):
                    pr = rc % 2
                    b.waits(sy, [ft["fx_free"][pr]])
                    tld = b.dma(sy, f"fxl{pr}", fx_tm[pr][:],
                                dram["fx"][e * cap + rc * 128: e * cap + (rc + 1) * 128, :])
                    b.wait(ve, tld)
                    b.raw(ve, ve.reduce_sum(stat[:, 0:1], fx_tm[pr][:], AXX), "dv")
                    b.waits(ac, [tld, ft["fx_free"][1 - pr]])
                    sqh = fsq[:, 0:512]
                    b.raw(ac, ac.activation(sqh, fx_tm[pr][:, 0:512], AF.Square, accum_out=stat[:, 1:2]), "ac")
                    b.raw(ac, ac.activation(sqh, fx_tm[pr][:, 512:1024], AF.Square, accum_out=stat[:, 2:3]), "ac")
                    b.raw(ac, ac.activation(sqh, fx_tm[pr][:, 1024:1536], AF.Square, accum_out=stat[:, 9:10]), "ac")
                    tsq = b.inc(ac.activation(sqh, fx_tm[pr][:, 1536:2048], AF.Square,
                                              accum_out=stat[:, 10:11]), "ac")
                    b.wait(ve, tsq)
                    ve.tensor_scalar(stat[:, 3:4], stat[:, 0:1], 1.0 / D, None, ALU.mult)
                    ve.tensor_tensor(stat[:, 11:12], stat[:, 1:2], stat[:, 2:3], ALU.add)
                    b.raw(ve, ve.tensor_tensor(stat[:, 12:13], stat[:, 9:10], stat[:, 10:11], ALU.add), "dv")
                    ve.tensor_tensor(stat[:, 13:14], stat[:, 3:4], stat[:, 3:4], ALU.mult)
                    b.raw(ve, ve.tensor_tensor(stat[:, 5:6], stat[:, 11:12], stat[:, 12:13], ALU.add), "dv")
                    b.raw(ve, ve.tensor_scalar(stat[:, 4:5], stat[:, 13:14], EPS, None, ALU.subtract), "dv")
                    tvar = b.raw(ve, ve.scalar_tensor_tensor(stat[:, 6:7], stat[:, 5:6], 1.0 / D,
                                                             stat[:, 4:5], ALU.mult, ALU.subtract), "dv")
                    b.wait(ac, tvar)
                    b.raw(ac, ac.activation(stat[:, 7:8], stat[:, 6:7], AF.Ln), "ac")
                    trs = b.inc(ac.activation(stat[:, 8:9], stat[:, 7:8], AF.Exp, scale=-0.5), "ac")
                    b.waits(ve, [trs, ft["xn_done"]])
                    tnm = b.inc(ve.tensor_scalar(fxn_tm[:], fx_tm[pr][:], stat[:, 3:4], stat[:, 8:9],
                                                 ALU.subtract, ALU.mult), "dv")
                    ft["fx_free"][pr] = tnm
                    for half in range(2):
                        PT.pre_write(b, te)
                        b.wait(te, tnm)
                        for fc8 in range(8):
                            fc = half * 8 + fc8
                            inst = te.transpose(PT.t[:, fc8 * 128:(fc8 + 1) * 128],
                                                fxn_tm[:, fc * 128:(fc + 1) * 128], cs["id_bf"][:])
                        tpe = b.inc(inst, "pe")
                        PT.writer = tpe
                        b.wait(ac, tpe)
                        for fc8 in range(8):
                            fc = half * 8 + fc8
                            inst = ac.activation(fxn_fm[:, fc * cap + rc * 128: fc * cap + (rc + 1) * 128],
                                                 PT.t[:, fc8 * 128:(fc8 + 1) * 128], AF.Copy)
                        tac = b.inc(inst, "ac")
                        PT.readers.append(tac)
                        ft["xn_done"] = tpe
                t_fxn = b.tok("ac")

                for ic in range(11):
                    pr = ic % 2
                    tws = []
                    for m, wname in ((0, "fgw"), (1, "fuw")):
                        b.waits(gp, [ft["fgu_free"][m][pr]])
                        tws.append(b.dma(gp, f"fgu{m}{pr}", fgu[m][pr][:],
                                         dram[wname][e:e + 1, ic:ic + 1]
                                         .rearrange("a b p f -> (a b p) f")))
                    banks = (PA, PB) if pr == 0 else (PD, PE)
                    r0 = 0
                    for rb in rblocks:
                        for m in range(2):
                            bank = banks[m]
                            bank.pre_write(b, te)
                            b.waits(te, [tws[m], t_fxn])
                            for fc in range(16):
                                inst = te.matmul(bank.t[:, 0:rb],
                                                 fgu[m][pr][:, fc * 128:(fc + 1) * 128],
                                                 fxn_fm[:, fc * cap + r0: fc * cap + r0 + rb],
                                                 start=(fc == 0), stop=(fc == 15))
                            tpe = b.inc(inst, "pe")
                            bank.writer = tpe
                            if rb == rblocks[-1] and r0 + rb >= cap:
                                ft["fgu_free"][m][pr] = tpe
                        b.waits(ac, [b.tok("pe"), ft.get("prev_th")])
                        tsl = b.inc(ac.activation(fhtmp[:, 0:rb], banks[0].t[:, 0:rb], AF.Silu), "ac")
                        banks[0].readers.append(tsl)
                        banks[1].pre_read(b, ve)
                        b.waits(ve, [tsl, ft["h_free"]])
                        th = b.inc(ve.tensor_tensor(fh[:, ic * cap + r0: ic * cap + r0 + rb],
                                                    fhtmp[:, 0:rb], banks[1].t[:, 0:rb], ALU.mult), "dv")
                        banks[1].readers.append(th)
                        ft["prev_th"] = th
                        r0 += rb
                t_h = b.tok("dv")
                ft["fxn_free"] = b.tok("pe")

                for hg in range(4):
                    pr = hg % 2
                    b.waits(gp, [ft["fdw_free"][pr]])
                    twd = b.dma(gp, f"fdw{pr}", fdwb[pr][:],
                                dram["fdw"][e:e + 1, hg:hg + 1].rearrange("a b p f -> (a b p) f"))
                    for rc in range(ncap):
                        bank = (PC, PF)[rc % 2]
                        bank.pre_write(b, te)
                        b.waits(te, [twd, t_h])
                        for ic in range(11):
                            inst = te.matmul(bank.t[:],
                                             fh[:, ic * cap + rc * 128: ic * cap + (rc + 1) * 128],
                                             fdwb[pr][:, ic * 512:(ic + 1) * 512],
                                             start=(ic == 0), stop=(ic == 10))
                        tpe = b.inc(inst, "pe")
                        bank.writer = tpe
                        if rc == ncap - 1:
                            ft["fdw_free"][pr] = tpe
                        dpr = (hg * ncap + rc) % 2
                        b.waits(ve, [tpe, ft["fdo_free"][dpr]])
                        tds = b.inc(ve.tensor_scalar(fdo[dpr][:], bank.t[:],
                                                     fw_s[:, e * ncap + rc: e * ncap + rc + 1],
                                                     None, ALU.mult), "dv")
                        bank.readers.append(tds)
                        b.wait(sy, tds)
                        d0 = (e * cap + rc * 128) * 4
                        dst = ag_in[OGROWS + d0 + hg: OGROWS + d0 + 512 + hg, :] \
                            .rearrange("(p f) c -> p f c", f=4)[:, 0]
                        ft["fdo_free"][dpr] = b.dma(sy, f"fdst{dpr}", dst, fdo[dpr][:])
                ft["h_free"] = b.tok("pe")

        if _KPH == "pf":
            with nc.sbuf_tensor("bounce", [128, 2048], F32) as bnc:
                prev = None
                for tc in range(16):
                    b.waits(sy, [prev])
                    tld = b.dma(sy, "bnc", bnc[:], dram["xout"][tc * 128:(tc + 1) * 128, :])
                    b.wait(sy, tld)
                    prev = b.dma(sy, "bnc2", out_d[tc * 128:(tc + 1) * 128, :], bnc[:])
                b.wait(sy, b.tok("bnc2"))
                for snm in ("ogst0", "ogst1", "fdst0", "fdst1", "zpd"):
                    b.wait(sy, b.tok(snm))
            return nc, rank512
        for s in ("fdst0", "fdst1", "zpd"):
            b.wait(gp, b.tok(s))
        gp.collective_compute(
            "AllGather", ALU.bypass, replica_groups=[list(range(8))],
            ins=[ag_in[OGROWS:rank512, :].opt()], outs=[agd.ap().opt()],
        ).then_inc(b.sem("cc3"))
        b.cnt["cc3"] += 1
        t_ag1, t_ag2, t_ag3 = b.tok("cc1"), b.tok("cc2"), b.tok("cc3")

        if _KPH == "pfa":
            b.waits(gp, [t_ag1, t_ag2, t_ag3])
            with nc.sbuf_tensor("bounce", [128, 2048], F32) as bnc:
                prev = None
                for tc in range(16):
                    b.waits(sy, [prev])
                    tld = b.dma(sy, "bnc", bnc[:], dram["xout"][tc * 128:(tc + 1) * 128, :])
                    b.wait(sy, tld)
                    prev = b.dma(sy, "bnc2", out_d[tc * 128:(tc + 1) * 128, :], bnc[:])
                b.wait(sy, b.tok("bnc2"))
                b.waits(sy, [t_ag1, t_ag2, t_ag3])
            return nc, rank512
        b.waits(gp, [t_scan_pe])
        two = b.dma(gp, "wol", w_sb[:].rearrange("p (fc o) -> p fc o", o=D),
                    dram["wo"][:, :].rearrange("(fc p) o -> p fc o", p=128))

        with ExitStack() as arenaW:
            EW = arenaW.enter_context
            sw = lambda name, shape, dt: EW(nc.sbuf_tensor(name, shape, dt))
            ogt = [sw(f"ogt{q}", [128, 512], BF16) for q in range(4)]
            xo_fm = [sw(f"xo_fm{p}", [128, 2048], BF16) for p in range(2)]
            fcb = [sw(f"fcb{q}", [128, 2048], BF16) for q in range(4)]
            fcA = sw("fcA", [128, 2048], BF16)
            fcB = sw("fcB", [128, 2048], BF16)
            fsum = sw("fsum", [128, 2048], F32)
            xrl = [sw(f"xrl{p}", [128, 2048], F32) for p in range(2)]
            outsb = [sw(f"outsb{p}", [128, 2048], F32) for p in range(2)]
            wt1 = sw("wt1", [128, 2048], F32)

            wtk = {"ogt_free": [None] * 4, "fcb_free": [None] * 4,
                   "xofm_free": [None, None], "fsum_free": None,
                   "xrl_free": [None, None], "out_free": [None, None]}

            for tc in range(16):
                pr = tc % 2
                b.waits(gp, [t_ag1, t_ag2])
                tog = []
                if tc > 0:
                    b.wait(gp, (f"ogth3", b.cnt["ogth3"]))
                for jr in range(4):
                    b.waits(gp, [wtk["ogt_free"][jr]])
                    if _KNOG:
                        tog.append(b.dma(gp, f"ogth{jr}", ogt[jr][:],
                                         ago[tc * 128:(tc + 1) * 128, :]))
                        continue
                    gp.indirect_dma_start(
                        out=ogt[jr][:], out_offset=None, in_=ago[:, :],
                        in_offset=IndirectOffsetOnAxis(oidx_s[:, tc * 4 + jr: tc * 4 + jr + 1], 0),
                    ).then_inc(b.sem(f"ogth{jr}"), 16)
                    b.cnt[f"ogth{jr}"] += 16
                    tog.append(b.tok(f"ogth{jr}"))
                    if _KGSER and jr >= 1:
                        b.wait(gp, b.tok(f"ogth{jr - 1}"))
                tfc = []
                b.wait(gp, (f"ogth3", b.cnt["ogth3"]))
                b.wait(gp, t_ag3)
                for j2 in range(4):
                    b.waits(gp, [wtk["fcb_free"][j2]])
                    if _KNOG:
                        tfc.append(b.dma(gp, f"fcbg{j2}", fcb[j2][:],
                                         agd[tc * 512:(tc + 1) * 512, :]
                                         .rearrange("(p f) c -> p (f c)", f=4)))
                        continue
                    gp.indirect_dma_start(
                        out=fcb[j2][:], out_offset=None, in_=agd[:, :],
                        in_offset=IndirectOffsetOnAxis(cidx_s[:, tc * 4 + j2: tc * 4 + j2 + 1], 0),
                    ).then_inc(b.sem(f"fcbg{j2}"), 16)
                    b.cnt[f"fcbg{j2}"] += 16
                    tfc.append(b.tok(f"fcbg{j2}"))
                    if _KGSER:
                        if j2 >= 1:
                            b.wait(gp, b.tok(f"fcbg{j2 - 1}"))
                        elif tc > 0:
                            b.wait(gp, (f"fcbg3", b.cnt["fcbg3"] - 16)) if False else None
                b.waits(sy, [wtk["xrl_free"][pr]])
                txr = b.dma(sy, f"xrl{pr}", xrl[pr][:], dram["xout"][tc * 128:(tc + 1) * 128, :])

                for half in range(2):
                    PT.pre_write(b, te)
                    b.waits(te, tog[half * 2:(half + 1) * 2] + [wtk["xofm_free"][pr]])
                    for q2 in range(2):
                        jr = half * 2 + q2
                        for cc4 in range(4):
                            inst = te.transpose(PT.t[:, q2 * 512 + cc4 * 128: q2 * 512 + (cc4 + 1) * 128],
                                                ogt[jr][:, cc4 * 128:(cc4 + 1) * 128], cs["id_bf"][:])
                    tpe = b.inc(inst, "pe")
                    PT.writer = tpe
                    for q2 in range(2):
                        wtk["ogt_free"][half * 2 + q2] = tpe
                    b.wait(ac, tpe)
                    for q2 in range(2):
                        jr = half * 2 + q2
                        for cc4 in range(4):
                            fc = jr * 4 + cc4
                            inst = ac.activation(xo_fm[pr][:, fc * 128:(fc + 1) * 128],
                                                 PT.t[:, q2 * 512 + cc4 * 128: q2 * 512 + (cc4 + 1) * 128],
                                                 AF.Copy)
                    tac = b.inc(inst, "ac")
                    PT.readers.append(tac)
                txo = b.tok("ac")

                b.waits(ve, tfc + [wtk["fsum_free"]])
                ve.tensor_tensor(fcA[:], fcb[0][:], fcb[1][:], ALU.add)
                b.raw(ve, ve.tensor_tensor(fcB[:], fcb[2][:], fcb[3][:], ALU.add), "dv")
                tfs = b.inc(ve.tensor_tensor(fsum[:], fcA[:], fcB[:], ALU.add), "dv")
                for j2 in range(4):
                    wtk["fcb_free"][j2] = tfs

                b.waits(te, [two, txo])
                banksW = (PA, PB, PD, PE)
                for og4 in range(4):
                    bank = banksW[og4]
                    bank.pre_write(b, te)
                    for fc in range(16):
                        inst = te.matmul(bank.t[:],
                                         xo_fm[pr][:, fc * 128:(fc + 1) * 128],
                                         w_sb[:, fc * 2048 + og4 * 512: fc * 2048 + (og4 + 1) * 512],
                                         start=(fc == 0), stop=(fc == 15))
                    tpe = b.inc(inst, "pe")
                    bank.writer = tpe
                    if og4 == 3:
                        wtk["xofm_free"][pr] = tpe
                b.waits(ve, [tpe, txr, tfs, wtk["out_free"][pr]])
                for og4 in range(4):
                    inst = ve.tensor_tensor(wt1[:, og4 * 512:(og4 + 1) * 512], banksW[og4].t[:],
                                            xrl[pr][:, og4 * 512:(og4 + 1) * 512], ALU.add)
                tmid = b.raw(ve, inst, "dv")
                for og4 in range(4):
                    inst = ve.tensor_tensor(outsb[pr][:, og4 * 512:(og4 + 1) * 512],
                                            wt1[:, og4 * 512:(og4 + 1) * 512],
                                            fsum[:, og4 * 512:(og4 + 1) * 512], ALU.add)
                tout = b.inc(inst, "dv")
                for og4 in range(4):
                    banksW[og4].readers.append(tmid)
                wtk["fsum_free"] = b.tok("dv")
                wtk["xrl_free"][pr] = b.tok("dv")
                b.wait(sy, b.tok("dv"))
                wtk["out_free"][pr] = b.dma(sy, f"outst{pr}", out_d[tc * 128:(tc + 1) * 128, :],
                                            outsb[pr][:])

            b.wait(sy, b.tok("outst0"))
            b.wait(sy, b.tok("outst1"))
            if _TAPS:
                b.wait(sy, b.tok("dbgst"))

    return nc, rank512


def _ffn_prep(topk_idx, topk_weight, n_experts=16, topk=4):
    fe = np.asarray(topk_idx).reshape(-1)
    fw = np.asarray(topk_weight).reshape(-1).astype(np.float32)
    n_pairs = fe.shape[0]
    tok_of_pair = np.arange(n_pairs) // topk
    order = np.argsort(fe, kind="stable")
    row_of_pair = np.empty(n_pairs, np.int64)
    row_of_pair[order] = np.arange(n_pairs)
    gs = np.bincount(fe, minlength=n_experts)
    return tok_of_pair[order], fw[order], gs, row_of_pair


def kernel(**inputs):
    x = np.ascontiguousarray(np.asarray(inputs["L0_attn_x"], np.float32))
    Wq = np.asarray(inputs["L0_attn_Wq"], np.float32)
    Wk = np.asarray(inputs["L0_attn_Wk"], np.float32)
    Wv = np.asarray(inputs["L0_attn_Wv"], np.float32)
    Wa = np.asarray(inputs["L0_attn_Wa"], np.float32)
    ba = np.asarray(inputs["L0_attn_ba"], np.float32)
    Wb = np.asarray(inputs["L0_attn_Wb"], np.float32)
    bb = np.asarray(inputs["L0_attn_bb"], np.float32)
    Wg = np.asarray(inputs["L0_attn_Wg"], np.float32)
    Wo = np.asarray(inputs["L0_attn_Wo"], np.float32)
    qcw = np.asarray(inputs["L0_attn_qcw"], np.float32)
    qcb = np.asarray(inputs["L0_attn_qcb"], np.float32)
    kcw = np.asarray(inputs["L0_attn_kcw"], np.float32)
    kcb = np.asarray(inputs["L0_attn_kcb"], np.float32)
    vcw = np.asarray(inputs["L0_attn_vcw"], np.float32)
    vcb = np.asarray(inputs["L0_attn_vcb"], np.float32)
    ln_w = np.asarray(inputs["L0_attn_ln_w"], np.float32)
    ln_b = np.asarray(inputs["L0_attn_ln_b"], np.float32)
    tidx = np.asarray(inputs["L0_ffn_topk_idx"])
    tw = np.asarray(inputs["L0_ffn_topk_weight"], np.float32)
    gw = np.asarray(inputs["L0_ffn_gate_weight"], np.float32)
    uw = np.asarray(inputs["L0_ffn_up_weight"], np.float32)
    dw = np.asarray(inputs["L0_ffn_down_weight"], np.float32)

    tok_rows, w_rows, gs, row_of_pair = _ffn_prep(tidx, tw)
    cap = 640
    if gs.max() > cap:
        cap = int(-(-int(gs.max()) // 128) * 128)
    if cap not in _CACHE:
        _CACHE[cap] = build_program(cap)
    nc, rank512 = _CACHE[cap]

    starts = np.concatenate([[0], np.cumsum(gs)])
    pad_row = np.empty(len(tok_rows), np.int64)
    for e in range(16):
        seg = slice(starts[e], starts[e + 1])
        pad_row[seg] = (e // NE) * 0
        pad_row[seg] = (e % NE) * cap + np.arange(gs[e])
    rank_of_row = np.repeat(np.arange(16) // NE, gs)

    in_maps = []
    for c in range(8):
        bi, hg = c // 4, c % 4
        hs = hg * 4
        m = {}
        m["xb"] = x[bi]
        m["xout"] = np.ascontiguousarray(x[bi, 2048 * hg: 2048 * (hg + 1)])
        m["wq"] = np.ascontiguousarray(Wq[:, hs * 128:(hs + 4) * 128])
        m["wk"] = np.ascontiguousarray(Wk[:, hs * 128:(hs + 4) * 128])
        m["wv"] = np.ascontiguousarray(Wv[:, hs * 128:(hs + 4) * 128])
        m["wg"] = np.ascontiguousarray(Wg[:, hs * 128:(hs + 4) * 128])
        m["wo"] = Wo
        m["wab"] = np.ascontiguousarray(
            np.concatenate([Wa[:, hg * 4: hg * 4 + 4], Wb[:, hg * 4: hg * 4 + 4]], axis=1))
        m["ba4"] = np.ascontiguousarray(ba[hg * 4: hg * 4 + 4].reshape(4, 1))
        m["bb4"] = np.ascontiguousarray(bb[hg * 4: hg * 4 + 4].reshape(4, 1))
        cwA = np.zeros((128, 48), np.float32)
        cbA = np.zeros((128, 12), np.float32)
        for mi, (cwm, cbm) in enumerate(((qcw, qcb), (kcw, kcb), (vcw, vcb))):
            for oc in range(4):
                ch0 = hs * 128 + oc * 128
                cwA[:, (mi * 4 + oc) * 4:(mi * 4 + oc) * 4 + 4] = cwm[ch0:ch0 + 128, :]
                cbA[:, mi * 4 + oc] = cbm[ch0:ch0 + 128]
        m["cw"] = cwA
        m["cb"] = cbA
        m["lnwB"] = np.ascontiguousarray(np.tile(ln_w[None, :], (128, 4)))
        m["lnbB"] = np.ascontiguousarray(np.tile(ln_b[None, :], (128, 4)))
        fxr = np.zeros((NE * cap, D), np.float32)
        fwc = np.zeros((128, NE * (cap // 128)), np.float32)
        fgwA = np.zeros((NE, 11, 128, D), np.float32)
        fuwA = np.zeros((NE, 11, 128, D), np.float32)
        fdwA = np.zeros((NE, 4, 128, 11 * 512), np.float32)
        for el in range(NE):
            e = NE * c + el
            n = gs[e]
            seg = slice(starts[e], starts[e + 1])
            fxr[el * cap: el * cap + n] = x[0, tok_rows[seg]]
            wpad = np.zeros(cap, np.float32)
            wpad[:n] = w_rows[seg]
            fwc[:, el * (cap // 128):(el + 1) * (cap // 128)] = wpad.reshape(cap // 128, 128).T
            fgwA[el] = gw[e].reshape(11, 128, 16, 128).transpose(0, 3, 2, 1).reshape(11, 128, D)
            fuwA[el] = uw[e].reshape(11, 128, 16, 128).transpose(0, 3, 2, 1).reshape(11, 128, D)
            fdwA[el] = dw[e].reshape(4, 512, 11, 128).transpose(0, 3, 2, 1).reshape(4, 128, 11 * 512)
        m["fx"] = fxr
        m["fwcol"] = fwc
        m["fgw"] = fgwA
        m["fuw"] = fuwA
        m["fdw"] = fdwA
        drows4 = 4 * NE * cap + NPAD * 4
        zrow = 0 * drows4 + 4 * (NE * cap)
        cidx = np.full((128, 64), zrow, np.int64)
        base_flat = 2048 * c
        if base_flat < 2048:
            for tc in range(16):
                for j2 in range(4):
                    toks = base_flat + tc * 128 + np.arange(128)
                    pairs = toks * 4 + j2
                    rows = row_of_pair[pairs]
                    cidx[:, tc * 4 + j2] = (rank_of_row[rows] * drows4
                                            + 4 * (pad_row[rows]))
        m["cidx"] = cidx.astype(np.int32)
        g0 = 4 * bi
        oidx = np.zeros((128, 64), np.int64)
        for tc in range(16):
            for jr in range(4):
                sq = 2048 * hg + tc * 128 + np.arange(128)
                oidx[:, tc * 4 + jr] = ((sq // 4096) * 8 + g0 + jr) * 4096 + (sq % 4096)
        m["oidx"] = oidx.astype(np.int32)
        in_maps.append(m)

    res = run_bass_kernel_spmd(nc, in_maps, core_ids=list(range(8)))
    shards = [np.asarray(res.results[c]["out"]) for c in range(8)]
    if _TAPS:
        kernel.dbg = [np.asarray(res.results[c]["dbg"]) for c in range(8)]
    return np.concatenate(shards, axis=0).reshape(B_, T, D)


# version 16 (final)
# speedup vs baseline: 1.3034x; 1.0053x over previous
import os
import numpy as np
import ml_dtypes
import concourse.bass as bass
import concourse.mybir as mybir
from concourse.bass_utils import run_bass_kernel_spmd
from concourse.bass_core_rust import IndirectOffsetOnAxis

F32 = mybir.dt.float32
BF16 = mybir.dt.bfloat16
I32 = mybir.dt.int32
AF = mybir.ActivationFunctionType
ALU = mybir.AluOpType
AXX = mybir.AxisListType.X

B_, T, D = 2, 8192, 2048
C = 128
NCHUNK = T // C
TILE = 512
CPT = TILE // C
NTILE = T // TILE
NE = 2
NPAD = 8
OGROWS = T
EPS = 1e-5
NEG = -3.0e38
POS = 3.0e38

_TAPS = set(s for s in os.environ.get("KDBG", "").split(",") if s)
_KPH = os.environ.get("KPH", "all")
_KTILES = int(os.environ.get("KTILES", "16"))
_KNOG = os.environ.get("KNOG", "0") == "1"
_KGSER = os.environ.get("KGSER", "1") == "1"
_CACHE = {}


class Bld:
    def __init__(self, nc):
        self.nc = nc
        self.sems = {}
        self.cnt = {}
        self.last_wait = {}
        self._ctx = None

    def sem(self, name):
        if name not in self.sems:
            self.sems[name] = self._ctx.enter_context(self.nc.semaphore("s_" + name))
            self.cnt[name] = 0
        return self.sems[name]

    def inc(self, inst, name, by=1):
        inst.then_inc(self.sem(name), by)
        self.cnt[name] += by
        return (name, self.cnt[name])

    def tok(self, name):
        self.sem(name)
        return (name, self.cnt[name])

    def wait(self, eng, tok):
        if tok is None:
            return
        name, val = tok
        if val <= 0:
            return
        key = (eng.engine, name)
        if self.last_wait.get(key, -1) >= val:
            return
        eng.wait_ge(self.sems[name], val)
        self.last_wait[key] = val

    def waits(self, eng, toks):
        for t in toks:
            self.wait(eng, t)

    def raw(self, eng, inst, sem):
        tok = self.inc(inst, sem)
        self.wait(eng, tok)
        return tok

    def dma(self, eng, name, out, in_, **kw):
        eng.dma_start(out=out, in_=in_, **kw).then_inc(self.sem(name), 16)
        self.cnt[name] += 16
        return (name, self.cnt[name])


class PBank:
    def __init__(self, tens):
        self.t = tens
        self.writer = None
        self.readers = []

    def pre_write(self, b, eng):
        b.waits(eng, self.readers)
        self.readers = []

    def pre_read(self, b, eng):
        b.wait(eng, self.writer)


def build_program(cap):
    drows = NE * cap + NPAD
    rank512 = OGROWS + drows * 4
    ncap = cap // 128
    rblocks = []
    r = cap
    while r > 0:
        rblocks.append(min(512, r))
        r -= min(512, r)

    nc = bass.Bass("TRN2", target_bir_lowering=False)
    b = Bld(nc)

    dram = {}
    for name, shape, dt in (
        ("xb", [T, D], F32), ("xout", [2048, D], F32),
        ("wq", [D, 512], F32), ("wk", [D, 512], F32), ("wv", [D, 512], F32),
        ("wg", [D, 512], F32), ("wo", [D, D], F32), ("wab", [D, 8], F32),
        ("ba4", [4, 1], F32), ("bb4", [4, 1], F32),
        ("cw", [128, 48], F32), ("cb", [128, 12], F32),
        ("lnwB", [128, 512], F32), ("lnbB", [128, 512], F32),
        ("fx", [NE * cap, D], F32),
        ("fgw", [NE, 11, 128, D], F32), ("fuw", [NE, 11, 128, D], F32),
        ("fdw", [NE, 4, 128, 11 * 512], F32),
        ("fwcol", [128, NE * ncap], F32),
        ("cidx", [128, 64], I32), ("oidx", [128, 64], I32),
    ):
        dram[name] = nc.dram_tensor(name, shape, dt, kind="ExternalInput")
    out_d = nc.dram_tensor("out", [2048, D], F32, kind="ExternalOutput")
    dbg_d = nc.dram_tensor("dbg", [2048, D], F32, kind="ExternalOutput") if _TAPS else None
    dbg_row = [0]

    ag_in = nc.dram_tensor("ag_in", [rank512, 512], BF16, kind="Internal")
    drows4 = rank512 - OGROWS
    ago = nc.dram_tensor("ago", [16 * 4096, 512], BF16, kind="Internal", addr_space="Shared")
    agd = nc.dram_tensor("agd", [8 * drows4, 512], BF16, kind="Internal", addr_space="Shared")

    si = np.arange(128)[:, None]
    ti = np.arange(128)[None, :]
    consts = {
        "id_bf": np.eye(128).astype(ml_dtypes.bfloat16),
        "id_f": np.eye(128, dtype=np.float32),
        "ut_s": np.triu(np.ones((128, 128), np.float32)),
        "e127_s": np.where(si == 127, 1.0, 0.0).astype(np.float32) * np.ones((1, 128), np.float32),
        "bndi_s": np.where(si <= ti, 0.0, NEG).astype(np.float32),
        "bnds_s": np.where(si < ti, 0.0, NEG).astype(np.float32),
        "bndt_s": np.where(si > ti, 0.0, POS).astype(np.float32),
        "lts_s": np.where(si > ti, 1.0, 0.0).astype(np.float32),
    }
    ehp = np.zeros((4, 512), np.float32)
    for h in range(4):
        ehp[h, h * 128:(h + 1) * 128] = 1.0
    consts["ehp_s"] = ehp
    cten = {k: nc.inline_tensor(v, name="c_" + k) for k, v in consts.items()}

    from contextlib import ExitStack
    with ExitStack() as ctx:
        b._ctx = ctx
        E = ctx.enter_context
        te, ve, ac, gp, sy = nc.tensor, nc.vector, nc.scalar, nc.gpsimd, nc.sync

        PA = PBank(E(nc.psum_tensor("PA", [128, 512], F32)))
        PB = PBank(E(nc.psum_tensor("PB", [128, 512], F32)))
        PC = PBank(E(nc.psum_tensor("PC", [128, 512], F32)))
        PD = PBank(E(nc.psum_tensor("PD", [128, 512], F32)))
        PE = PBank(E(nc.psum_tensor("PE", [128, 512], F32)))
        PF = PBank(E(nc.psum_tensor("PF", [128, 512], F32)))
        PT = PBank(E(nc.psum_tensor("PT", [128, 1024], BF16)))
        PS = PBank(E(nc.psum_tensor("PS", [128, 512], F32)))
        PARITY = [(PA, PB, PC), (PD, PE, PF)]

        sb = lambda name, shape, dt: E(nc.sbuf_tensor(name, shape, dt))
        cs = {k: sb(k, list(v.shape), BF16 if v.dtype == ml_dtypes.bfloat16 else F32)
              for k, v in consts.items()}
        lnw_s = sb("lnw_s", [128, 512], F32)
        lnb_s = sb("lnb_s", [128, 512], F32)
        cw_s = sb("cw_s", [128, 48], F32)
        cb_s = sb("cb_s", [128, 12], F32)
        ba_s = sb("ba_s", [4, 1], F32)
        bb_s = sb("bb_s", [4, 1], F32)
        w_sb = sb("w_sb", [128, 32768], BF16)
        wab_s = sb("wab_s", [128, 128], BF16)
        z32 = [sb(f"z32_{p}", [128, 512], F32) for p in range(2)]
        zbf = sb("zbf", [128, 512], BF16)
        fw_s = sb("fw_s", [128, NE * ncap], F32)
        cidx_s = sb("cidx_s", [128, 64], I32)
        oidx_s = sb("oidx_s", [128, 64], I32)
        smalls = [sb(f"smalls{p}", [128, 64], F32) for p in range(3)]
        stat = sb("stat", [128, 24], F32)

        SM = dict(laT=0, lbT=4, bT=8, gT=12, ngT=16, gamT=20, gCb=24, cs=28, gCr=44,
                  gpT=32, bgp=36, r2T=40, osum=44, osq=48, omu=52, orstd=56, tmp=60)

        def tap(name, src_ap, rows=128, cols=None, via=None):
            if name not in _TAPS:
                return
            for snm in ("dv", "ac", "pe", "gp"):
                b.wait(sy, b.tok(snm))
            cols = cols if cols is not None else src_ap.shape[-1]
            r = dbg_row[0]
            b.wait(gp, b.tok("dbgst"))
            for snm in ("dv", "ac", "pe", "gp"):
                b.wait(gp, b.tok(snm))
            b.dma(gp, "dbgst", dbg_d[r:r + rows, 0:cols], src_ap)
            for eng in (te, ve, ac, gp, sy):
                b.wait(eng, b.tok("dbgst"))
            dbg_row[0] += rows
            print(f"  [tap {name} -> dbg rows {r}:{r+rows} cols {cols}]")

        for k in cs:
            b.dma(sy, "init", cs[k][:], cten[k].ap())
        for dst, src in ((lnw_s, "lnwB"), (lnb_s, "lnbB"), (cw_s, "cw"),
                         (cb_s, "cb"), (ba_s, "ba4"), (bb_s, "bb4"),
                         (fw_s, "fwcol"), (cidx_s, "cidx"), (oidx_s, "oidx")):
            b.dma(sy, "init", dst[:], dram[src][:, :])
        t_init = b.tok("init")

        for k4, wn in enumerate(("wq", "wk", "wv", "wg")):
            b.dma(gp, "wl", w_sb[:].rearrange("p (m fc o) -> p m fc o", m=4, o=512)[:, k4],
                  dram[wn][:, :].rearrange("(fc p) o -> p fc o", p=128))
        b.dma(gp, "wl", wab_s[:].rearrange("p (fc o) -> p fc o", o=8),
              dram["wab"][:, :].rearrange("(fc p) o -> p fc o", p=128))
        t_wl = b.tok("wl")

        b.wait(gp, t_wl)
        gp.memset(z32[0][:], 0.0)
        tz0 = b.inc(gp.memset(zbf[:], 0.0), "gp")

        with ExitStack() as arenaP:
            EP = arenaP.enter_context
            sp = lambda name, shape, dt: EP(nc.sbuf_tensor(name, shape, dt))
            x_tm = [sp(f"x_tm{p}", [128, 2048], F32) for p in range(2)]
            sqscr = sp("sqscr", [128, 256], F32)
            xn_tm = sp("xn_tm", [128, 2048], BF16)
            xn_fm = sp("xn_fm", [128, 8192], BF16)
            halo_sb = sp("halo_sb", [128, 36], BF16)
            ctmp = [sp(f"ctmp{p}", [128, 512], BF16) for p in range(2)]
            qkv = [sp(f"qkv{p}", [128, 6144], BF16) for p in range(2)]
            g_ch = [sp(f"g_ch{p}", [128, 512], BF16) for p in range(3)]
            zz = sp("zz", [4, 512], F32)
            abt1 = sp("abt1", [4, 512], F32)
            la4 = sp("la4", [4, 512], F32)
            lb4 = sp("lb4", [4, 512], F32)
            b4 = sp("b4", [4, 512], F32)
            rows_cat = sp("rows_cat", [4, 384], F32)
            e_scr = sp("e_scr", [128, 2048], F32)
            nsm1 = [sp(f"nsm1_{p}", [128, 512], BF16) for p in range(2)]
            ntm1 = [sp(f"ntm1_{p}", [128, 512], BF16) for p in range(2)]
            dsm = [sp(f"dsm{p}", [128, 512], BF16) for p in range(3)]
            ktm = [sp(f"ktm{p}", [128, 512], BF16) for p in range(3)]
            vtm = [sp(f"vtm{p}", [128, 512], BF16) for p in range(2)]
            nsm_s = [sp(f"nsm_s{p}", [128, 512], BF16) for p in range(2)]
            ntm_s = [sp(f"ntm_s{p}", [128, 512], BF16) for p in range(2)]
            rhs0 = [sp(f"rhs0_{p}", [128, 512], BF16) for p in range(2)]
            bufU = [sp(f"bufU{p}", [128, 512], BF16) for p in range(2)]
            bufUf = [sp(f"bufUf{p}", [128, 512], BF16) for p in range(2)]
            rbuf = sp("rbuf", [128, 512], BF16)
            usc = sp("usc", [128, 512], BF16)
            rz_t = sp("rz_t", [128, 512], BF16)
            o_sb = sp("o_sb", [128, 512], F32)
            on_b = sp("on_b", [128, 512], BF16)
            gtanh = sp("gtanh", [128, 512], BF16)
            gcA = sp("gcA", [128, 512], BF16)
            gcB = sp("gcB", [128, 512], BF16)
            og_sb = [sp(f"og_sb{p}", [128, 512], BF16) for p in range(2)]

            FB1, FB2, FX = PD, PE, PF

            b.wait(gp, tz0)
            t_cz = b.inc(gp.memset(halo_sb[:], 0.0), "gp")

            tk = {
                "x_free": [None, None], "xn_done": None, "gp_stat": None, "xn_tnm": None,
                "xnfm": None, "qkv_free": [None, None],
                "qkv": [None, None], "g": [None] * 3, "g_free": [None] * 3,
                "ab": None, "ab_free": None, "ab_free2": None,
                "kvtm": [None] * 3, "nd": [None, None],
                "sm_free": [None] * 3, "og_free": [None, None],
                "z_ok": (tz0[0], tz0[1]),
                "gts": None, "rhs0": [None, None], "rhs0_free": [None, None],
                "ufin_free": [[], []], "ufz": [None, None],
                "lad": [None, None], "rtok": [None, None], "og_done": None,
            }

            def xn_ln(i, tcn, tok_escr):
                t0 = i * TILE
                jp = (i * 4 + tcn) % 2
                b.waits(sy, [tk["x_free"][jp]])
                tld = b.dma(sy, f"xl{jp}", x_tm[jp][:], dram["xb"][t0 + tcn * 128: t0 + (tcn + 1) * 128, :])
                b.waits(ve, [tld, tk["gp_stat"]])
                tgs = b.inc(ve.reduce_sum(stat[:, 0:1], x_tm[jp][:], AXX), "dv")
                b.waits(ac, [tld, tk["x_free"][1 - jp], tok_escr])
                tsq = b.inc(ac.activation(e_scr[:], x_tm[jp][:], AF.Square,
                                          accum_out=stat[:, 1:2]), "ac")
                b.waits(ve, [tsq, tgs])
                b.raw(ve, ve.tensor_scalar(stat[:, 3:4], stat[:, 0:1], 1.0 / D, None, ALU.mult), "dv")
                b.raw(ve, ve.tensor_tensor(stat[:, 13:14], stat[:, 3:4], stat[:, 3:4], ALU.mult), "dv")
                b.raw(ve, ve.tensor_scalar(stat[:, 4:5], stat[:, 13:14], EPS, None, ALU.subtract), "dv")
                tvar = b.raw(ve, ve.scalar_tensor_tensor(stat[:, 6:7], stat[:, 1:2], 1.0 / D,
                                                         stat[:, 4:5], ALU.mult, ALU.subtract), "dv")
                tk["gp_stat"] = tvar
                b.wait(ac, tvar)
                b.raw(ac, ac.activation(stat[:, 7:8], stat[:, 6:7], AF.Ln), "ac")
                trs = b.inc(ac.activation(stat[:, 8:9], stat[:, 7:8], AF.Exp, scale=-0.5), "ac")
                b.waits(ve, [trs, tk["xn_done"]])
                tnm = b.inc(ve.tensor_scalar(xn_tm[:], x_tm[jp][:], stat[:, 3:4], stat[:, 8:9],
                                             ALU.subtract, ALU.mult), "dv")
                tk["x_free"][jp] = tnm
                tk["xn_tnm"] = tnm

            def xn_tr(i, tcn):
                tnm = tk["xn_tnm"]
                for half in range(2):
                    PT.pre_write(b, te)
                    b.waits(te, [tnm, t_init])
                    for fc8 in range(8):
                        fc = half * 8 + fc8
                        inst = te.transpose(PT.t[:, fc8 * 128:(fc8 + 1) * 128],
                                            xn_tm[:, fc * 128:(fc + 1) * 128], cs["id_bf"][:])
                    tpe = b.inc(inst, "pe")
                    PT.writer = tpe
                    b.wait(ac, tpe)
                    for fc8 in range(8):
                        fc = half * 8 + fc8
                        inst = ac.activation(
                            xn_fm[:, fc * 512 + tcn * 128: fc * 512 + (tcn + 1) * 128],
                            PT.t[:, fc8 * 128:(fc8 + 1) * 128], AF.Copy)
                    tac = b.inc(inst, "ac")
                    PT.readers.append(tac)
                    tk["xn_done"] = tpe
                if tcn == 3:
                    tk["xnfm"] = b.tok("ac")

            def tile_rest(i):
                par = i % 2
                prev_trl = None
                for chn in range(2):
                    FX.pre_write(b, te)
                    b.waits(te, [tk["ab_free"], tk["ab_free2"], tk["xnfm"], t_wl])
                    for fc in range(16):
                        inst = te.matmul(FX.t[0:4, :],
                                         wab_s[:, fc * 8 + chn * 4: fc * 8 + chn * 4 + 4],
                                         xn_fm[:, fc * 512:(fc + 1) * 512],
                                         start=(fc == 0), stop=(fc == 15))
                    tpe = b.inc(inst, "pe")
                    FX.writer = tpe
                    b.waits(ve, [tpe, prev_trl])
                    dst = la4 if chn == 0 else lb4
                    tzz = b.inc(ve.tensor_scalar(zz[:], FX.t[0:4, :],
                                                 (ba_s if chn == 0 else bb_s)[:],
                                                 None, ALU.add), "dv")
                    FX.readers.append(tzz)
                    b.wait(ac, tzz)
                    b.raw(ac, ac.activation(abt1[:], zz[:], AF.Abs), "ac")
                    b.raw(ac, ac.activation(dst[:], abt1[:], AF.Exp, scale=-1.0), "ac")
                    b.raw(ac, ac.activation(abt1[:], dst[:], AF.Ln, bias=1.0), "ac")
                    trl = b.inc(ac.activation(dst[:], zz[:], AF.Relu, scale=-1.0), "ac")
                    prev_trl = trl
                    b.wait(ve, trl)
                    b.inc(ve.scalar_tensor_tensor(dst[:], abt1[:], -1.0, dst[:],
                                                  ALU.mult, ALU.subtract), "dv")
                b.wait(ac, b.tok("dv"))
                tk["ab"] = b.inc(ac.activation(b4[:], lb4[:], AF.Exp), "ac")

                pp = [FB1, FB2]
                prev_tcv = None
                for m in range(3):
                    for oc in range(4):
                        blk = m * 4 + oc
                        bank = pp[blk % 2]
                        bank.pre_write(b, te)
                        b.waits(te, [tk["xnfm"], t_wl, tk["qkv_free"][par]])
                        for fc in range(16):
                            inst = te.matmul(bank.t[:],
                                             w_sb[:, m * 8192 + fc * 512 + oc * 128:
                                                  m * 8192 + fc * 512 + (oc + 1) * 128],
                                             xn_fm[:, fc * 512:(fc + 1) * 512],
                                             start=(fc == 0), stop=(fc == 15))
                        tpe = b.inc(inst, "pe")
                        bank.writer = tpe
                        cwc = lambda j2: cw_s[:, blk * 4 + j2: blk * 4 + j2 + 1]
                        hl = lambda a, b2: halo_sb[:, blk * 3 + a: blk * 3 + b2]
                        b.waits(ve, [tpe, prev_tcv, t_cz])
                        b.raw(ve, ve.tensor_scalar(ctmp[0][:, 3:512], bank.t[:, 0:509],
                                                   cwc(0), None, ALU.mult), "dv")
                        b.raw(ve, ve.tensor_scalar(ctmp[0][:, 0:3], hl(0, 3),
                                                   cwc(0), None, ALU.mult), "dv")
                        b.raw(ve, ve.scalar_tensor_tensor(ctmp[1][:, 2:512], bank.t[:, 0:510],
                                                          cwc(1), ctmp[0][:, 2:512], ALU.mult, ALU.add), "dv")
                        b.raw(ve, ve.scalar_tensor_tensor(ctmp[1][:, 0:2], hl(1, 3),
                                                          cwc(1), ctmp[0][:, 0:2], ALU.mult, ALU.add), "dv")
                        b.raw(ve, ve.scalar_tensor_tensor(ctmp[0][:, 1:512], bank.t[:, 0:511],
                                                          cwc(2), ctmp[1][:, 1:512], ALU.mult, ALU.add), "dv")
                        b.raw(ve, ve.scalar_tensor_tensor(ctmp[0][:, 0:1], hl(2, 3),
                                                          cwc(2), ctmp[1][:, 0:1], ALU.mult, ALU.add), "dv")
                        b.raw(ve, ve.scalar_tensor_tensor(ctmp[1][:], bank.t[:],
                                                          cwc(3), ctmp[0][:], ALU.mult, ALU.add), "dv")
                        thl = b.inc(ve.tensor_copy(hl(0, 3), bank.t[:, 509:512]), "dv")
                        bank.readers.append(thl)
                        b.wait(ac, thl)
                        if m == 1:
                            tsl = b.inc(ac.activation(ctmp[0][:], ctmp[1][:], AF.Silu,
                                                      bias=cb_s[:, blk: blk + 1]), "ac")
                            b.wait(ve, tsl)
                            prev_tcv = b.inc(
                                ve.tensor_scalar(qkv[par][:, blk * 512:(blk + 1) * 512],
                                                 ctmp[0][:], float(C) ** -0.5, None, ALU.mult), "dv")
                        else:
                            prev_tcv = b.inc(ac.activation(qkv[par][:, blk * 512:(blk + 1) * 512],
                                                           ctmp[1][:], AF.Silu,
                                                           bias=cb_s[:, blk: blk + 1]), "ac")
                tk["qkv"][par] = ("MULTI", [b.tok("ac"), b.tok("dv")])

            def qkv_wait(eng, par):
                tt = tk["qkv"][par]
                if tt and tt[0] == "MULTI":
                    b.waits(eng, tt[1])

            def front_a(j):
                p3 = j % 3
                i, ccn = j // 4, j % 4
                sm = smalls[p3]
                t_lo = ccn * 128

                FX.pre_write(b, te)
                b.waits(te, [tk["sm_free"][p3], tk["ab"]])
                te.transpose(FX.t[:, 0:4], la4[0:4, t_lo:t_lo + 128], cs["id_f"][0:4, 0:4])
                te.transpose(FX.t[:, 4:8], lb4[0:4, t_lo:t_lo + 128], cs["id_f"][0:4, 0:4])
                tpe = b.inc(te.transpose(FX.t[:, 8:12], b4[0:4, t_lo:t_lo + 128],
                                         cs["id_f"][0:4, 0:4]), "pe")
                FX.writer = tpe
                tk["ab_free"] = tpe
                b.wait(ac, tpe)
                te1 = b.inc(ac.activation(sm[:, SM["laT"]:SM["laT"] + 12], FX.t[:, 0:12], AF.Copy), "ac")
                FX.readers.append(te1)

                FX.pre_write(b, te)
                b.wait(te, te1)
                te.matmul(FX.t[:, 16:20], cs["ut_s"][:], sm[:, SM["laT"]:SM["laT"] + 4],
                          start=True, stop=True)
                te.matmul(FX.t[:, 20:24], cs["lts_s"][:], sm[:, SM["laT"]:SM["laT"] + 4],
                          start=True, stop=True)
                tpe = b.inc(te.matmul(FX.t[0:4, 128:256], sm[:, SM["laT"]:SM["laT"] + 4],
                                      cs["ut_s"][:], start=True, stop=True), "pe")
                FX.writer = tpe
                b.wait(ac, tpe)
                ac.activation(sm[:, SM["gT"]:SM["gT"] + 4], FX.t[:, 16:20], AF.Copy)
                ac.activation(sm[:, SM["ngT"]:SM["ngT"] + 4], FX.t[:, 16:20], AF.Copy, scale=-1.0)
                ac.activation(sm[:, SM["gamT"]:SM["gamT"] + 4], FX.t[:, 16:20], AF.Exp)
                ac.activation(sm[:, SM["cs"]:SM["cs"] + 4], FX.t[:, 20:24], AF.Exp)
                te2 = b.inc(ac.activation(rows_cat[:, 0:128], FX.t[0:4, 128:256], AF.Copy), "ac")
                FX.readers.append(te2)
                b.wait(ve, te2)
                b.inc(ve.tensor_tensor(sm[:, SM["gCb"]:SM["gCb"] + 4],
                                       sm[:, SM["cs"]:SM["cs"] + 4],
                                       sm[:, SM["gamT"]:SM["gamT"] + 4], ALU.mult), "dv")
                ve.scalar_tensor_tensor(rows_cat[:, 256:384], la4[0:4, t_lo:t_lo + 128], -1.0,
                                        rows_cat[:, 0:128], ALU.mult, ALU.add)
                b.raw(ve, ve.scalar_tensor_tensor(sm[:, SM["tmp"]:SM["tmp"] + 4],
                                                  sm[:, SM["laT"]:SM["laT"] + 4], -1.0,
                                                  sm[:, SM["gT"]:SM["gT"] + 4], ALU.mult, ALU.add), "dv")
                ve.tensor_tensor(rows_cat[:, 128:256], rows_cat[:, 256:384],
                                 lb4[0:4, t_lo:t_lo + 128], ALU.add)
                tr2 = b.inc(ve.tensor_tensor(sm[:, SM["r2T"]:SM["r2T"] + 4],
                                             sm[:, SM["tmp"]:SM["tmp"] + 4],
                                             sm[:, SM["lbT"]:SM["lbT"] + 4], ALU.add), "dv")
                tk["ab_free2"] = tr2
                b.wait(ac, tr2)
                tgp = b.inc(ac.activation(sm[:, SM["gpT"]:SM["gpT"] + 4],
                                          sm[:, SM["tmp"]:SM["tmp"] + 4], AF.Exp), "ac")
                b.wait(ve, tgp)
                tbgp = b.inc(ve.scalar_tensor_tensor(sm[:, SM["bgp"]:SM["bgp"] + 4],
                                                     sm[:, SM["gpT"]:SM["gpT"] + 4], -1.0,
                                                     sm[:, SM["bT"]:SM["bT"] + 4],
                                                     ALU.mult, ALU.mult), "dv")
                tk["bgp"] = tbgp

                FX.pre_write(b, te)
                b.waits(te, [tk["g_free"][p3], tk["xnfm"]])
                for fc in range(16):
                    inst = te.matmul(FX.t[:],
                                     xn_fm[:, fc * 512 + t_lo: fc * 512 + t_lo + 128],
                                     w_sb[:, 3 * 8192 + fc * 512: 3 * 8192 + (fc + 1) * 512],
                                     start=(fc == 0), stop=(fc == 15))
                tg = b.inc(inst, "pe")
                FX.writer = tg
                b.waits(ac, [tg, tk["gts"]])
                tth = b.inc(ac.activation(gtanh[:], FX.t[:], AF.Tanh, scale=0.5), "ac")
                FX.readers.append(tth)
                b.wait(ve, tth)
                tk["gts"] = b.inc(ve.tensor_scalar(g_ch[p3][:], gtanh[:], 1.0, 0.5,
                                                   ALU.add, ALU.mult), "dv")
                tk["g"][p3] = tk["gts"]

                FB1.pre_write(b, te)
                FB2.pre_write(b, te)
                b.waits(te, [tr2])
                for h in range(4):
                    te.matmul(FB1.t[:, h * 128:(h + 1) * 128],
                              cs["ehp_s"][0:4, h * 128:(h + 1) * 128], rows_cat[:, 0:128],
                              start=True, stop=True)
                for h in range(4):
                    inst = te.matmul(FB2.t[:, h * 128:(h + 1) * 128],
                                     cs["ehp_s"][0:4, h * 128:(h + 1) * 128], rows_cat[:, 128:256],
                                     start=True, stop=True)
                tbcp = b.inc(inst, "pe")
                FB1.writer = tbcp
                FB2.writer = tbcp
                bc_g = lambda h: FB1.t[:, h * 128:(h + 1) * 128]
                bc_r2 = lambda h: FB2.t[:, h * 128:(h + 1) * 128]

                b.waits(ve, [tbcp, b.tok("ac")])
                for h in range(4):
                    ve.scalar_tensor_tensor(e_scr[:, h * 128:(h + 1) * 128],
                                            bc_g(h),
                                            sm[:, SM["gT"] + h: SM["gT"] + h + 1],
                                            cs["bndi_s"][:], ALU.subtract, ALU.min)
                for h in range(4):
                    ve.scalar_tensor_tensor(e_scr[:, 512 + h * 128: 512 + (h + 1) * 128],
                                            bc_r2(h),
                                            sm[:, SM["gT"] + h: SM["gT"] + h + 1],
                                            cs["bnds_s"][:], ALU.subtract, ALU.min)
                for h in range(4):
                    inst = ve.scalar_tensor_tensor(e_scr[:, 1024 + h * 128: 1024 + (h + 1) * 128],
                                                   bc_g(h),
                                                   sm[:, SM["r2T"] + h: SM["r2T"] + h + 1],
                                                   cs["bndt_s"][:], ALU.subtract, ALU.max)
                targ3 = b.inc(inst, "dv")
                FB1.readers.append(targ3)
                FB2.readers.append(targ3)
                tk["targ3"] = targ3
                b.wait(ac, targ3)
                b.raw(ac, ac.activation(e_scr[:, 1536:2048], e_scr[:, 0:512], AF.Exp), "ac")
                b.raw(ac, ac.activation(e_scr[:, 0:512], e_scr[:, 512:1024], AF.Exp), "ac")
                tk["texp"] = b.inc(ac.activation(e_scr[:, 512:1024], e_scr[:, 1024:1536],
                                                 AF.Exp, scale=-1.0), "ac")

            def front_b(j):
                p2 = j % 2
                p3 = j % 3
                i, ccn = j // 4, j % 4
                sm = smalls[p3]
                t_lo = ccn * 128
                kfm = lambda h: qkv[i % 2][:, (4 + h) * 512 + t_lo: (4 + h) * 512 + t_lo + 128]
                vfm = lambda h: qkv[i % 2][:, (8 + h) * 512 + t_lo: (8 + h) * 512 + t_lo + 128]
                qfm = lambda h: qkv[i % 2][:, h * 512 + t_lo: h * 512 + t_lo + 128]

                FB1.pre_write(b, te)
                FB2.pre_write(b, te)
                PT.pre_write(b, te)
                qkv_wait(te, i % 2)
                for h in range(4):
                    te.matmul(FB1.t[:, h * 128:(h + 1) * 128], kfm(h), kfm(h), start=True, stop=True)
                for h in range(4):
                    te.matmul(FB2.t[:, h * 128:(h + 1) * 128], kfm(h), qfm(h), start=True, stop=True)
                for h in range(4):
                    te.transpose(PT.t[:, h * 128:(h + 1) * 128], kfm(h), cs["id_bf"][:])
                for h in range(4):
                    inst = te.transpose(PT.t[:, 512 + h * 128: 512 + (h + 1) * 128], vfm(h), cs["id_bf"][:])
                tp4 = b.inc(inst, "pe")
                FB1.writer = tp4
                FB2.writer = tp4
                PT.writer = tp4
                tk["qkv_free"][i % 2] = tp4
                b.wait(ac, tp4)
                ac.activation(ktm[p3][:], PT.t[:, 0:512], AF.Copy)
                tkv = b.inc(ac.activation(vtm[p2][:], PT.t[:, 512:1024], AF.Copy), "ac")
                PT.readers.append(tkv)
                tk["kvtm"][p3] = tkv

                b.waits(ve, [tk["texp"], tp4])
                for h in range(4):
                    ve.scalar_tensor_tensor(dsm[p3][:, h * 128:(h + 1) * 128],
                                            e_scr[:, 1536 + h * 128: 1536 + (h + 1) * 128], 1.0,
                                            FB2.t[:, h * 128:(h + 1) * 128], ALU.mult, ALU.mult)
                for h in range(4):
                    ve.scalar_tensor_tensor(nsm1[p2][:, h * 128:(h + 1) * 128],
                                            e_scr[:, h * 128:(h + 1) * 128], -1.0,
                                            FB1.t[:, h * 128:(h + 1) * 128], ALU.mult, ALU.mult)
                for h in range(4):
                    inst = ve.scalar_tensor_tensor(ntm1[p2][:, h * 128:(h + 1) * 128],
                                                   e_scr[:, 512 + h * 128: 512 + (h + 1) * 128], -1.0,
                                                   FB1.t[:, h * 128:(h + 1) * 128], ALU.mult, ALU.mult)
                tb2 = b.inc(inst, "dv")
                FB1.readers.append(tb2)
                FB2.readers.append(tb2)
                tk["nd"][p2] = tb2

                b.waits(ve, [tkv, tk["bgp"], tk["rhs0_free"][p2]])
                for h in range(4):
                    inst = ve.tensor_scalar(rhs0[p2][:, h * 128:(h + 1) * 128],
                                            vtm[p2][:, h * 128:(h + 1) * 128],
                                            sm[:, SM["bT"] + h: SM["bT"] + h + 1], None, ALU.mult)
                tk["rhs0"][p2] = b.inc(inst, "dv")
                if j == 0:
                    tap("fsm0", smalls[0][:, 0:64], cols=64)
                    tap("fdsm0", dsm[0][:, 0:512])
                    tap("fnsm0", nsm1[0][:, 0:512])
                    tap("fntm0", ntm1[0][:, 0:512])

            def r_step(j):
                p2 = j % 2
                p3 = j % 3
                sm = smalls[p3]
                i, ccn = j // 4, j % 4
                t_lo = ccn * 128
                kfm = lambda h: qkv[i % 2][:, (4 + h) * 512 + t_lo: (4 + h) * 512 + t_lo + 128]
                PS.pre_write(b, te)
                b.waits(te, [tk["z_ok"], tk["qkv_free"][i % 2] if False else None])
                for h in range(4):
                    inst = te.matmul(PS.t[:, h * 128:(h + 1) * 128], kfm(h),
                                     zbf[:, h * 128:(h + 1) * 128], start=True, stop=True)
                tkz = b.inc(inst, "pe")
                PS.writer = tkz
                b.waits(ve, [tkz, tk["rhs0"][p2]])
                for h in range(4):
                    inst = ve.scalar_tensor_tensor(rbuf[:, h * 128:(h + 1) * 128],
                                                   PS.t[:, h * 128:(h + 1) * 128],
                                                   sm[:, SM["bgp"] + h: SM["bgp"] + h + 1],
                                                   rhs0[p2][:, h * 128:(h + 1) * 128],
                                                   ALU.mult, ALU.add)
                trr = b.inc(inst, "dv")
                PS.readers.append(trr)
                tk["rtok"][p2] = trr
                tk["rhs0_free"][p2] = trr

            def ladder(j):
                p2 = j % 2
                ucur = rbuf
                tok_u = tk["rtok"][p2]
                ncs, nct = nsm1[p2], ntm1[p2]
                tok_n = tk["nd"][p2]
                for lvl in range(7):
                    if lvl < 6:
                        PA.pre_write(b, te)
                        PB.pre_write(b, te)
                        b.waits(te, [tok_n])
                        for h in range(4):
                            te.matmul(PA.t[:, h * 128:(h + 1) * 128],
                                      ncs[:, h * 128:(h + 1) * 128],
                                      nct[:, h * 128:(h + 1) * 128], start=True, stop=True)
                        for h in range(4):
                            inst = te.matmul(PB.t[:, h * 128:(h + 1) * 128],
                                             nct[:, h * 128:(h + 1) * 128],
                                             ncs[:, h * 128:(h + 1) * 128], start=True, stop=True)
                        tsq_pe = b.inc(inst, "pe")
                        PA.writer = tsq_pe
                        PB.writer = tsq_pe
                    PC.pre_write(b, te)
                    b.waits(te, [tok_u, tok_n])
                    for h in range(4):
                        te.matmul(PC.t[:, h * 128:(h + 1) * 128], cs["id_bf"][:],
                                  ucur[:, h * 128:(h + 1) * 128], start=True, stop=False)
                        inst = te.matmul(PC.t[:, h * 128:(h + 1) * 128],
                                         ncs[:, h * 128:(h + 1) * 128],
                                         ucur[:, h * 128:(h + 1) * 128], start=False,
                                         stop=True)
                    tU_pe = b.inc(inst, "pe")
                    PC.writer = tU_pe
                    if lvl < 6:
                        b.wait(ac, tsq_pe)
                        ac.activation(ntm_s[lvl % 2][:], PA.t[:], AF.Copy)
                        tok_n = b.inc(ac.activation(nsm_s[lvl % 2][:], PB.t[:], AF.Copy), "ac")
                        PA.readers.append(tok_n)
                        PB.readers.append(tok_n)
                        ncs, nct = nsm_s[lvl % 2], ntm_s[lvl % 2]
                    udst = bufUf[p2] if lvl == 6 else bufU[lvl % 2]
                    if lvl % 2 == 1:
                        b.wait(ve, tU_pe)
                        tok_u = b.inc(ve.tensor_copy(udst[:], PC.t[:]), "dv")
                    else:
                        b.wait(ac, tU_pe)
                        if lvl == 6:
                            b.waits(ac, tk["ufin_free"][p2])
                        tok_u = b.inc(ac.activation(udst[:], PC.t[:], AF.Copy), "ac")
                    PC.readers.append(tok_u)
                    ucur = udst
                tk["lad"][p2] = tok_u
                if j == 0:
                    tap("fu0", bufUf[0][:, 0:512])

            def back_z(j):
                p2 = j % 2
                p3 = j % 3
                sm = smalls[p3]
                zcur, znxt = z32[j % 2], z32[(j + 1) % 2]
                i, ccn = j // 4, j % 4
                t_lo = ccn * 128
                qfm = lambda h: qkv[i % 2][:, h * 512 + t_lo: h * 512 + t_lo + 128]
                PS.pre_write(b, te)
                b.waits(te, [tk["z_ok"]])
                for h in range(4):
                    inst = te.matmul(PS.t[:, h * 128:(h + 1) * 128], qfm(h),
                                     zbf[:, h * 128:(h + 1) * 128], start=True, stop=True)
                tO1 = b.inc(inst, "pe")
                PS.writer = tO1
                b.waits(ve, [tk["lad"][p2]])
                for h in range(4):
                    inst = ve.tensor_scalar(usc[:, h * 128:(h + 1) * 128],
                                            bufUf[p2][:, h * 128:(h + 1) * 128],
                                            sm[:, SM["cs"] + h: SM["cs"] + h + 1], None, ALU.mult)
                tusc = b.inc(inst, "dv")
                tk["ufz"][p2] = tusc
                b.wait(ve, tO1)
                for h in range(4):
                    inst = ve.tensor_scalar(rz_t[:, h * 128:(h + 1) * 128],
                                            PS.t[:, h * 128:(h + 1) * 128],
                                            sm[:, SM["gamT"] + h: SM["gamT"] + h + 1],
                                            None, ALU.mult)
                trz = b.raw(ve, inst, "dv")
                PS.readers.append(trz)
                PS.pre_write(b, te)
                b.waits(te, [tusc, tk["kvtm"][p3]])
                for h in range(4):
                    inst = te.matmul(PS.t[:, h * 128:(h + 1) * 128],
                                     ktm[p3][:, h * 128:(h + 1) * 128],
                                     usc[:, h * 128:(h + 1) * 128], start=True, stop=True)
                tzm = b.inc(inst, "pe")
                PS.writer = tzm
                b.wait(ve, tzm)
                for h in range(4):
                    inst = ve.scalar_tensor_tensor(znxt[:, h * 128:(h + 1) * 128],
                                                   zcur[:, h * 128:(h + 1) * 128],
                                                   sm[:, SM["gCb"] + h: SM["gCb"] + h + 1],
                                                   PS.t[:, h * 128:(h + 1) * 128], ALU.mult, ALU.add)
                tz1 = b.raw(ve, inst, "dv")
                PS.readers.append(tz1)
                tz = b.inc(ve.tensor_copy(zbf[:], znxt[:]), "dv")
                tk["z_ok"] = tz

            def back_o(j):
                p3 = j % 3
                sm = smalls[p3]
                p2 = j % 2
                FB1.pre_write(b, te)
                b.waits(te, [tk["lad"][p2]])
                for h in range(4):
                    inst = te.matmul(FB1.t[:, h * 128:(h + 1) * 128],
                                     dsm[p3][:, h * 128:(h + 1) * 128],
                                     bufUf[p2][:, h * 128:(h + 1) * 128], start=True, stop=True)
                tO = b.inc(inst, "pe")
                FB1.writer = tO
                tk["ufin_free"][p2] = [tk["ufz"][p2], tO]
                b.wait(ve, tO)
                for h in range(4):
                    inst = ve.tensor_tensor(o_sb[:, h * 128:(h + 1) * 128],
                                            rz_t[:, h * 128:(h + 1) * 128],
                                            FB1.t[:, h * 128:(h + 1) * 128], ALU.add)
                tO2 = b.inc(inst, "dv")
                FB1.readers.append(tO2)
                b.wait(ve, tO2)
                for h in range(4):
                    inst = ve.reduce_sum(sm[:, SM["osum"] + h: SM["osum"] + h + 1],
                                         o_sb[:, h * 128:(h + 1) * 128], AXX)
                tsum = b.raw(ve, inst, "dv")
                b.wait(ac, tsum)
                for h in range(4):
                    inst = ac.activation(sqscr[:, 0:128], o_sb[:, h * 128:(h + 1) * 128], AF.Square,
                                         accum_out=sm[:, SM["osq"] + h: SM["osq"] + h + 1])
                    if h < 3:
                        b.raw(ac, inst, "ac")
                tsq = b.inc(inst, "ac")
                b.wait(ve, tsq)
                b.raw(ve, ve.tensor_scalar(sm[:, SM["omu"]:SM["omu"] + 4],
                                           sm[:, SM["osum"]:SM["osum"] + 4],
                                           1.0 / 128.0, None, ALU.mult), "dv")
                b.raw(ve, ve.tensor_tensor(sm[:, SM["tmp"]:SM["tmp"] + 4],
                                           sm[:, SM["omu"]:SM["omu"] + 4],
                                           sm[:, SM["omu"]:SM["omu"] + 4], ALU.mult), "dv")
                b.raw(ve, ve.tensor_scalar(sm[:, SM["osum"]:SM["osum"] + 4],
                                           sm[:, SM["tmp"]:SM["tmp"] + 4],
                                           EPS, None, ALU.subtract), "dv")
                tvar = b.inc(ve.scalar_tensor_tensor(sm[:, SM["orstd"]:SM["orstd"] + 4],
                                                     sm[:, SM["osq"]:SM["osq"] + 4], 1.0 / 128.0,
                                                     sm[:, SM["osum"]:SM["osum"] + 4],
                                                     ALU.mult, ALU.subtract), "dv")
                b.wait(ac, tvar)
                b.raw(ac, ac.activation(sm[:, SM["tmp"]:SM["tmp"] + 4],
                                        sm[:, SM["orstd"]:SM["orstd"] + 4], AF.Ln), "ac")
                trs = b.inc(ac.activation(sm[:, SM["orstd"]:SM["orstd"] + 4],
                                          sm[:, SM["tmp"]:SM["tmp"] + 4], AF.Exp, scale=-0.5), "ac")
                b.waits(ve, [trs, tk.get("og_done")])
                for h in range(4):
                    inst = ve.tensor_scalar(on_b[:, h * 128:(h + 1) * 128],
                                            o_sb[:, h * 128:(h + 1) * 128],
                                            sm[:, SM["omu"] + h: SM["omu"] + h + 1],
                                            sm[:, SM["orstd"] + h: SM["orstd"] + h + 1],
                                            ALU.subtract, ALU.mult)
                tOn = b.inc(inst, "dv")
                par2 = j % 2
                b.waits(gp, [tOn, tk["g"][p3], tk["og_free"][par2], t_init])
                gslc = g_ch[p3][:]
                b.raw(gp, gp.tensor_tensor(gcA[:], on_b[:], lnw_s[:], ALU.mult), "gp")
                b.raw(gp, gp.tensor_tensor(gcB[:], gcA[:], gslc, ALU.mult), "gp")
                b.raw(gp, gp.tensor_tensor(gcA[:], gslc, lnb_s[:], ALU.mult), "gp")
                tog = b.inc(gp.tensor_tensor(og_sb[par2][:], gcB[:], gcA[:], ALU.add), "gp")
                tk["g_free"][p3] = tog
                tk["og_done"] = tog
                b.wait(sy, tog)
                tk["og_free"][par2] = b.dma(sy, f"ogst{par2}", ag_in[j * 128:(j + 1) * 128, :],
                                            og_sb[par2][:])
                tk["sm_free"][p3] = tOn
                if j == 0:
                    tap("o0", o_sb[:, 0:512])

            nch = _KTILES * CPT
            for tcn in range(4):
                xn_ln(0, tcn, None)
                xn_tr(0, tcn)
            for m in range(nch + 2):
                if m >= 2:
                    back_z(m - 2)
                if m < nch:
                    if m % 4 == 0:
                        tile_rest(m // 4)
                    front_a(m)
                if 1 <= m <= nch:
                    r_step(m - 1)
                    ladder(m - 1)
                if m < nch:
                    front_b(m)
                if m < nch and (m // 4 + 1) < _KTILES:
                    xn_ln(m // 4 + 1, m % 4, tk["nd"][m % 2])
                if m >= 2:
                    back_o(m - 2)
                if m - 2 == nch // 2 - 1 and nch == 64:
                    b.waits(gp, [b.tok("ogst0"), b.tok("ogst1")])
                    gp.collective_compute(
                        "AllGather", ALU.bypass, replica_groups=[list(range(8))],
                        ins=[ag_in[0:4096, :].opt()], outs=[ago[0:8 * 4096, :].opt()],
                    ).then_inc(b.sem("cc1"))
                    b.cnt["cc1"] += 1
                if m < nch and (m // 4 + 1) < _KTILES:
                    xn_tr(m // 4 + 1, m % 4)
            t_scan_pe = b.tok("pe")
            if nch == 64:
                b.waits(gp, [b.tok("ogst0"), b.tok("ogst1")])
                gp.collective_compute(
                    "AllGather", ALU.bypass, replica_groups=[list(range(8))],
                    ins=[ag_in[4096:8192, :].opt()], outs=[ago[8 * 4096:16 * 4096, :].opt()],
                ).then_inc(b.sem("cc2"))
                b.cnt["cc2"] += 1

        if _KPH == "p":
            with ExitStack() as arenaS, nc.sbuf_tensor("bounce", [128, 2048], F32) as bnc:
                prev = None
                for tc in range(16):
                    b.waits(sy, [prev])
                    tld = b.dma(sy, "bnc", bnc[:], dram["xout"][tc * 128:(tc + 1) * 128, :])
                    b.wait(sy, tld)
                    prev = b.dma(sy, "bnc2", out_d[tc * 128:(tc + 1) * 128, :], bnc[:])
                b.wait(sy, b.tok("bnc2"))
                for snm in ("ogst0", "ogst1"):
                    b.wait(sy, b.tok(snm))
            return nc, rank512
        with ExitStack() as arenaF:
            EF = arenaF.enter_context
            sf = lambda name, shape, dt: EF(nc.sbuf_tensor(name, shape, dt))
            fx_tm = [sf(f"fx_tm{p}", [128, 2048], F32) for p in range(2)]
            fsq = sf("fsq", [128, 1024], F32)
            fxn_tm = sf("fxn_tm", [128, 2048], BF16)
            fxn_fm = sf("fxn_fm", [128, 16 * cap], BF16)
            fh = sf("fh", [128, 11 * cap], BF16)
            fgu = [[sf(f"fgu{m}{p}", [128, 2048], BF16) for p in range(2)] for m in range(2)]
            fdwb = [sf(f"fdwb{p}", [128, 11 * 512], BF16) for p in range(2)]
            fdo = [sf(f"fdo{p}", [128, 512], BF16) for p in range(2)]
            fhtmp = sf("fhtmp", [128, 512], BF16)

            b.wait(gp, b.tok("gp"))
            tzp = b.inc(gp.memset(fdo[0][:], 0.0), "gp")
            b.wait(sy, tzp)
            tzd = b.dma(sy, "zpd",
                        ag_in[OGROWS + 4 * NE * cap: OGROWS + 4 * NE * cap + NPAD * 4, :],
                        fdo[0][0:NPAD * 4, :])

            ft = {"xn_done": None, "fx_free": [None, None],
                  "fgu_free": [[None, None], [None, None]], "fdw_free": [None, None],
                  "fdo_free": [tzd, None], "h_free": None, "fxn_free": None}

            for e in range(NE):
                b.waits(te, [ft["fxn_free"]])
                for rc in range(ncap):
                    pr = rc % 2
                    b.waits(sy, [ft["fx_free"][pr]])
                    tld = b.dma(sy, f"fxl{pr}", fx_tm[pr][:],
                                dram["fx"][e * cap + rc * 128: e * cap + (rc + 1) * 128, :])
                    b.wait(ve, tld)
                    b.raw(ve, ve.reduce_sum(stat[:, 0:1], fx_tm[pr][:], AXX), "dv")
                    b.waits(ac, [tld, ft["fx_free"][1 - pr]])
                    sqh = fsq[:, 0:512]
                    b.raw(ac, ac.activation(sqh, fx_tm[pr][:, 0:512], AF.Square, accum_out=stat[:, 1:2]), "ac")
                    b.raw(ac, ac.activation(sqh, fx_tm[pr][:, 512:1024], AF.Square, accum_out=stat[:, 2:3]), "ac")
                    b.raw(ac, ac.activation(sqh, fx_tm[pr][:, 1024:1536], AF.Square, accum_out=stat[:, 9:10]), "ac")
                    tsq = b.inc(ac.activation(sqh, fx_tm[pr][:, 1536:2048], AF.Square,
                                              accum_out=stat[:, 10:11]), "ac")
                    b.wait(ve, tsq)
                    ve.tensor_scalar(stat[:, 3:4], stat[:, 0:1], 1.0 / D, None, ALU.mult)
                    ve.tensor_tensor(stat[:, 11:12], stat[:, 1:2], stat[:, 2:3], ALU.add)
                    b.raw(ve, ve.tensor_tensor(stat[:, 12:13], stat[:, 9:10], stat[:, 10:11], ALU.add), "dv")
                    ve.tensor_tensor(stat[:, 13:14], stat[:, 3:4], stat[:, 3:4], ALU.mult)
                    b.raw(ve, ve.tensor_tensor(stat[:, 5:6], stat[:, 11:12], stat[:, 12:13], ALU.add), "dv")
                    b.raw(ve, ve.tensor_scalar(stat[:, 4:5], stat[:, 13:14], EPS, None, ALU.subtract), "dv")
                    tvar = b.raw(ve, ve.scalar_tensor_tensor(stat[:, 6:7], stat[:, 5:6], 1.0 / D,
                                                             stat[:, 4:5], ALU.mult, ALU.subtract), "dv")
                    b.wait(ac, tvar)
                    b.raw(ac, ac.activation(stat[:, 7:8], stat[:, 6:7], AF.Ln), "ac")
                    trs = b.inc(ac.activation(stat[:, 8:9], stat[:, 7:8], AF.Exp, scale=-0.5), "ac")
                    b.waits(ve, [trs, ft["xn_done"]])
                    tnm = b.inc(ve.tensor_scalar(fxn_tm[:], fx_tm[pr][:], stat[:, 3:4], stat[:, 8:9],
                                                 ALU.subtract, ALU.mult), "dv")
                    ft["fx_free"][pr] = tnm
                    for half in range(2):
                        PT.pre_write(b, te)
                        b.wait(te, tnm)
                        for fc8 in range(8):
                            fc = half * 8 + fc8
                            inst = te.transpose(PT.t[:, fc8 * 128:(fc8 + 1) * 128],
                                                fxn_tm[:, fc * 128:(fc + 1) * 128], cs["id_bf"][:])
                        tpe = b.inc(inst, "pe")
                        PT.writer = tpe
                        b.wait(ac, tpe)
                        for fc8 in range(8):
                            fc = half * 8 + fc8
                            inst = ac.activation(fxn_fm[:, fc * cap + rc * 128: fc * cap + (rc + 1) * 128],
                                                 PT.t[:, fc8 * 128:(fc8 + 1) * 128], AF.Copy)
                        tac = b.inc(inst, "ac")
                        PT.readers.append(tac)
                        ft["xn_done"] = tpe
                t_fxn = b.tok("ac")

                for ic in range(11):
                    pr = ic % 2
                    tws = []
                    for m, wname in ((0, "fgw"), (1, "fuw")):
                        b.waits(gp, [ft["fgu_free"][m][pr]])
                        tws.append(b.dma(gp, f"fgu{m}{pr}", fgu[m][pr][:],
                                         dram[wname][e:e + 1, ic:ic + 1]
                                         .rearrange("a b p f -> (a b p) f")))
                    banks = (PA, PB) if pr == 0 else (PD, PE)
                    r0 = 0
                    for rb in rblocks:
                        for m in range(2):
                            bank = banks[m]
                            bank.pre_write(b, te)
                            b.waits(te, [tws[m], t_fxn])
                            for fc in range(16):
                                inst = te.matmul(bank.t[:, 0:rb],
                                                 fgu[m][pr][:, fc * 128:(fc + 1) * 128],
                                                 fxn_fm[:, fc * cap + r0: fc * cap + r0 + rb],
                                                 start=(fc == 0), stop=(fc == 15))
                            tpe = b.inc(inst, "pe")
                            bank.writer = tpe
                            if rb == rblocks[-1] and r0 + rb >= cap:
                                ft["fgu_free"][m][pr] = tpe
                        b.waits(ac, [b.tok("pe"), ft.get("prev_th")])
                        tsl = b.inc(ac.activation(fhtmp[:, 0:rb], banks[0].t[:, 0:rb], AF.Silu), "ac")
                        banks[0].readers.append(tsl)
                        banks[1].pre_read(b, ve)
                        b.waits(ve, [tsl, ft["h_free"]])
                        th = b.inc(ve.tensor_tensor(fh[:, ic * cap + r0: ic * cap + r0 + rb],
                                                    fhtmp[:, 0:rb], banks[1].t[:, 0:rb], ALU.mult), "dv")
                        banks[1].readers.append(th)
                        ft["prev_th"] = th
                        r0 += rb
                t_h = b.tok("dv")
                ft["fxn_free"] = b.tok("pe")

                for hg in range(4):
                    pr = hg % 2
                    b.waits(gp, [ft["fdw_free"][pr]])
                    twd = b.dma(gp, f"fdw{pr}", fdwb[pr][:],
                                dram["fdw"][e:e + 1, hg:hg + 1].rearrange("a b p f -> (a b p) f"))
                    for rc in range(ncap):
                        bank = (PC, PF)[rc % 2]
                        bank.pre_write(b, te)
                        b.waits(te, [twd, t_h])
                        for ic in range(11):
                            inst = te.matmul(bank.t[:],
                                             fh[:, ic * cap + rc * 128: ic * cap + (rc + 1) * 128],
                                             fdwb[pr][:, ic * 512:(ic + 1) * 512],
                                             start=(ic == 0), stop=(ic == 10))
                        tpe = b.inc(inst, "pe")
                        bank.writer = tpe
                        if rc == ncap - 1:
                            ft["fdw_free"][pr] = tpe
                        dpr = (hg * ncap + rc) % 2
                        b.waits(ve, [tpe, ft["fdo_free"][dpr]])
                        tds = b.inc(ve.tensor_scalar(fdo[dpr][:], bank.t[:],
                                                     fw_s[:, e * ncap + rc: e * ncap + rc + 1],
                                                     None, ALU.mult), "dv")
                        bank.readers.append(tds)
                        b.wait(sy, tds)
                        d0 = (e * cap + rc * 128) * 4
                        dst = ag_in[OGROWS + d0 + hg: OGROWS + d0 + 512 + hg, :] \
                            .rearrange("(p f) c -> p f c", f=4)[:, 0]
                        ft["fdo_free"][dpr] = b.dma(sy, f"fdst{dpr}", dst, fdo[dpr][:])
                ft["h_free"] = b.tok("pe")

        if _KPH == "pf":
            with nc.sbuf_tensor("bounce", [128, 2048], F32) as bnc:
                prev = None
                for tc in range(16):
                    b.waits(sy, [prev])
                    tld = b.dma(sy, "bnc", bnc[:], dram["xout"][tc * 128:(tc + 1) * 128, :])
                    b.wait(sy, tld)
                    prev = b.dma(sy, "bnc2", out_d[tc * 128:(tc + 1) * 128, :], bnc[:])
                b.wait(sy, b.tok("bnc2"))
                for snm in ("ogst0", "ogst1", "fdst0", "fdst1", "zpd"):
                    b.wait(sy, b.tok(snm))
            return nc, rank512
        for s in ("fdst0", "fdst1", "zpd"):
            b.wait(gp, b.tok(s))
        gp.collective_compute(
            "AllGather", ALU.bypass, replica_groups=[list(range(8))],
            ins=[ag_in[OGROWS:rank512, :].opt()], outs=[agd.ap().opt()],
        ).then_inc(b.sem("cc3"))
        b.cnt["cc3"] += 1
        t_ag1, t_ag2, t_ag3 = b.tok("cc1"), b.tok("cc2"), b.tok("cc3")

        if _KPH == "pfa":
            b.waits(gp, [t_ag1, t_ag2, t_ag3])
            with nc.sbuf_tensor("bounce", [128, 2048], F32) as bnc:
                prev = None
                for tc in range(16):
                    b.waits(sy, [prev])
                    tld = b.dma(sy, "bnc", bnc[:], dram["xout"][tc * 128:(tc + 1) * 128, :])
                    b.wait(sy, tld)
                    prev = b.dma(sy, "bnc2", out_d[tc * 128:(tc + 1) * 128, :], bnc[:])
                b.wait(sy, b.tok("bnc2"))
                b.waits(sy, [t_ag1, t_ag2, t_ag3])
            return nc, rank512
        b.waits(gp, [t_scan_pe])
        two = b.dma(gp, "wol", w_sb[:].rearrange("p (fc o) -> p fc o", o=D),
                    dram["wo"][:, :].rearrange("(fc p) o -> p fc o", p=128))

        with ExitStack() as arenaW:
            EW = arenaW.enter_context
            sw = lambda name, shape, dt: EW(nc.sbuf_tensor(name, shape, dt))
            ogt = [sw(f"ogt{q}", [128, 512], BF16) for q in range(4)]
            xo_fm = [sw(f"xo_fm{p}", [128, 2048], BF16) for p in range(2)]
            fcb = [sw(f"fcb{q}", [128, 2048], BF16) for q in range(4)]
            fcA = sw("fcA", [128, 2048], BF16)
            fcB = sw("fcB", [128, 2048], BF16)
            fsum = sw("fsum", [128, 2048], F32)
            xrl = [sw(f"xrl{p}", [128, 2048], F32) for p in range(2)]
            outsb = [sw(f"outsb{p}", [128, 2048], F32) for p in range(2)]
            wt1 = sw("wt1", [128, 2048], F32)

            wtk = {"ogt_free": [None] * 4, "fcb_free": [None] * 4,
                   "xofm_free": [None, None], "fsum_free": None,
                   "xrl_free": [None, None], "out_free": [None, None]}

            for tc in range(16):
                pr = tc % 2
                b.waits(gp, [t_ag1, t_ag2])
                tog = []
                if tc > 0:
                    b.wait(gp, (f"ogth3", b.cnt["ogth3"]))
                for jr in range(4):
                    b.waits(gp, [wtk["ogt_free"][jr]])
                    if _KNOG:
                        tog.append(b.dma(gp, f"ogth{jr}", ogt[jr][:],
                                         ago[tc * 128:(tc + 1) * 128, :]))
                        continue
                    gp.indirect_dma_start(
                        out=ogt[jr][:], out_offset=None, in_=ago[:, :],
                        in_offset=IndirectOffsetOnAxis(oidx_s[:, tc * 4 + jr: tc * 4 + jr + 1], 0),
                    ).then_inc(b.sem(f"ogth{jr}"), 16)
                    b.cnt[f"ogth{jr}"] += 16
                    tog.append(b.tok(f"ogth{jr}"))
                    if _KGSER and jr >= 1:
                        b.wait(gp, b.tok(f"ogth{jr - 1}"))
                tfc = []
                b.wait(gp, (f"ogth3", b.cnt["ogth3"]))
                b.wait(gp, t_ag3)
                for j2 in range(4):
                    b.waits(gp, [wtk["fcb_free"][j2]])
                    if _KNOG:
                        tfc.append(b.dma(gp, f"fcbg{j2}", fcb[j2][:],
                                         agd[tc * 512:(tc + 1) * 512, :]
                                         .rearrange("(p f) c -> p (f c)", f=4)))
                        continue
                    gp.indirect_dma_start(
                        out=fcb[j2][:], out_offset=None, in_=agd[:, :],
                        in_offset=IndirectOffsetOnAxis(cidx_s[:, tc * 4 + j2: tc * 4 + j2 + 1], 0),
                    ).then_inc(b.sem(f"fcbg{j2}"), 16)
                    b.cnt[f"fcbg{j2}"] += 16
                    tfc.append(b.tok(f"fcbg{j2}"))
                    if _KGSER:
                        if j2 >= 1:
                            b.wait(gp, b.tok(f"fcbg{j2 - 1}"))
                        elif tc > 0:
                            b.wait(gp, (f"fcbg3", b.cnt["fcbg3"] - 16)) if False else None
                b.waits(sy, [wtk["xrl_free"][pr]])
                txr = b.dma(sy, f"xrl{pr}", xrl[pr][:], dram["xout"][tc * 128:(tc + 1) * 128, :])

                for half in range(2):
                    PT.pre_write(b, te)
                    b.waits(te, tog[half * 2:(half + 1) * 2] + [wtk["xofm_free"][pr]])
                    for q2 in range(2):
                        jr = half * 2 + q2
                        for cc4 in range(4):
                            inst = te.transpose(PT.t[:, q2 * 512 + cc4 * 128: q2 * 512 + (cc4 + 1) * 128],
                                                ogt[jr][:, cc4 * 128:(cc4 + 1) * 128], cs["id_bf"][:])
                    tpe = b.inc(inst, "pe")
                    PT.writer = tpe
                    for q2 in range(2):
                        wtk["ogt_free"][half * 2 + q2] = tpe
                    b.wait(ac, tpe)
                    for q2 in range(2):
                        jr = half * 2 + q2
                        for cc4 in range(4):
                            fc = jr * 4 + cc4
                            inst = ac.activation(xo_fm[pr][:, fc * 128:(fc + 1) * 128],
                                                 PT.t[:, q2 * 512 + cc4 * 128: q2 * 512 + (cc4 + 1) * 128],
                                                 AF.Copy)
                    tac = b.inc(inst, "ac")
                    PT.readers.append(tac)
                txo = b.tok("ac")

                b.waits(ve, tfc + [wtk["fsum_free"]])
                ve.tensor_tensor(fcA[:], fcb[0][:], fcb[1][:], ALU.add)
                b.raw(ve, ve.tensor_tensor(fcB[:], fcb[2][:], fcb[3][:], ALU.add), "dv")
                tfs = b.inc(ve.tensor_tensor(fsum[:], fcA[:], fcB[:], ALU.add), "dv")
                for j2 in range(4):
                    wtk["fcb_free"][j2] = tfs

                b.waits(te, [two, txo])
                banksW = (PA, PB, PD, PE)
                for og4 in range(4):
                    bank = banksW[og4]
                    bank.pre_write(b, te)
                    for fc in range(16):
                        inst = te.matmul(bank.t[:],
                                         xo_fm[pr][:, fc * 128:(fc + 1) * 128],
                                         w_sb[:, fc * 2048 + og4 * 512: fc * 2048 + (og4 + 1) * 512],
                                         start=(fc == 0), stop=(fc == 15))
                    tpe = b.inc(inst, "pe")
                    bank.writer = tpe
                    if og4 == 3:
                        wtk["xofm_free"][pr] = tpe
                b.waits(ve, [tpe, txr, tfs, wtk["out_free"][pr]])
                for og4 in range(4):
                    inst = ve.tensor_tensor(wt1[:, og4 * 512:(og4 + 1) * 512], banksW[og4].t[:],
                                            xrl[pr][:, og4 * 512:(og4 + 1) * 512], ALU.add)
                tmid = b.raw(ve, inst, "dv")
                for og4 in range(4):
                    inst = ve.tensor_tensor(outsb[pr][:, og4 * 512:(og4 + 1) * 512],
                                            wt1[:, og4 * 512:(og4 + 1) * 512],
                                            fsum[:, og4 * 512:(og4 + 1) * 512], ALU.add)
                tout = b.inc(inst, "dv")
                for og4 in range(4):
                    banksW[og4].readers.append(tmid)
                wtk["fsum_free"] = b.tok("dv")
                wtk["xrl_free"][pr] = b.tok("dv")
                b.wait(sy, b.tok("dv"))
                wtk["out_free"][pr] = b.dma(sy, f"outst{pr}", out_d[tc * 128:(tc + 1) * 128, :],
                                            outsb[pr][:])

            b.wait(sy, b.tok("outst0"))
            b.wait(sy, b.tok("outst1"))
            if _TAPS:
                b.wait(sy, b.tok("dbgst"))

    return nc, rank512


def _ffn_prep(topk_idx, topk_weight, n_experts=16, topk=4):
    fe = np.asarray(topk_idx).reshape(-1)
    fw = np.asarray(topk_weight).reshape(-1).astype(np.float32)
    n_pairs = fe.shape[0]
    tok_of_pair = np.arange(n_pairs) // topk
    order = np.argsort(fe, kind="stable")
    row_of_pair = np.empty(n_pairs, np.int64)
    row_of_pair[order] = np.arange(n_pairs)
    gs = np.bincount(fe, minlength=n_experts)
    return tok_of_pair[order], fw[order], gs, row_of_pair


def kernel(**inputs):
    x = np.ascontiguousarray(np.asarray(inputs["L0_attn_x"], np.float32))
    Wq = np.asarray(inputs["L0_attn_Wq"], np.float32)
    Wk = np.asarray(inputs["L0_attn_Wk"], np.float32)
    Wv = np.asarray(inputs["L0_attn_Wv"], np.float32)
    Wa = np.asarray(inputs["L0_attn_Wa"], np.float32)
    ba = np.asarray(inputs["L0_attn_ba"], np.float32)
    Wb = np.asarray(inputs["L0_attn_Wb"], np.float32)
    bb = np.asarray(inputs["L0_attn_bb"], np.float32)
    Wg = np.asarray(inputs["L0_attn_Wg"], np.float32)
    Wo = np.asarray(inputs["L0_attn_Wo"], np.float32)
    qcw = np.asarray(inputs["L0_attn_qcw"], np.float32)
    qcb = np.asarray(inputs["L0_attn_qcb"], np.float32)
    kcw = np.asarray(inputs["L0_attn_kcw"], np.float32)
    kcb = np.asarray(inputs["L0_attn_kcb"], np.float32)
    vcw = np.asarray(inputs["L0_attn_vcw"], np.float32)
    vcb = np.asarray(inputs["L0_attn_vcb"], np.float32)
    ln_w = np.asarray(inputs["L0_attn_ln_w"], np.float32)
    ln_b = np.asarray(inputs["L0_attn_ln_b"], np.float32)
    tidx = np.asarray(inputs["L0_ffn_topk_idx"])
    tw = np.asarray(inputs["L0_ffn_topk_weight"], np.float32)
    gw = np.asarray(inputs["L0_ffn_gate_weight"], np.float32)
    uw = np.asarray(inputs["L0_ffn_up_weight"], np.float32)
    dw = np.asarray(inputs["L0_ffn_down_weight"], np.float32)

    tok_rows, w_rows, gs, row_of_pair = _ffn_prep(tidx, tw)
    cap = 640
    if gs.max() > cap:
        cap = int(-(-int(gs.max()) // 128) * 128)
    if cap not in _CACHE:
        _CACHE[cap] = build_program(cap)
    nc, rank512 = _CACHE[cap]

    starts = np.concatenate([[0], np.cumsum(gs)])
    pad_row = np.empty(len(tok_rows), np.int64)
    for e in range(16):
        seg = slice(starts[e], starts[e + 1])
        pad_row[seg] = (e // NE) * 0
        pad_row[seg] = (e % NE) * cap + np.arange(gs[e])
    rank_of_row = np.repeat(np.arange(16) // NE, gs)

    in_maps = []
    for c in range(8):
        bi, hg = c // 4, c % 4
        hs = hg * 4
        m = {}
        m["xb"] = x[bi]
        m["xout"] = np.ascontiguousarray(x[bi, 2048 * hg: 2048 * (hg + 1)])
        m["wq"] = np.ascontiguousarray(Wq[:, hs * 128:(hs + 4) * 128])
        m["wk"] = np.ascontiguousarray(Wk[:, hs * 128:(hs + 4) * 128])
        m["wv"] = np.ascontiguousarray(Wv[:, hs * 128:(hs + 4) * 128])
        m["wg"] = np.ascontiguousarray(Wg[:, hs * 128:(hs + 4) * 128])
        m["wo"] = Wo
        m["wab"] = np.ascontiguousarray(
            np.concatenate([Wa[:, hg * 4: hg * 4 + 4], Wb[:, hg * 4: hg * 4 + 4]], axis=1))
        m["ba4"] = np.ascontiguousarray(ba[hg * 4: hg * 4 + 4].reshape(4, 1))
        m["bb4"] = np.ascontiguousarray(bb[hg * 4: hg * 4 + 4].reshape(4, 1))
        cwA = np.zeros((128, 48), np.float32)
        cbA = np.zeros((128, 12), np.float32)
        for mi, (cwm, cbm) in enumerate(((qcw, qcb), (kcw, kcb), (vcw, vcb))):
            for oc in range(4):
                ch0 = hs * 128 + oc * 128
                cwA[:, (mi * 4 + oc) * 4:(mi * 4 + oc) * 4 + 4] = cwm[ch0:ch0 + 128, :]
                cbA[:, mi * 4 + oc] = cbm[ch0:ch0 + 128]
        m["cw"] = cwA
        m["cb"] = cbA
        m["lnwB"] = np.ascontiguousarray(np.tile(ln_w[None, :], (128, 4)))
        m["lnbB"] = np.ascontiguousarray(np.tile(ln_b[None, :], (128, 4)))
        fxr = np.zeros((NE * cap, D), np.float32)
        fwc = np.zeros((128, NE * (cap // 128)), np.float32)
        fgwA = np.zeros((NE, 11, 128, D), np.float32)
        fuwA = np.zeros((NE, 11, 128, D), np.float32)
        fdwA = np.zeros((NE, 4, 128, 11 * 512), np.float32)
        for el in range(NE):
            e = NE * c + el
            n = gs[e]
            seg = slice(starts[e], starts[e + 1])
            fxr[el * cap: el * cap + n] = x[0, tok_rows[seg]]
            wpad = np.zeros(cap, np.float32)
            wpad[:n] = w_rows[seg]
            fwc[:, el * (cap // 128):(el + 1) * (cap // 128)] = wpad.reshape(cap // 128, 128).T
            fgwA[el] = gw[e].reshape(11, 128, 16, 128).transpose(0, 3, 2, 1).reshape(11, 128, D)
            fuwA[el] = uw[e].reshape(11, 128, 16, 128).transpose(0, 3, 2, 1).reshape(11, 128, D)
            fdwA[el] = dw[e].reshape(4, 512, 11, 128).transpose(0, 3, 2, 1).reshape(4, 128, 11 * 512)
        m["fx"] = fxr
        m["fwcol"] = fwc
        m["fgw"] = fgwA
        m["fuw"] = fuwA
        m["fdw"] = fdwA
        drows4 = 4 * NE * cap + NPAD * 4
        zrow = 0 * drows4 + 4 * (NE * cap)
        cidx = np.full((128, 64), zrow, np.int64)
        base_flat = 2048 * c
        if base_flat < 2048:
            for tc in range(16):
                for j2 in range(4):
                    toks = base_flat + tc * 128 + np.arange(128)
                    pairs = toks * 4 + j2
                    rows = row_of_pair[pairs]
                    cidx[:, tc * 4 + j2] = (rank_of_row[rows] * drows4
                                            + 4 * (pad_row[rows]))
        m["cidx"] = cidx.astype(np.int32)
        g0 = 4 * bi
        oidx = np.zeros((128, 64), np.int64)
        for tc in range(16):
            for jr in range(4):
                sq = 2048 * hg + tc * 128 + np.arange(128)
                oidx[:, tc * 4 + jr] = ((sq // 4096) * 8 + g0 + jr) * 4096 + (sq % 4096)
        m["oidx"] = oidx.astype(np.int32)
        in_maps.append(m)

    res = run_bass_kernel_spmd(nc, in_maps, core_ids=list(range(8)))
    shards = [np.asarray(res.results[c]["out"]) for c in range(8)]
    if _TAPS:
        kernel.dbg = [np.asarray(res.results[c]["dbg"]) for c in range(8)]
    return np.concatenate(shards, axis=0).reshape(B_, T, D)
